# version 01
# baseline (speedup 1.0000x reference)
import numpy as np
import ml_dtypes
from contextlib import ExitStack

import concourse.bass as bass
import concourse.mybir as mybir
from concourse.bass_utils import run_bass_kernel_spmd

F32 = mybir.dt.float32
BF16 = mybir.dt.bfloat16
AF = mybir.ActivationFunctionType
ALU = mybir.AluOpType
AX = mybir.AxisListType.X
NPBF = ml_dtypes.bfloat16

STAGE = 3
GAMMA = 0.96875
EPS = 1e-5
NEXP = 20


def _lhsT_chunks(W, order):
    K, F = W.shape
    Kt = K // 128
    arr = W.reshape(Kt, 128, F).transpose(1, 0, 2)
    out = np.empty((len(order), 128, Kt * 128), dtype=np.float32)
    for i, f in enumerate(order):
        out[i] = arr[:, :, f * 128:(f + 1) * 128].reshape(128, Kt * 128)
    return out


def _prep_shared(WQ, WK, WV, gate_w, sgate_w, w13, w2, sgu, sd):
    d = {}
    perm = np.concatenate([np.arange(0, 2048, 2), np.arange(1, 2048, 2)])
    pair_order = [x for i in range(8) for x in (i, i + 8)]
    d["wk"] = _lhsT_chunks(np.ascontiguousarray(WK[:, perm]), pair_order)
    d["wq"] = _lhsT_chunks(np.ascontiguousarray(WQ[:, perm]), pair_order)
    d["wv"] = _lhsT_chunks(np.ascontiguousarray(WV), list(range(16)))
    w13s = np.empty((NEXP, 22, 128, 2048), np.float32)
    w2s = np.empty((NEXP, 16, 128, 1408), np.float32)
    inter = [x for c in range(11) for x in (c, 11 + c)]
    for e in range(16):
        w13s[e] = _lhsT_chunks(np.ascontiguousarray(w13[e].T), inter)
        w2s[e] = _lhsT_chunks(np.ascontiguousarray(w2[e].T), list(range(16)))
    SGU = np.ascontiguousarray(sgu.T)
    SD = np.ascontiguousarray(sd.T)
    for ic in range(4):
        order = [x for c in range(11) for x in (ic * 11 + c, 44 + ic * 11 + c)]
        w13s[16 + ic] = _lhsT_chunks(SGU, order)
        w2s[16 + ic] = _lhsT_chunks(
            np.ascontiguousarray(SD[ic * 1408:(ic + 1) * 1408]), list(range(16)))
    d["w13s"] = w13s
    d["w2s"] = w2s
    d["grhs"] = np.ascontiguousarray(
        gate_w.T.reshape(16, 128, 16).transpose(1, 0, 2).reshape(128, 256))
    d["sgw"] = np.ascontiguousarray(sgate_w.reshape(16, 128).T)
    d["eye"] = np.eye(128, dtype=np.float32)
    d["onesc"] = np.ones((1, 128), np.float32)
    d["onesk"] = np.ones((128, 1), np.float32)
    return d


def _prep_core(X, c):
    b, j = c // 4, c % 4
    pos = np.arange(j * 1024, (j + 1) * 1024)
    d = {"x0": np.ascontiguousarray(X[b, pos, :].T).astype(np.float32)}
    i = np.arange(1024, dtype=np.float64)
    sv = (2.0 * i + 0.4 * 2048.0) / (1.4 * 2048.0)
    invf = 10000.0 ** (-i / 1024.0)
    p64 = pos.astype(np.float64)
    scale = sv[:, None] ** (p64[None, :] / 512.0)
    ang = invf[:, None] * p64[None, :]
    d["csq"] = (np.cos(ang) * scale).astype(np.float32)
    d["ssq"] = (np.sin(ang) * scale).astype(np.float32)
    d["csk"] = (np.cos(ang) / scale).astype(np.float32)
    d["ssk"] = (np.sin(ang) / scale).astype(np.float32)
    dec = np.zeros((2, 32, 128, 512), np.float32)
    for nc2 in range(2):
        n_idx = (2 * j + nc2) * 512 + np.arange(512).astype(np.float64)
        for mb in range(32):
            m_idx = mb * 128 + np.arange(128).astype(np.float64)
            dm = n_idx[None, :] - m_idx[:, None]
            dec[nc2, mb] = np.where(dm >= 0, GAMMA ** np.maximum(dm, 0.0), 0.0
                                    ).astype(np.float32)
    d["dec"] = dec
    return d


SEM_NAMES = ["wld0", "wld1", "tld0", "tld1", "kld0", "kld1", "dld0", "dld1",
             "vld0", "vld1", "vld2", "vld3", "xld", "pe", "dve", "act", "cc", "st",
             "kst0", "kst1", "vst"]


class _Bank:
    __slots__ = ("writer", "readers")

    def __init__(self):
        self.writer = 0
        self.readers = {}


class Prog:

    def __init__(self, cur, eng, sems):
        self.cur = cur
        self.eng = eng
        self.sems = sems
        self.c = {n: 0 for n in SEM_NAMES}
        self.banks = {}

    def op(self, key, fn, waits=(), inc=None, n=1):
        if self.cur == key:
            for s, v in waits:
                assert v <= self.c[s], f"wait {s}>={v} but only {self.c[s]} emitted"
                if v > 0:
                    self.eng.wait_ge(self.sems[s], v)
            inst = fn(self.eng)
            if inc is not None:
                inst.then_inc(self.sems[inc], n)
        else:
            for s, v in waits:
                assert v <= self.c[s], f"wait {s}>={v} but only {self.c[s]} emitted"
        if inc is not None:
            self.c[inc] += n
            return self.c[inc]
        return None

    def wait(self, key, waits):
        if self.cur == key:
            for s, v in waits:
                assert v <= self.c[s]
                if v > 0:
                    self.eng.wait_ge(self.sems[s], v)
        else:
            for s, v in waits:
                assert v <= self.c[s]

    def bank(self, name):
        if name not in self.banks:
            self.banks[name] = _Bank()
        return self.banks[name]

    def bank_wwaits(self, name):
        return sorted(self.bank(name).readers.items())

    def bank_written(self, name):
        b = self.bank(name)
        b.writer = self.c["pe"]
        b.readers = {}

    def bank_rwaits(self, name):
        return [("pe", self.bank(name).writer)]

    def bank_read(self, name, sem):
        self.bank(name).readers[sem] = self.c[sem]


class WStream:

    def __init__(self, p, slots, srcs):
        self.p = p
        self.slots = slots
        self.srcs = srcs
        self.consumed = []
        self.emitted = 0

    def emit_load(self, i):
        assert i == self.emitted
        ap, width = self.srcs[i]
        par = i % 2
        waits = []
        if i >= 2:
            waits = [("pe", self.consumed[i - 2])]
        slot = self.slots[par]
        self.p.op("g", lambda g, ap=ap, slot=slot, w=width:
                  g.dma_start(out=slot(w), in_=ap),
                  waits=waits, inc="wld%d" % par, n=16)
        self.emitted += 1

    def consume_waits(self, i):
        return [("wld%d" % (i % 2), 16 * (i // 2 + 1))]

    def mark_consumed(self, i):
        assert len(self.consumed) == i
        self.consumed.append(self.p.c["pe"])
        if i + 2 < len(self.srcs):
            self.emit_load(i + 2)



def build_nc(stage=STAGE, dumps=()):
    nc = bass.Bass("TRN2", target_bir_lowering=False)
    ddbg = {nm: nc.dram_tensor("dbg_" + nm, [128, 1024], F32, kind="ExternalOutput")
            for nm in dumps}

    dx0 = nc.dram_tensor("x0", [2048, 1024], F32, kind="ExternalInput")
    dw13 = nc.dram_tensor("w13s", [NEXP, 22, 128, 2048], F32, kind="ExternalInput")
    dw2 = nc.dram_tensor("w2s", [NEXP, 16, 128, 1408], F32, kind="ExternalInput")
    dwq = nc.dram_tensor("wq", [16, 128, 2048], F32, kind="ExternalInput")
    dwk = nc.dram_tensor("wk", [16, 128, 2048], F32, kind="ExternalInput")
    dwv = nc.dram_tensor("wv", [16, 128, 2048], F32, kind="ExternalInput")
    dgrhs = nc.dram_tensor("grhs", [128, 256], F32, kind="ExternalInput")
    dsgw = nc.dram_tensor("sgw", [128, 16], F32, kind="ExternalInput")
    deye = nc.dram_tensor("eye", [128, 128], F32, kind="ExternalInput")
    donesc = nc.dram_tensor("onesc", [1, 128], F32, kind="ExternalInput")
    donesk = nc.dram_tensor("onesk", [128, 1], F32, kind="ExternalInput")
    dcsq = nc.dram_tensor("csq", [1024, 1024], F32, kind="ExternalInput")
    dssq = nc.dram_tensor("ssq", [1024, 1024], F32, kind="ExternalInput")
    dcsk = nc.dram_tensor("csk", [1024, 1024], F32, kind="ExternalInput")
    dssk = nc.dram_tensor("ssk", [1024, 1024], F32, kind="ExternalInput")
    ddec = nc.dram_tensor("dec", [2, 32, 128, 512], F32, kind="ExternalInput")
    dy = nc.dram_tensor("y", [2048, 1024], F32, kind="ExternalOutput")
    kbi = nc.dram_tensor("kbi", [8, 128, 16, 128], BF16)
    kbg = nc.dram_tensor("kbg", [4 * 8 * 128 * 16, 128], BF16)
    vbi = nc.dram_tensor("vbi", [1024, 2048], BF16)
    vbg = nc.dram_tensor("vbg", [4096, 2048], BF16)

    ctx = ExitStack()
    with ctx:
        sb = ctx.enter_context
        x = [sb(nc.sbuf_tensor(f"sx{i}", [128, 1024], F32)) for i in range(16)]
        xn = [sb(nc.sbuf_tensor(f"xn{i}", [128, 1024], BF16)) for i in range(16)]
        hh = [sb(nc.sbuf_tensor(f"hh{i}", [128, 1024], BF16)) for i in range(16)]
        wb = [sb(nc.sbuf_tensor(f"wb{i}", [128, 2048], BF16)) for i in range(2)]
        kb = [sb(nc.sbuf_tensor(f"kb{i}", [128, 2048], BF16)) for i in range(2)]
        vb = [sb(nc.sbuf_tensor(f"vb{i}", [128, 2048], BF16)) for i in range(4)]
        vtmp = sb(nc.sbuf_tensor("vtmp", [128, 1024], BF16))
        vtg = sb(nc.sbuf_tensor("vtg", [128, 1024], BF16))
        tb0 = sb(nc.sbuf_tensor("tb0", [128, 1024], F32))
        dbgt = sb(nc.sbuf_tensor("dbgt", [128, 1024], F32))
        ts1 = sb(nc.sbuf_tensor("ts1", [128, 1024], F32))
        aB = sb(nc.sbuf_tensor("aB", [128, 1024], F32))
        bB = sb(nc.sbuf_tensor("bB", [128, 1024], F32))
        cB = sb(nc.sbuf_tensor("cB", [128, 1024], F32))
        sgB = sb(nc.sbuf_tensor("sgB", [128, 1024], F32))
        mrow = sb(nc.sbuf_tensor("mrow", [1, 1024], F32))
        vrow = sb(nc.sbuf_tensor("vrow", [1, 1024], F32))
        rwall = sb(nc.sbuf_tensor("rwall", [128, 128], F32))
        rsc = sb(nc.sbuf_tensor("rsc", [128, 8], F32))
        mxs = sb(nc.sbuf_tensor("mxs", [128, 1], F32))
        msks = sb(nc.sbuf_tensor("msks", [128, 16], F32))
        rw2s = sb(nc.sbuf_tensor("rw2s", [128, 16], F32))
        cofs = sb(nc.sbuf_tensor("cofs", [128, 32], F32))
        coefT = sb(nc.sbuf_tensor("coefT", [16, 1024], F32))
        eyef = sb(nc.sbuf_tensor("eyef", [128, 128], F32))
        eyeb = sb(nc.sbuf_tensor("eyeb", [128, 128], BF16))
        onescf = sb(nc.sbuf_tensor("onescf", [1, 128], F32))
        oneskb = sb(nc.sbuf_tensor("oneskb", [128, 1], BF16))
        grhsb = sb(nc.sbuf_tensor("grhsb", [128, 256], BF16))
        sgwb = sb(nc.sbuf_tensor("sgwb", [128, 16], BF16))
        P = [sb(nc.psum_tensor(f"P{i}", [128, 512], F32)) for i in range(6)]
        p6b = sb(nc.psum_tensor("p6b", [128, 1024], BF16))
        p7b = sb(nc.psum_tensor("p7b", [128, 1024], BF16))
        sems = {n: sb(nc.semaphore(n)) for n in SEM_NAMES}
        block = sb(nc.Block())

        srcs = []
        for call in range(2 if stage >= 3 else 1):
            for e in range(NEXP):
                for c in range(22):
                    srcs.append((dw13[e, c, :, :], 2048))
                for f in range(16):
                    srcs.append((dw2[e, f, :, :], 1408))
            if call == 0 and stage >= 2:
                for i in range(16):
                    srcs.append((dwk[i, :, :], 2048))
                for i in range(16):
                    srcs.append((dwq[i, :, :], 2048))
                for i in range(16):
                    srcs.append((dwv[i, :, :], 2048))

        def emit_all(p: Prog):
            ws = WStream(p, [lambda w, s=wb[0]: s[:, 0:w],
                             lambda w, s=wb[1]: s[:, 0:w]], srcs)
            st_total = [0]

            p.op("g", lambda g: g.dma_start(out=grhsb[:, :], in_=dgrhs[:, :]), inc="xld", n=16)
            p.op("g", lambda g: g.dma_start(out=sgwb[:, :], in_=dsgw[:, :]), inc="xld", n=16)
            p.op("g", lambda g: g.dma_start(out=eyeb[:, :], in_=deye[:, :]), inc="xld", n=16)
            p.op("g", lambda g: g.dma_start(out=oneskb[:, :], in_=donesk[:, :]), inc="xld", n=16)
            ws.emit_load(0)
            ws.emit_load(1)
            p.op("y", lambda y: y.dma_start(out=eyef[:, :], in_=deye[:, :]), inc="xld", n=16)
            p.op("y", lambda y: y.dma_start(out=onescf[:, :], in_=donesc[:, :]), inc="xld", n=16)
            consts_ld = p.c["xld"]
            for t in range(16):
                p.op("y", lambda y, t=t: y.dma_start(out=x[t][:, :], in_=dx0[t * 128:(t + 1) * 128, :]),
                     inc="xld", n=16)
            pro_ld = p.c["xld"]
            consts_ld = pro_ld

            state = {"xn_ready": 0, "w2_end": 0, "xn_free_pe": 0, "x_ready_dve": 0,
                     "x_ready_xld": pro_ld}

            def dump(nm, ap, rows=128, cols=1024):
                if nm not in ddbg:
                    return
                w = [("pe", p.c["pe"]), ("act", p.c["act"]), ("st", p.c["st"])]
                p.op("v", lambda v: v.tensor_copy(dbgt[0:rows, 0:cols], ap), waits=w,
                     inc="dve")
                p.op("y", lambda y, nm=nm: y.dma_start(
                    out=ddbg[nm][0:rows, 0:cols], in_=dbgt[0:rows, 0:cols]),
                    waits=[("dve", p.c["dve"])], inc="st", n=16)

            def emit_ln():
                for t in range(16):
                    w = [("xld", state["x_ready_xld"]), ("pe", state["xn_free_pe"]),
                         ("dve", state["x_ready_dve"])] if t == 0 else []
                    if t == 0:
                        w.append(("xld", consts_ld))
                    p.op("s", lambda s, t=t: s.activation(xn[t][:, :], x[t][:, :], AF.Copy),
                         waits=w, inc="act")
                    a_copy = p.c["act"]
                    for sp in range(2):
                        bank = f"P{sp}"
                        bw = p.bank_wwaits(bank) if t == 0 else []
                        p.op("t", lambda te, t=t, sp=sp: te.matmul(
                            P[sp][0:1, :], oneskb[:, 0:1], xn[t][:, sp * 512:(sp + 1) * 512],
                            start=(t == 0), stop=(t == 15), skip_group_check=True),
                            waits=bw + [("act", a_copy)], inc="pe")
                        if t == 15:
                            p.bank_written(bank)
                sum_done = p.c["pe"]
                for t in range(16):
                    p.op("s", lambda s, t=t: s.activation(xn[t][:, :], x[t][:, :], AF.Square),
                         waits=[("pe", sum_done)] if t == 0 else [], inc="act")
                    a_sq = p.c["act"]
                    for sp in range(2):
                        bank = f"P{2 + sp}"
                        bw = p.bank_wwaits(bank) if t == 0 else []
                        p.op("t", lambda te, t=t, sp=sp: te.matmul(
                            P[2 + sp][0:1, :], oneskb[:, 0:1], xn[t][:, sp * 512:(sp + 1) * 512],
                            start=(t == 0), stop=(t == 15), skip_group_check=True),
                            waits=bw + [("act", a_sq)], inc="pe")
                        if t == 15:
                            p.bank_written(bank)
                sq_done = p.c["pe"]
                for sp in range(2):
                    sl = slice(sp * 512, (sp + 1) * 512)
                    p.op("v", lambda v, sp=sp, sl=sl: v.tensor_scalar_mul(
                        mrow[0:1, sl], P[sp][0:1, :], 1.0 / 2048.0),
                        waits=[("pe", sq_done)], inc="dve")
                    p.bank_read(f"P{sp}", "dve")
                    p.op("v", lambda v, sl=sl: v.tensor_mul(
                        vrow[0:1, sl], mrow[0:1, sl], mrow[0:1, sl]), inc="dve")
                    p.op("v", lambda v, sp=sp, sl=sl: v.scalar_tensor_tensor(
                        vrow[0:1, sl], P[2 + sp][0:1, :], 1.0 / 2048.0, vrow[0:1, sl],
                        ALU.mult, ALU.subtract), inc="dve")
                    p.bank_read(f"P{2 + sp}", "dve")
                for sp in range(2):
                    sl = slice(sp * 512, (sp + 1) * 512)
                    p.op("v", lambda v, sl=sl: v.tensor_scalar_add(
                        vrow[0:1, sl], vrow[0:1, sl], EPS), inc="dve")
                row_v = p.c["dve"]
                for sp in range(2):
                    sl = slice(sp * 512, (sp + 1) * 512)
                    p.op("s", lambda s, sl=sl: s.activation(
                        vrow[0:1, sl], vrow[0:1, sl], AF.Ln),
                        waits=[("dve", row_v)] if sp == 0 else [], inc="act")
                    p.op("s", lambda s, sl=sl: s.activation(
                        vrow[0:1, sl], vrow[0:1, sl], AF.Exp, scale=-0.5), inc="act")
                row_r = p.c["act"]
                for sp in range(2):
                    sl = slice(sp * 512, (sp + 1) * 512)
                    p.op("v", lambda v, sl=sl: v.scalar_tensor_tensor(
                        mrow[0:1, sl], mrow[0:1, sl], -1.0, vrow[0:1, sl],
                        ALU.mult, ALU.mult),
                        waits=[("act", row_r)] if sp == 0 else [], inc="dve")
                row_b = p.c["dve"]
                for sp in range(2):
                    sl = slice(sp * 512, (sp + 1) * 512)
                    for (dst, src, rw) in ((aB, vrow, row_r), (bB, mrow, row_b)):
                        bank = "P4" if dst is aB else "P5"
                        pi = P[4] if dst is aB else P[5]
                        p.op("t", lambda te, pi=pi, src=src, sl=sl: te.matmul(
                            pi[:, :], onescf[0:1, :], src[0:1, sl],
                            start=True, stop=True, skip_group_check=True),
                            waits=p.bank_wwaits(bank) + [("dve", row_b), ("act", row_r)],
                            inc="pe")
                        p.bank_written(bank)
                        p.op("v", lambda v, dst=dst, pi=pi, sl=sl: v.tensor_copy(
                            dst[:, sl], pi[:, :]),
                            waits=p.bank_rwaits(bank), inc="dve")
                        p.bank_read(bank, "dve")
                ab_done = p.c["dve"]
                for t in range(16):
                    w = [("dve", ab_done), ("pe", sq_done)] if t == 0 else []
                    p.op("v", lambda v, t=t: v.tensor_mul(ts1[:, :], x[t][:, :], aB[:, :]),
                         waits=w, inc="dve")
                    p.op("v", lambda v, t=t: v.tensor_add(xn[t][:, :], ts1[:, :], bB[:, :]),
                         inc="dve")
                state["xn_ready"] = p.c["dve"]
                dump("xn0", xn[0][:, :])
                dump("ab", aB[:, :])
                dump("bb", bB[:, :])

            def emit_sgate_router():
                for sp in range(2):
                    bank = f"P{4 + sp}"
                    pi = P[4 + sp]
                    for kt in range(16):
                        w = p.bank_wwaits(bank) + [("dve", state["xn_ready"])] if kt == 0 else []
                        p.op("t", lambda te, pi=pi, kt=kt, sp=sp: te.matmul(
                            pi[0:1, :], sgwb[:, kt:kt + 1],
                            xn[kt][:, sp * 512:(sp + 1) * 512],
                            start=(kt == 0), stop=(kt == 15), skip_group_check=True),
                            waits=w, inc="pe")
                    p.bank_written(bank)
                    p.op("s", lambda s, pi=pi, sp=sp: s.activation(
                        mrow[0:1, sp * 512:(sp + 1) * 512], pi[0:1, :], AF.Sigmoid),
                        waits=p.bank_rwaits(bank), inc="act")
                    p.bank_read(bank, "act")
                sg_row = p.c["act"]
                for sp in range(2):
                    bank = f"P{4 + sp}"
                    pi = P[4 + sp]
                    sl = slice(sp * 512, (sp + 1) * 512)
                    p.op("t", lambda te, pi=pi, sl=sl: te.matmul(
                        pi[:, :], onescf[0:1, :], mrow[0:1, sl],
                        start=True, stop=True, skip_group_check=True),
                        waits=p.bank_wwaits(bank) + [("act", sg_row)], inc="pe")
                    p.bank_written(bank)
                    p.op("v", lambda v, pi=pi, sl=sl: v.tensor_scalar_mul(
                        sgB[:, sl], pi[:, :], 0.5),
                        waits=p.bank_rwaits(bank), inc="dve")
                    p.bank_read(bank, "dve")
                aexp_hist = [0, 0]
                lnx_hist = [0, 0]
                rinv_hist = [0, 0]
                tpos_hist = [0, 0]

                def ved(fn, waits=()):
                    val = p.op("v", fn, waits=waits, inc="dve")
                    p.wait("v", [("dve", val)])
                    return val

                for t in range(8):
                    par = t % 2
                    bank = f"P{par}"
                    pi = P[par]
                    nmc = rsc[:, par:par + 1]
                    sumc = rsc[:, 2 + par:3 + par]
                    rinvc = rsc[:, 4 + par:5 + par]
                    coefc = cofs[:, par * 16:(par + 1) * 16]
                    for kt in range(16):
                        w = p.bank_wwaits(bank) + ([("dve", state["xn_ready"])] if t == 0 else []) \
                            if kt == 0 else []
                        p.op("t", lambda te, pi=pi, kt=kt, t=t: te.matmul(
                            pi[:, 0:16], xn[kt][:, t * 128:(t + 1) * 128],
                            grhsb[:, kt * 16:(kt + 1) * 16],
                            start=(kt == 0), stop=(kt == 15), skip_group_check=True),
                            waits=w, inc="pe")
                    p.bank_written(bank)
                    nm = p.op("v", lambda v, pi=pi, nmc=nmc: v.reduce_max(
                        nmc, pi[:, 0:16], axis=AX, negate=True),
                        waits=p.bank_rwaits(bank) + [("act", aexp_hist[par])], inc="dve")
                    rsl = slice(t * 16, (t + 1) * 16)
                    p.op("s", lambda s, pi=pi, rsl=rsl, nmc=nmc: s.activation(
                        rwall[:, rsl], pi[:, 0:16], AF.Exp, bias=nmc[:, 0:1]),
                        waits=p.bank_rwaits(bank) + [("dve", nm)], inc="act")
                    p.bank_read(bank, "act")
                    aexp = p.c["act"]
                    aexp_hist[par] = aexp
                    sm = p.op("v", lambda v, rsl=rsl, sumc=sumc: v.reduce_sum(
                        sumc, rwall[:, rsl], axis=AX),
                        waits=[("act", max(aexp, lnx_hist[par]))], inc="dve")
                    p.op("s", lambda s, sumc=sumc, rinvc=rinvc: s.activation(
                        rinvc, sumc, AF.Ln),
                        waits=[("dve", max(sm, rinv_hist[par]))], inc="act")
                    p.op("s", lambda s, rinvc=rinvc: s.activation(
                        rinvc, rinvc, AF.Exp, scale=-1.0), inc="act")
                    arinv = p.c["act"]
                    rinv_hist[par] = ved(lambda v, rsl=rsl, rinvc=rinvc: v.tensor_scalar_mul(
                        rwall[:, rsl], rwall[:, rsl], rinvc), waits=[("act", arinv)])
                    for it in range(4):
                        src = rwall[:, rsl] if it == 0 else rw2s[:, :]
                        ved(lambda v, src=src: v.reduce_max(mxs[:, 0:1], src, axis=AX))
                        if it < 3:
                            ved(lambda v, src=src: v.tensor_scalar(
                                msks[:, :], src, mxs[:, 0:1], None, ALU.is_lt))
                            ved(lambda v, src=src: v.tensor_mul(
                                rw2s[:, :], src, msks[:, :]))
                    ved(lambda v, rsl=rsl: v.tensor_scalar(
                        msks[:, :], rwall[:, rsl], mxs[:, 0:1], None, ALU.is_ge))
                    cdone = ved(lambda v, rsl=rsl, coefc=coefc: v.scalar_tensor_tensor(
                        coefc, rwall[:, rsl], 0.5, msks[:, :],
                        ALU.mult, ALU.mult),
                        waits=[("pe", tpos_hist[par])])
                    p.op("t", lambda te, coefc=coefc: te.transpose(
                        P[4][0:16, 0:128], coefc, eyef[:, :]),
                        waits=p.bank_wwaits("P4") + [("dve", cdone), ("xld", consts_ld)],
                        inc="pe")
                    p.bank_written("P4")
                    tpos_hist[par] = p.c["pe"]
                    p.op("v", lambda v, t=t: v.tensor_copy(
                        coefT[:, t * 128:(t + 1) * 128], P[4][0:16, 0:128]),
                        waits=p.bank_rwaits("P4"), inc="dve")
                    p.bank_read("P4", "dve")
                state["coefT_ready"] = p.c["dve"]
                dump("sgb", sgB[:, :])
                dump("rw", rwall[:, :], cols=128)
                dump("coeft", coefT[:, :], rows=16)

            def emit_expert(e):
                cb = cB if e < 16 else sgB
                if e < 16:
                    for sp in range(2):
                        bank = f"P{2 + sp}"
                        pi = P[2 + sp]
                        sl = slice(sp * 512, (sp + 1) * 512)
                        p.op("t", lambda te, pi=pi, sl=sl, e=e: te.matmul(
                            pi[0:1, :], eyef[0:16, e:e + 1], coefT[0:16, sl],
                            start=True, stop=True, skip_group_check=True),
                            waits=p.bank_wwaits(bank) + [("dve", state["coefT_ready"])],
                            inc="pe")
                        p.bank_written(bank)
                        p.op("v", lambda v, pi=pi, sl=sl: v.tensor_copy(mrow[0:1, sl], pi[0:1, :]),
                             waits=p.bank_rwaits(bank), inc="dve")
                        p.bank_read(bank, "dve")
                    row_done = p.c["dve"]
                    for sp in range(2):
                        bank = f"P{2 + sp}"
                        pi = P[2 + sp]
                        sl = slice(sp * 512, (sp + 1) * 512)
                        p.op("t", lambda te, pi=pi, sl=sl: te.matmul(
                            pi[:, :], onescf[0:1, :], mrow[0:1, sl],
                            start=True, stop=True, skip_group_check=True),
                            waits=p.bank_wwaits(bank) + [("dve", row_done)],
                            inc="pe")
                        p.bank_written(bank)
                        p.op("v", lambda v, pi=pi, sl=sl: v.tensor_copy(cB[:, sl], pi[:, :]),
                             waits=p.bank_rwaits(bank), inc="dve")
                        p.bank_read(bank, "dve")
                cb_ready = p.c["dve"]
                hprev = [0, 0]
                for c in range(11):
                    sil = [0, 0]
                    for half in range(2):
                        i = len(ws.consumed)
                        bk = (0, 1) if half == 0 else (2, 3)
                        for sp in range(2):
                            bank = f"P{bk[sp]}"
                            w = []
                            if sp == 0:
                                w += ws.consume_waits(i)
                                w += [("dve", max(state["xn_ready"], cb_ready))]
                            w += p.bank_wwaits(bank)
                            par = i % 2
                            for kt in range(16):
                                p.op("t", lambda te, bk=bk, sp=sp, par=par, kt=kt: te.matmul(
                                    P[bk[sp]][:, :], wb[par][:, kt * 128:(kt + 1) * 128],
                                    xn[kt][:, sp * 512:(sp + 1) * 512],
                                    start=(kt == 0), stop=(kt == 15), skip_group_check=True),
                                    waits=w if kt == 0 else [], inc="pe" if kt == 15 else None)
                            p.bank_written(bank)
                        ws.mark_consumed(i)
                        if half == 0:
                            for sp in range(2):
                                bank = f"P{sp}"
                                p.op("s", lambda s, sp=sp, c=c: s.activation(
                                    ts1[:, sp * 512:(sp + 1) * 512], P[sp][:, :], AF.Silu),
                                    waits=p.bank_rwaits(bank) +
                                    [("dve", hprev[sp])], inc="act")
                                p.bank_read(bank, "act")
                                sil[sp] = p.c["act"]
                    for sp in range(2):
                        bank = f"P{2 + sp}"
                        sl = slice(sp * 512, (sp + 1) * 512)
                        p.op("v", lambda v, sp=sp, sl=sl, cb=cb: v.tensor_mul(
                            tb0[:, sl], P[2 + sp][:, :], cb[:, sl]),
                            waits=p.bank_rwaits(bank), inc="dve")
                        p.bank_read(bank, "dve")
                        p.op("v", lambda v, c=c, sl=sl: v.tensor_mul(
                            hh[c][:, sl], tb0[:, sl], ts1[:, sl]),
                            waits=[("act", sil[sp]), ("pe", state["w2_end"])], inc="dve")
                        hprev[sp] = p.c["dve"]
                h_done = p.c["dve"]
                if e == 0:
                    dump("cb0", cB[:, :])
                    dump("h00", hh[0][:, :])
                for f in range(16):
                    i = len(ws.consumed)
                    bank = f"P{4 + f % 2}"
                    pi = P[4 + f % 2]
                    par = i % 2
                    evs = [0, 0]
                    for sp in range(2):
                        w = []
                        if sp == 0:
                            w += ws.consume_waits(i)
                            if f == 0:
                                w += [("dve", h_done)]
                        w += p.bank_wwaits(bank)
                        for kt in range(11):
                            p.op("t", lambda te, pi=pi, par=par, kt=kt, sp=sp: te.matmul(
                                pi[:, :], wb[par][:, kt * 128:(kt + 1) * 128],
                                hh[kt][:, sp * 512:(sp + 1) * 512],
                                start=(kt == 0), stop=(kt == 10), skip_group_check=True),
                                waits=w if kt == 0 else [], inc="pe" if kt == 10 else None)
                        p.bank_written(bank)
                        p.op("v", lambda v, pi=pi, f=f, sp=sp: v.scalar_tensor_tensor(
                            x[f][:, sp * 512:(sp + 1) * 512], pi[:, :], 1.0,
                            x[f][:, sp * 512:(sp + 1) * 512], ALU.mult, ALU.add),
                            waits=p.bank_rwaits(bank), inc="dve")
                        p.bank_read(bank, "dve")
                    ws.mark_consumed(i)
                state["w2_end"] = p.c["pe"]
                if e == 0:
                    dump("x0e0", x[0][:, :])
                if e == 16:
                    dump("x0e16", x[0][:, :])

            def emit_ffn():
                emit_ln()
                emit_sgate_router()
                for e in range(NEXP):
                    emit_expert(e)
                state["x_ready_dve"] = p.c["dve"]
                state["xn_free_pe"] = p.c["pe"]

            def emit_qkv():
                emit_ln()
                tab_consumed = {"c": p.c["dve"], "s": p.c["dve"]}

                def emit_pair(i, dcs, dss, out_lo, out_hi, store):
                    p.op("y", lambda y, i=i, dcs=dcs: y.dma_start(
                        out=tb0[:, :], in_=dcs[i * 128:(i + 1) * 128, :]),
                        waits=[("dve", tab_consumed["c"])], inc="tld0", n=16)
                    p.op("y", lambda y, i=i, dss=dss: y.dma_start(
                        out=cB[:, :], in_=dss[i * 128:(i + 1) * 128, :]),
                        waits=[("dve", tab_consumed["s"])], inc="tld1", n=16)
                    tl0, tl1 = p.c["tld0"], p.c["tld1"]
                    for half in range(2):
                        i_ch = len(ws.consumed)
                        bk = (0, 1) if half == 0 else (2, 3)
                        par = i_ch % 2
                        for sp in range(2):
                            bank = f"P{bk[sp]}"
                            w = (ws.consume_waits(i_ch) if sp == 0 else []) + p.bank_wwaits(bank)
                            if sp == 0 and half == 0:
                                w += [("dve", state["xn_ready"])]
                            for kt in range(16):
                                p.op("t", lambda te, bk=bk, sp=sp, par=par, kt=kt: te.matmul(
                                    P[bk[sp]][:, :], wb[par][:, kt * 128:(kt + 1) * 128],
                                    xn[kt][:, sp * 512:(sp + 1) * 512],
                                    start=(kt == 0), stop=(kt == 15), skip_group_check=True),
                                    waits=w if kt == 0 else [], inc="pe" if kt == 15 else None)
                            p.bank_written(bank)
                        ws.mark_consumed(i_ch)
                    for sp in range(2):
                        sl = slice(sp * 512, (sp + 1) * 512)
                        blo, bup = f"P{sp}", f"P{2 + sp}"
                        w0 = p.bank_rwaits(blo) + p.bank_rwaits(bup) + \
                            [("tld0", tl0), ("tld1", tl1)]
                        if store is not None:
                            w0 += store["stage_free"]
                        p.op("v", lambda v, sp=sp, sl=sl: v.tensor_mul(
                            ts1[:, sl], P[sp][:, :], tb0[:, sl]), waits=w0, inc="dve")
                        p.op("v", lambda v, sp=sp, sl=sl: v.tensor_mul(
                            sgB[:, sl], P[2 + sp][:, :], cB[:, sl]), inc="dve")
                        p.op("v", lambda v, sl=sl, out_lo=out_lo: v.tensor_sub(
                            out_lo[:, sl], ts1[:, sl], sgB[:, sl]), inc="dve")
                        p.op("v", lambda v, sp=sp, sl=sl: v.tensor_mul(
                            ts1[:, sl], P[2 + sp][:, :], tb0[:, sl]), inc="dve")
                        p.bank_read(bup, "dve")
                        tab_consumed["c"] = p.c["dve"]
                        p.op("v", lambda v, sp=sp, sl=sl: v.tensor_mul(
                            sgB[:, sl], P[sp][:, :], cB[:, sl]), inc="dve")
                        p.bank_read(blo, "dve")
                        tab_consumed["s"] = p.c["dve"]
                        p.op("v", lambda v, sl=sl, out_hi=out_hi: v.tensor_add(
                            out_hi[:, sl], ts1[:, sl], sgB[:, sl]), inc="dve")
                    return p.c["dve"]

                stage_free = [[], []]
                for i in range(8):
                    lo, hi = hh[11 + 2 * (i % 2)], hh[12 + 2 * (i % 2)]
                    dn = emit_pair(i, dcsk, dssk, lo, hi,
                                   {"stage_free": stage_free[i % 2]})
                    ks = "kst%d" % (i % 2)
                    for (tile, ht) in ((lo, i), (hi, i + 8)):
                        p.op("y", lambda y, tile=tile, ht=ht: y.dma_start(
                            out=kbi[:, :, ht, :].rearrange("l p m -> p l m"),
                            in_=tile[:, :].rearrange("p (l m) -> p l m", l=8)),
                            waits=[("dve", dn)], inc=ks, n=16)
                    stage_free[i % 2] = [(ks, p.c[ks])]
                k_store_done = [("kst0", p.c["kst0"]), ("kst1", p.c["kst1"])]
                for i in range(8):
                    sfree = [("pe", state["w2_end"])]
                    if i in (3, 4, 5, 6):
                        sfree = sfree + k_store_done
                    emit_pair(i, dcsq, dssq, hh[i], hh[i + 8], {"stage_free": sfree})
                vst = [0]
                for f in range(16):
                    i_ch = len(ws.consumed)
                    par = i_ch % 2
                    for sp in range(2):
                        bank = f"P{4 + sp}"
                        w = (ws.consume_waits(i_ch) if sp == 0 else []) + p.bank_wwaits(bank)
                        for kt in range(16):
                            p.op("t", lambda te, sp=sp, par=par, kt=kt: te.matmul(
                                P[4 + sp][:, :], wb[par][:, kt * 128:(kt + 1) * 128],
                                xn[kt][:, sp * 512:(sp + 1) * 512],
                                start=(kt == 0), stop=(kt == 15), skip_group_check=True),
                                waits=w if kt == 0 else [], inc="pe" if kt == 15 else None)
                        p.bank_written(bank)
                    ws.mark_consumed(i_ch)
                    for sp in range(2):
                        bank = f"P{4 + sp}"
                        w = p.bank_rwaits(bank)
                        if f > 0 and sp == 0:
                            w += [("vst", vst[0])]
                        p.op("v", lambda v, sp=sp: v.tensor_copy(
                            vtmp[:, sp * 512:(sp + 1) * 512], P[4 + sp][:, :]),
                            waits=w, inc="dve")
                        p.bank_read(bank, "dve")
                    vt_done = p.c["dve"]
                    for m in range(8):
                        pb = p6b if m % 2 == 0 else p7b
                        bname = "p6b" if m % 2 == 0 else "p7b"
                        p.op("t", lambda te, pb=pb, m=m: te.transpose(
                            pb[:, 0:128], vtmp[:, m * 128:(m + 1) * 128], eyeb[:, :]),
                            waits=p.bank_wwaits(bname) +
                            ([("dve", vt_done), ("xld", consts_ld)] if m == 0 else []),
                            inc="pe")
                        p.bank_written(bname)
                        p.op("v", lambda v, pb=pb, m=m: v.tensor_copy(
                            vtg[:, m * 128:(m + 1) * 128], pb[:, 0:128]),
                            waits=p.bank_rwaits(bname), inc="dve")
                        p.bank_read(bname, "dve")
                    p.op("y", lambda y, f=f: y.dma_start(
                        out=vbi[:, f * 128:(f + 1) * 128].rearrange("(m p) c -> p m c", p=128),
                        in_=vtg[:, :].rearrange("p (m c) -> p m c", m=8)),
                        waits=[("dve", p.c["dve"])], inc="vst", n=16)
                    vst[0] = p.c["vst"]
                p.op("g", lambda g: g.collective_compute(
                    "AllGather", ALU.bypass,
                    replica_groups=[[0, 1, 2, 3], [4, 5, 6, 7]],
                    ins=[kbi.ap().rearrange("a p h m -> (a p h) m").opt()],
                    outs=[kbg.ap().opt()]),
                    waits=k_store_done + [("vst", p.c["vst"])], inc="cc")
                p.op("g", lambda g: g.collective_compute(
                    "AllGather", ALU.bypass,
                    replica_groups=[[0, 1, 2, 3], [4, 5, 6, 7]],
                    ins=[vbi.ap().opt()],
                    outs=[vbg.ap().opt()]),
                    inc="cc")
                state["qkv_end_pe"] = p.c["pe"]
                state["cc_done"] = p.c["cc"]

            def emit_attn():
                kq_consumed = []
                dq_consumed = []
                avg_done = []
                entry_dve = p.c["dve"]

                def load_kblock(q):
                    mb = q % 32
                    r, lb = mb // 8, mb % 8
                    rows = slice((r * 8 + lb) * 2048, (r * 8 + lb) * 2048 + 2048)
                    w = [("cc", state["cc_done"] - 1)]
                    if q >= 2:
                        w = [("pe", kq_consumed[q - 2])]
                    p.op("y", lambda y, rows=rows, q=q: y.dma_start(
                        out=kb[q % 2][:, :],
                        in_=kbg[rows, :].rearrange("(p h) m -> p (h m)", p=128)),
                        waits=w, inc="kld%d" % (q % 2), n=16)

                def load_dtile(q):
                    nc2, mb = q // 32, q % 32
                    w = [("dve", entry_dve)] if q < 2 else [("dve", dq_consumed[q - 2])]
                    sl = slice((q % 2) * 512, (q % 2) * 512 + 512)
                    p.op("y", lambda y, nc2=nc2, mb=mb, sl=sl: y.dma_start(
                        out=tb0[:, sl], in_=ddec[nc2, mb, :, :]),
                        waits=w, inc="dld%d" % (q % 2), n=16)

                def load_vblock(g2, lb):
                    mb = g2 % 8 * 4 + lb
                    nc2 = g2 // 8
                    w = [("cc", state["cc_done"])]
                    if g2 >= 1:
                        w = [("pe", avg_done[g2 - 1])]
                    p.op("y", lambda y, mb=mb, lb=lb: y.dma_start(
                        out=vb[lb][:, :], in_=vbg[mb * 128:(mb + 1) * 128, :]),
                        waits=w, inc="vld%d" % lb, n=16)

                load_kblock(0)
                load_kblock(1)
                load_dtile(0)
                load_dtile(1)
                for lb in range(4):
                    load_vblock(0, lb)

                av_last_dve = 0
                for nc2 in range(2):
                    for g in range(8):
                        g2 = nc2 * 8 + g
                        for lb in range(4):
                            q = g2 * 4 + lb
                            mb = 4 * g + lb
                            bank = f"P{q % 2}"
                            pi = P[q % 2]
                            w = [("kld%d" % (q % 2), 16 * (q // 2 + 1))] + p.bank_wwaits(bank)
                            if q == 0:
                                w += [("pe", state["qkv_end_pe"])]
                            for ht in range(16):
                                p.op("t", lambda te, pi=pi, q=q, ht=ht, nc2=nc2: te.matmul(
                                    pi[:, :], kb[q % 2][:, ht * 128:(ht + 1) * 128],
                                    hh[ht][:, nc2 * 512:(nc2 + 1) * 512],
                                    start=(ht == 0), stop=(ht == 15), skip_group_check=True),
                                    waits=w if ht == 0 else [], inc="pe" if ht == 15 else None)
                            p.bank_written(bank)
                            kq_consumed.append(p.c["pe"])
                            if q + 2 < 64:
                                load_kblock(q + 2)
                            sl = slice((q % 2) * 512, (q % 2) * 512 + 512)
                            w = p.bank_rwaits(bank) + [("dld%d" % (q % 2), 16 * (q // 2 + 1))]
                            if nc2 == 0 and g == 0 and lb == 0:
                                w += [("pe", state["qkv_end_pe"])]
                            if nc2 == 1 and g == 0 and lb == 0:
                                w += [("pe", avg_done[7])]
                            p.op("v", lambda v, pi=pi, mb=mb, sl=sl: v.tensor_mul(
                                xn[mb // 2][:, (mb % 2) * 512:(mb % 2) * 512 + 512],
                                pi[:, :], tb0[:, sl]),
                                waits=w, inc="dve")
                            p.bank_read(bank, "dve")
                            dq_consumed.append(p.c["dve"])
                            if q + 2 < 64:
                                load_dtile(q + 2)
                        attd_done = p.c["dve"]
                        for f in range(16):
                            bank = f"P{4 + f % 2}"
                            pi = P[4 + f % 2]
                            w = p.bank_wwaits(bank) + [("dve", attd_done)]
                            if f == 0:
                                w += [("vld%d" % lb2, 16 * (g2 + 1)) for lb2 in range(4)]
                            for lb in range(4):
                                mb = 4 * g + lb
                                p.op("t", lambda te, pi=pi, f=f, lb=lb, mb=mb: te.matmul(
                                    pi[:, :], vb[lb][:, f * 128:(f + 1) * 128],
                                    xn[mb // 2][:, (mb % 2) * 512:(mb % 2) * 512 + 512],
                                    start=(lb == 0), stop=(lb == 3), skip_group_check=True),
                                    waits=w if lb == 0 else [], inc="pe" if lb == 3 else None)
                            p.bank_written(bank)
                            p.op("v", lambda v, pi=pi, f=f, nc2=nc2: v.scalar_tensor_tensor(
                                x[f][:, nc2 * 512:(nc2 + 1) * 512], pi[:, :], 1.0,
                                x[f][:, nc2 * 512:(nc2 + 1) * 512], ALU.mult, ALU.add),
                                waits=p.bank_rwaits(bank), inc="dve")
                            p.bank_read(bank, "dve")
                        avg_done.append(p.c["pe"])
                        if g2 + 1 < 16:
                            for lb in range(4):
                                load_vblock(g2 + 1, lb)
                state["x_ready_dve"] = p.c["dve"]
                state["xn_free_pe"] = p.c["pe"]
                state["w2_end"] = p.c["pe"]

            emit_ffn()
            if stage >= 2:
                emit_qkv()
                emit_attn()
            if stage >= 3:
                emit_ffn()

            xfin = p.c["dve"]
            for f in range(16):
                p.op("y", lambda y, f=f: y.dma_start(
                    out=dy[f * 128:(f + 1) * 128, :], in_=x[f][:, :]),
                    waits=[("dve", xfin)] if f == 0 else [], inc="st", n=16)
            p.wait("y", [("st", p.c["st"])])

        @block.tensor
        def _(te):
            emit_all(Prog("t", te, sems))

        @block.vector
        def _(v):
            emit_all(Prog("v", v, sems))

        @block.scalar
        def _(s):
            emit_all(Prog("s", s, sems))

        @block.gpsimd
        def _(g):
            emit_all(Prog("g", g, sems))

        @block.sync
        def _(y):
            emit_all(Prog("y", y, sems))

    return nc


_CACHE = {}


def kernel(L0_attn_X, L0_attn_W_Q, L0_attn_W_K, L0_attn_W_V,
           L0_ffn_gate_w, L0_ffn_sgate_w, L0_ffn_w13, L0_ffn_w2,
           L0_ffn_shared_gate_up_w, L0_ffn_shared_down_w):
    X = np.asarray(L0_attn_X)
    shared = _prep_shared(np.asarray(L0_attn_W_Q), np.asarray(L0_attn_W_K),
                          np.asarray(L0_attn_W_V), np.asarray(L0_ffn_gate_w),
                          np.asarray(L0_ffn_sgate_w), np.asarray(L0_ffn_w13),
                          np.asarray(L0_ffn_w2), np.asarray(L0_ffn_shared_gate_up_w),
                          np.asarray(L0_ffn_shared_down_w))
    in_maps = []
    for c in range(8):
        m = dict(shared)
        m.update(_prep_core(X, c))
        in_maps.append(m)
    if "nc" not in _CACHE:
        _CACHE["nc"] = build_nc(STAGE)
    nc = _CACHE["nc"]
    res = run_bass_kernel_spmd(nc, in_maps, core_ids=list(range(8)))
    out = np.empty((2, 4096, 2048), np.float32)
    for c in range(8):
        b, j = c // 4, c % 4
        out[b, j * 1024:(j + 1) * 1024, :] = res.results[c]["y"].T
    return out


if __name__ == "__main__":
    import sys
    if len(sys.argv) > 1 and sys.argv[1] == "buildtest":
        import time
        t0 = time.time()
        nc = build_nc(int(sys.argv[2]) if len(sys.argv) > 2 else STAGE)
        print(f"build ok in {time.time() - t0:.1f}s")


# version 02
# speedup vs baseline: 1.0126x; 1.0126x over previous
import numpy as np
import ml_dtypes
from contextlib import ExitStack

import concourse.bass as bass
import concourse.mybir as mybir
from concourse.bass_utils import run_bass_kernel_spmd

F32 = mybir.dt.float32
BF16 = mybir.dt.bfloat16
AF = mybir.ActivationFunctionType
ALU = mybir.AluOpType
AX = mybir.AxisListType.X
NPBF = ml_dtypes.bfloat16

STAGE = 3
GAMMA = 0.96875
EPS = 1e-5
NEXP = 20


def _lhsT_chunks(W, order):
    K, F = W.shape
    Kt = K // 128
    arr = W.reshape(Kt, 128, F).transpose(1, 0, 2)
    out = np.empty((len(order), 128, Kt * 128), dtype=np.float32)
    for i, f in enumerate(order):
        out[i] = arr[:, :, f * 128:(f + 1) * 128].reshape(128, Kt * 128)
    return out


def _prep_shared(WQ, WK, WV, gate_w, sgate_w, w13, w2, sgu, sd):
    d = {}
    perm = np.concatenate([np.arange(0, 2048, 2), np.arange(1, 2048, 2)])
    pair_order = [x for i in range(8) for x in (i, i + 8)]
    d["wk"] = _lhsT_chunks(np.ascontiguousarray(WK[:, perm]), pair_order)
    d["wq"] = _lhsT_chunks(np.ascontiguousarray(WQ[:, perm]), pair_order)
    d["wv"] = _lhsT_chunks(np.ascontiguousarray(WV), list(range(16)))
    w13s = np.empty((NEXP, 22, 128, 2048), np.float32)
    w2s = np.empty((NEXP, 16, 128, 1408), np.float32)
    inter = [x for c in range(11) for x in (c, 11 + c)]
    SGU = np.ascontiguousarray(sgu.T)
    SD = np.ascontiguousarray(sd.T)
    for ic in range(4):
        order = [x for c in range(11) for x in (ic * 11 + c, 44 + ic * 11 + c)]
        w13s[ic] = _lhsT_chunks(SGU, order)
        w2s[ic] = _lhsT_chunks(
            np.ascontiguousarray(SD[ic * 1408:(ic + 1) * 1408]), list(range(16)))
    for e in range(16):
        w13s[4 + e] = _lhsT_chunks(np.ascontiguousarray(w13[e].T), inter)
        w2s[4 + e] = _lhsT_chunks(np.ascontiguousarray(w2[e].T), list(range(16)))
    d["w13s"] = w13s
    d["w2s"] = w2s
    d["grhs"] = np.ascontiguousarray(
        gate_w.T.reshape(16, 128, 16).transpose(1, 0, 2).reshape(128, 256))
    d["sgw"] = np.ascontiguousarray(sgate_w.reshape(16, 128).T)
    d["eye"] = np.eye(128, dtype=np.float32)
    d["onesc"] = np.ones((1, 128), np.float32)
    d["onesk"] = np.ones((128, 1), np.float32)
    return d


def _prep_core(X, c):
    b, j = c // 4, c % 4
    pos = np.arange(j * 1024, (j + 1) * 1024)
    d = {"x0": np.ascontiguousarray(X[b, pos, :].T).astype(np.float32)}
    i = np.arange(1024, dtype=np.float64)
    sv = (2.0 * i + 0.4 * 2048.0) / (1.4 * 2048.0)
    invf = 10000.0 ** (-i / 1024.0)
    p64 = pos.astype(np.float64)
    scale = sv[:, None] ** (p64[None, :] / 512.0)
    ang = invf[:, None] * p64[None, :]
    d["csq"] = (np.cos(ang) * scale).astype(np.float32)
    d["ssq"] = (np.sin(ang) * scale).astype(np.float32)
    d["csk"] = (np.cos(ang) / scale).astype(np.float32)
    d["ssk"] = (np.sin(ang) / scale).astype(np.float32)
    dec = np.zeros((2, 32, 128, 512), np.float32)
    for nc2 in range(2):
        n_idx = (2 * j + nc2) * 512 + np.arange(512).astype(np.float64)
        for mb in range(32):
            m_idx = mb * 128 + np.arange(128).astype(np.float64)
            dm = n_idx[None, :] - m_idx[:, None]
            dec[nc2, mb] = np.where(dm >= 0, GAMMA ** np.maximum(dm, 0.0), 0.0
                                    ).astype(np.float32)
    d["dec"] = dec
    return d


SEM_NAMES = ["wld0", "wld1", "wld2", "tld0", "tld1", "kld0", "kld1", "dld0", "dld1",
             "vld0", "vld1", "vld2", "vld3", "xld", "pe", "dve", "act", "cc", "st",
             "kst0", "kst1", "vst"]


class _Bank:
    __slots__ = ("writer", "readers")

    def __init__(self):
        self.writer = 0
        self.readers = {}


class Prog:

    def __init__(self, cur, eng, sems):
        self.cur = cur
        self.eng = eng
        self.sems = sems
        self.c = {n: 0 for n in SEM_NAMES}
        self.banks = {}

    def op(self, key, fn, waits=(), inc=None, n=1):
        if self.cur == key:
            for s, v in waits:
                assert v <= self.c[s], f"wait {s}>={v} but only {self.c[s]} emitted"
                if v > 0:
                    self.eng.wait_ge(self.sems[s], v)
            inst = fn(self.eng)
            if inc is not None:
                inst.then_inc(self.sems[inc], n)
        else:
            for s, v in waits:
                assert v <= self.c[s], f"wait {s}>={v} but only {self.c[s]} emitted"
        if inc is not None:
            self.c[inc] += n
            return self.c[inc]
        return None

    def wait(self, key, waits):
        if self.cur == key:
            for s, v in waits:
                assert v <= self.c[s]
                if v > 0:
                    self.eng.wait_ge(self.sems[s], v)
        else:
            for s, v in waits:
                assert v <= self.c[s]

    def bank(self, name):
        if name not in self.banks:
            self.banks[name] = _Bank()
        return self.banks[name]

    def bank_wwaits(self, name):
        return sorted(self.bank(name).readers.items())

    def bank_written(self, name):
        b = self.bank(name)
        b.writer = self.c["pe"]
        b.readers = {}

    def bank_rwaits(self, name):
        return [("pe", self.bank(name).writer)]

    def bank_read(self, name, sem):
        self.bank(name).readers[sem] = self.c[sem]


class WStream:
    NS = 3

    def __init__(self, p, slots, srcs):
        self.p = p
        self.slots = slots
        self.srcs = srcs
        self.consumed = []
        self.emitted = 0

    def emit_load(self, i):
        assert i == self.emitted
        ap, width = self.srcs[i]
        par = i % self.NS
        waits = []
        if i >= self.NS:
            waits = [("pe", self.consumed[i - self.NS])]
        slot = self.slots[par]
        self.p.op("g", lambda g, ap=ap, slot=slot, w=width:
                  g.dma_start(out=slot(w), in_=ap),
                  waits=waits, inc="wld%d" % par, n=16)
        self.emitted += 1

    def consume_waits(self, i):
        return [("wld%d" % (i % self.NS), 16 * (i // self.NS + 1))]

    def mark_consumed(self, i):
        assert len(self.consumed) == i
        self.consumed.append(self.p.c["pe"])
        if i + self.NS < len(self.srcs):
            self.emit_load(i + self.NS)



def build_nc(stage=STAGE, dumps=()):
    nc = bass.Bass("TRN2", target_bir_lowering=False)
    ddbg = {nm: nc.dram_tensor("dbg_" + nm, [128, 1024], F32, kind="ExternalOutput")
            for nm in dumps}

    dx0 = nc.dram_tensor("x0", [2048, 1024], F32, kind="ExternalInput")
    dw13 = nc.dram_tensor("w13s", [NEXP, 22, 128, 2048], F32, kind="ExternalInput")
    dw2 = nc.dram_tensor("w2s", [NEXP, 16, 128, 1408], F32, kind="ExternalInput")
    dwq = nc.dram_tensor("wq", [16, 128, 2048], F32, kind="ExternalInput")
    dwk = nc.dram_tensor("wk", [16, 128, 2048], F32, kind="ExternalInput")
    dwv = nc.dram_tensor("wv", [16, 128, 2048], F32, kind="ExternalInput")
    dgrhs = nc.dram_tensor("grhs", [128, 256], F32, kind="ExternalInput")
    dsgw = nc.dram_tensor("sgw", [128, 16], F32, kind="ExternalInput")
    deye = nc.dram_tensor("eye", [128, 128], F32, kind="ExternalInput")
    donesc = nc.dram_tensor("onesc", [1, 128], F32, kind="ExternalInput")
    donesk = nc.dram_tensor("onesk", [128, 1], F32, kind="ExternalInput")
    dcsq = nc.dram_tensor("csq", [1024, 1024], F32, kind="ExternalInput")
    dssq = nc.dram_tensor("ssq", [1024, 1024], F32, kind="ExternalInput")
    dcsk = nc.dram_tensor("csk", [1024, 1024], F32, kind="ExternalInput")
    dssk = nc.dram_tensor("ssk", [1024, 1024], F32, kind="ExternalInput")
    ddec = nc.dram_tensor("dec", [2, 32, 128, 512], F32, kind="ExternalInput")
    dy = nc.dram_tensor("y", [2048, 1024], F32, kind="ExternalOutput")
    kbi = nc.dram_tensor("kbi", [8, 128, 16, 128], BF16)
    kbg = nc.dram_tensor("kbg", [4 * 8 * 128 * 16, 128], BF16)
    vbi = nc.dram_tensor("vbi", [1024, 2048], BF16)
    vbg = nc.dram_tensor("vbg", [4096, 2048], BF16)

    ctx = ExitStack()
    with ctx:
        sb = ctx.enter_context
        x = [sb(nc.sbuf_tensor(f"sx{i}", [128, 1024], F32)) for i in range(16)]
        xn = [sb(nc.sbuf_tensor(f"xn{i}", [128, 1024], BF16)) for i in range(16)]
        hh = [sb(nc.sbuf_tensor(f"hh{i}", [128, 1024], BF16)) for i in range(16)]
        wb = [sb(nc.sbuf_tensor(f"wb{i}", [128, 2048], BF16)) for i in range(3)]
        kb = [sb(nc.sbuf_tensor(f"kb{i}", [128, 2048], BF16)) for i in range(2)]
        vb = [sb(nc.sbuf_tensor(f"vb{i}", [128, 2048], BF16)) for i in range(4)]
        vtmp = sb(nc.sbuf_tensor("vtmp", [128, 1024], BF16))
        vtg = sb(nc.sbuf_tensor("vtg", [128, 1024], BF16))
        tb0 = sb(nc.sbuf_tensor("tb0", [128, 1024], F32))
        dbgt = sb(nc.sbuf_tensor("dbgt", [128, 1024], F32)) if dumps else None
        ts1 = sb(nc.sbuf_tensor("ts1", [128, 1024], F32))
        aB = sb(nc.sbuf_tensor("aB", [128, 1024], F32))
        bB = sb(nc.sbuf_tensor("bB", [128, 1024], F32))
        cB = sb(nc.sbuf_tensor("cB", [128, 1024], F32))
        sgB = sb(nc.sbuf_tensor("sgB", [128, 1024], F32))
        mrow = sb(nc.sbuf_tensor("mrow", [1, 1024], F32))
        vrow = sb(nc.sbuf_tensor("vrow", [1, 1024], F32))
        rwall = sb(nc.sbuf_tensor("rwall", [128, 128], F32))
        rsc = sb(nc.sbuf_tensor("rsc", [128, 8], F32))
        mxs = sb(nc.sbuf_tensor("mxs", [128, 1], F32))
        msks = sb(nc.sbuf_tensor("msks", [128, 16], F32))
        rw2s = sb(nc.sbuf_tensor("rw2s", [128, 16], F32))
        cofs = sb(nc.sbuf_tensor("cofs", [128, 128], F32))
        coefT = sb(nc.sbuf_tensor("coefT", [16, 1024], F32))
        eyef = sb(nc.sbuf_tensor("eyef", [128, 128], F32))
        eyeb = sb(nc.sbuf_tensor("eyeb", [128, 128], BF16))
        onescf = sb(nc.sbuf_tensor("onescf", [1, 128], F32))
        oneskb = sb(nc.sbuf_tensor("oneskb", [128, 1], BF16))
        grhsb = sb(nc.sbuf_tensor("grhsb", [128, 256], BF16))
        sgwb = sb(nc.sbuf_tensor("sgwb", [128, 16], BF16))
        P = [sb(nc.psum_tensor(f"P{i}", [128, 512], F32)) for i in range(6)]
        p6b = sb(nc.psum_tensor("p6b", [128, 1024], BF16))
        p7b = sb(nc.psum_tensor("p7b", [128, 1024], BF16))
        sems = {n: sb(nc.semaphore(n)) for n in SEM_NAMES}
        block = sb(nc.Block())

        srcs = []
        for call in range(2 if stage >= 3 else 1):
            for e in range(NEXP):
                for c in range(22):
                    srcs.append((dw13[e, c, :, :], 2048))
                for f in range(16):
                    srcs.append((dw2[e, f, :, :], 1408))
            if call == 0 and stage >= 2:
                for i in range(16):
                    srcs.append((dwk[i, :, :], 2048))
                for i in range(16):
                    srcs.append((dwv[i, :, :], 2048))
                for i in range(16):
                    srcs.append((dwq[i, :, :], 2048))

        def emit_all(p: Prog):
            ws = WStream(p, [lambda w, s=wb[0]: s[:, 0:w],
                             lambda w, s=wb[1]: s[:, 0:w],
                             lambda w, s=wb[2]: s[:, 0:w]], srcs)
            st_total = [0]

            p.op("g", lambda g: g.dma_start(out=grhsb[:, :], in_=dgrhs[:, :]), inc="xld", n=16)
            p.op("g", lambda g: g.dma_start(out=sgwb[:, :], in_=dsgw[:, :]), inc="xld", n=16)
            p.op("g", lambda g: g.dma_start(out=eyeb[:, :], in_=deye[:, :]), inc="xld", n=16)
            p.op("g", lambda g: g.dma_start(out=oneskb[:, :], in_=donesk[:, :]), inc="xld", n=16)
            ws.emit_load(0)
            ws.emit_load(1)
            ws.emit_load(2)
            p.op("y", lambda y: y.dma_start(out=eyef[:, :], in_=deye[:, :]), inc="xld", n=16)
            p.op("y", lambda y: y.dma_start(out=onescf[:, :], in_=donesc[:, :]), inc="xld", n=16)
            consts_ld = p.c["xld"]
            for t in range(16):
                p.op("y", lambda y, t=t: y.dma_start(out=x[t][:, :], in_=dx0[t * 128:(t + 1) * 128, :]),
                     inc="xld", n=16)
            pro_ld = p.c["xld"]
            consts_ld = pro_ld

            state = {"xn_ready": 0, "w2_end": 0, "xn_free_pe": 0, "x_ready_dve": 0,
                     "x_ready_xld": pro_ld}

            def dump(nm, ap, rows=128, cols=1024):
                if nm not in ddbg:
                    return
                w = [("pe", p.c["pe"]), ("act", p.c["act"]), ("st", p.c["st"])]
                p.op("v", lambda v: v.tensor_copy(dbgt[0:rows, 0:cols], ap), waits=w,
                     inc="dve")
                p.op("y", lambda y, nm=nm: y.dma_start(
                    out=ddbg[nm][0:rows, 0:cols], in_=dbgt[0:rows, 0:cols]),
                    waits=[("dve", p.c["dve"])], inc="st", n=16)

            def emit_ln():
                for t in range(16):
                    w = [("xld", state["x_ready_xld"]), ("pe", state["xn_free_pe"]),
                         ("dve", state["x_ready_dve"])] if t == 0 else []
                    if t == 0:
                        w.append(("xld", consts_ld))
                    p.op("s", lambda s, t=t: s.activation(xn[t][:, :], x[t][:, :], AF.Copy),
                         waits=w, inc="act")
                    a_copy = p.c["act"]
                    for sp in range(2):
                        bank = f"P{sp}"
                        bw = p.bank_wwaits(bank) if t == 0 else []
                        p.op("t", lambda te, t=t, sp=sp: te.matmul(
                            P[sp][0:1, :], oneskb[:, 0:1], xn[t][:, sp * 512:(sp + 1) * 512],
                            start=(t == 0), stop=(t == 15), skip_group_check=True),
                            waits=bw + [("act", a_copy)], inc="pe")
                        if t == 15:
                            p.bank_written(bank)
                sum_done = p.c["pe"]
                for t in range(16):
                    p.op("s", lambda s, t=t: s.activation(xn[t][:, :], x[t][:, :], AF.Square),
                         waits=[("pe", sum_done)] if t == 0 else [], inc="act")
                    a_sq = p.c["act"]
                    for sp in range(2):
                        bank = f"P{2 + sp}"
                        bw = p.bank_wwaits(bank) if t == 0 else []
                        p.op("t", lambda te, t=t, sp=sp: te.matmul(
                            P[2 + sp][0:1, :], oneskb[:, 0:1], xn[t][:, sp * 512:(sp + 1) * 512],
                            start=(t == 0), stop=(t == 15), skip_group_check=True),
                            waits=bw + [("act", a_sq)], inc="pe")
                        if t == 15:
                            p.bank_written(bank)
                sq_done = p.c["pe"]
                for sp in range(2):
                    sl = slice(sp * 512, (sp + 1) * 512)
                    p.op("v", lambda v, sp=sp, sl=sl: v.tensor_scalar_mul(
                        mrow[0:1, sl], P[sp][0:1, :], 1.0 / 2048.0),
                        waits=[("pe", sq_done)], inc="dve")
                    p.bank_read(f"P{sp}", "dve")
                    p.op("v", lambda v, sl=sl: v.tensor_mul(
                        vrow[0:1, sl], mrow[0:1, sl], mrow[0:1, sl]), inc="dve")
                    p.op("v", lambda v, sp=sp, sl=sl: v.scalar_tensor_tensor(
                        vrow[0:1, sl], P[2 + sp][0:1, :], 1.0 / 2048.0, vrow[0:1, sl],
                        ALU.mult, ALU.subtract), inc="dve")
                    p.bank_read(f"P{2 + sp}", "dve")
                for sp in range(2):
                    sl = slice(sp * 512, (sp + 1) * 512)
                    p.op("v", lambda v, sl=sl: v.tensor_scalar_add(
                        vrow[0:1, sl], vrow[0:1, sl], EPS), inc="dve")
                row_v = p.c["dve"]
                for sp in range(2):
                    sl = slice(sp * 512, (sp + 1) * 512)
                    p.op("s", lambda s, sl=sl: s.activation(
                        vrow[0:1, sl], vrow[0:1, sl], AF.Ln),
                        waits=[("dve", row_v)] if sp == 0 else [], inc="act")
                    p.op("s", lambda s, sl=sl: s.activation(
                        vrow[0:1, sl], vrow[0:1, sl], AF.Exp, scale=-0.5), inc="act")
                row_r = p.c["act"]
                for sp in range(2):
                    sl = slice(sp * 512, (sp + 1) * 512)
                    p.op("v", lambda v, sl=sl: v.scalar_tensor_tensor(
                        mrow[0:1, sl], mrow[0:1, sl], -1.0, vrow[0:1, sl],
                        ALU.mult, ALU.mult),
                        waits=[("act", row_r)] if sp == 0 else [], inc="dve")
                row_b = p.c["dve"]
                for sp in range(2):
                    sl = slice(sp * 512, (sp + 1) * 512)
                    for (dst, src, rw) in ((aB, vrow, row_r), (bB, mrow, row_b)):
                        bank = "P4" if dst is aB else "P5"
                        pi = P[4] if dst is aB else P[5]
                        p.op("t", lambda te, pi=pi, src=src, sl=sl: te.matmul(
                            pi[:, :], onescf[0:1, :], src[0:1, sl],
                            start=True, stop=True, skip_group_check=True),
                            waits=p.bank_wwaits(bank) + [("dve", row_b), ("act", row_r)],
                            inc="pe")
                        p.bank_written(bank)
                        p.op("v", lambda v, dst=dst, pi=pi, sl=sl: v.tensor_copy(
                            dst[:, sl], pi[:, :]),
                            waits=p.bank_rwaits(bank), inc="dve")
                        p.bank_read(bank, "dve")
                ab_done = p.c["dve"]
                for t in range(16):
                    w = [("dve", ab_done), ("pe", sq_done)] if t == 0 else []
                    p.op("v", lambda v, t=t: v.tensor_mul(ts1[:, :], x[t][:, :], aB[:, :]),
                         waits=w, inc="dve")
                    p.op("v", lambda v, t=t: v.tensor_add(xn[t][:, :], ts1[:, :], bB[:, :]),
                         inc="dve")
                state["xn_ready"] = p.c["dve"]
                dump("xn0", xn[0][:, :])
                dump("ab", aB[:, :])
                dump("bb", bB[:, :])

            def emit_sgate_router():
                for sp in range(2):
                    bank = f"P{4 + sp}"
                    pi = P[4 + sp]
                    for kt in range(16):
                        w = p.bank_wwaits(bank) + [("dve", state["xn_ready"])] if kt == 0 else []
                        p.op("t", lambda te, pi=pi, kt=kt, sp=sp: te.matmul(
                            pi[0:1, :], sgwb[:, kt:kt + 1],
                            xn[kt][:, sp * 512:(sp + 1) * 512],
                            start=(kt == 0), stop=(kt == 15), skip_group_check=True),
                            waits=w, inc="pe")
                    p.bank_written(bank)
                    p.op("s", lambda s, pi=pi, sp=sp: s.activation(
                        mrow[0:1, sp * 512:(sp + 1) * 512], pi[0:1, :], AF.Sigmoid),
                        waits=p.bank_rwaits(bank), inc="act")
                    p.bank_read(bank, "act")
                sg_row = p.c["act"]
                for sp in range(2):
                    bank = f"P{4 + sp}"
                    pi = P[4 + sp]
                    sl = slice(sp * 512, (sp + 1) * 512)
                    p.op("t", lambda te, pi=pi, sl=sl: te.matmul(
                        pi[:, :], onescf[0:1, :], mrow[0:1, sl],
                        start=True, stop=True, skip_group_check=True),
                        waits=p.bank_wwaits(bank) + [("act", sg_row)], inc="pe")
                    p.bank_written(bank)
                    p.op("v", lambda v, pi=pi, sl=sl: v.tensor_scalar_mul(
                        sgB[:, sl], pi[:, :], 0.5),
                        waits=p.bank_rwaits(bank), inc="dve")
                    p.bank_read(bank, "dve")
                aexp_hist = [0, 0]
                lnx_hist = [0, 0]
                rinv_hist = [0, 0]

                def ved(fn, waits=()):
                    val = p.op("v", fn, waits=waits, inc="dve")
                    p.wait("v", [("dve", val)])
                    return val

                lg_pe = []
                for t in range(8):
                    bank = "P4" if t < 4 else "P5"
                    pi = P[4] if t < 4 else P[5]
                    cols = slice((t % 4) * 16, (t % 4) * 16 + 16)
                    for kt in range(16):
                        w = p.bank_wwaits(bank) + ([("dve", state["xn_ready"])] if t == 0 else []) \
                            if kt == 0 and t % 4 == 0 else []
                        p.op("t", lambda te, pi=pi, kt=kt, t=t, cols=cols: te.matmul(
                            pi[:, cols], xn[kt][:, t * 128:(t + 1) * 128],
                            grhsb[:, kt * 16:(kt + 1) * 16],
                            start=(kt == 0), stop=(kt == 15), skip_group_check=True),
                            waits=w, inc="pe")
                    p.bank_written(bank)
                    lg_pe.append(p.c["pe"])
                coef_dve = []
                for t in range(8):
                    par = t % 2
                    bank = "P4" if t < 4 else "P5"
                    pi = P[4] if t < 4 else P[5]
                    cols = slice((t % 4) * 16, (t % 4) * 16 + 16)
                    nmc = rsc[:, par:par + 1]
                    sumc = rsc[:, 2 + par:3 + par]
                    rinvc = rsc[:, 4 + par:5 + par]
                    coefc = cofs[:, t * 16:(t + 1) * 16]
                    bank_done = lg_pe[3] if t < 4 else lg_pe[7]
                    nm = p.op("v", lambda v, pi=pi, nmc=nmc, cols=cols: v.reduce_max(
                        nmc, pi[:, cols], axis=AX, negate=True),
                        waits=[("pe", bank_done), ("act", aexp_hist[par])], inc="dve")
                    rsl = slice(t * 16, (t + 1) * 16)
                    p.op("s", lambda s, pi=pi, rsl=rsl, nmc=nmc, cols=cols: s.activation(
                        rwall[:, rsl], pi[:, cols], AF.Exp, bias=nmc[:, 0:1]),
                        waits=[("pe", bank_done), ("dve", nm)], inc="act")
                    p.bank_read(bank, "act")
                    p.bank_read(bank, "dve")
                    aexp = p.c["act"]
                    aexp_hist[par] = aexp
                    sm = p.op("v", lambda v, rsl=rsl, sumc=sumc: v.reduce_sum(
                        sumc, rwall[:, rsl], axis=AX),
                        waits=[("act", max(aexp, lnx_hist[par]))], inc="dve")
                    p.op("s", lambda s, sumc=sumc, rinvc=rinvc: s.activation(
                        rinvc, sumc, AF.Ln),
                        waits=[("dve", max(sm, rinv_hist[par]))], inc="act")
                    p.op("s", lambda s, rinvc=rinvc: s.activation(
                        rinvc, rinvc, AF.Exp, scale=-1.0), inc="act")
                    arinv = p.c["act"]
                    rinv_hist[par] = ved(lambda v, rsl=rsl, rinvc=rinvc: v.tensor_scalar_mul(
                        rwall[:, rsl], rwall[:, rsl], rinvc), waits=[("act", arinv)])
                    for it in range(4):
                        srcv = rwall[:, rsl] if it == 0 else rw2s[:, :]
                        ved(lambda v, srcv=srcv: v.reduce_max(mxs[:, 0:1], srcv, axis=AX))
                        if it < 3:
                            ved(lambda v, srcv=srcv: v.tensor_scalar(
                                msks[:, :], srcv, mxs[:, 0:1], None, ALU.is_lt))
                            ved(lambda v, srcv=srcv: v.tensor_mul(
                                rw2s[:, :], srcv, msks[:, :]))
                    ved(lambda v, rsl=rsl: v.tensor_scalar(
                        msks[:, :], rwall[:, rsl], mxs[:, 0:1], None, ALU.is_ge))
                    cd = ved(lambda v, rsl=rsl, coefc=coefc: v.scalar_tensor_tensor(
                        coefc, rwall[:, rsl], 0.5, msks[:, :],
                        ALU.mult, ALU.mult))
                    coef_dve.append(cd)
                state["rt_pending"] = coef_dve
                state["coefT_ready"] = p.c["dve"]
                dump("sgb", sgB[:, :])
                dump("rw", rwall[:, :], cols=128)
                dump("coeft", coefT[:, :], rows=16)

            def emit_router_transposes():
                coef_dve = state.pop("rt_pending")
                for t in range(8):
                    coefc = cofs[:, t * 16:(t + 1) * 16]
                    p.op("t", lambda te, coefc=coefc: te.transpose(
                        P[4][0:16, 0:128], coefc, eyef[:, :]),
                        waits=p.bank_wwaits("P4") + [("dve", coef_dve[t]),
                                                     ("xld", consts_ld)],
                        inc="pe")
                    p.bank_written("P4")
                    p.op("v", lambda v, t=t: v.tensor_copy(
                        coefT[:, t * 128:(t + 1) * 128], P[4][0:16, 0:128]),
                        waits=p.bank_rwaits("P4"), inc="dve")
                    p.bank_read("P4", "dve")
                state["coefT_ready"] = p.c["dve"]

            def emit_expert(e):
                routed = e >= 4
                er = e - 4
                cb = cB if routed else sgB
                if routed:
                    for sp in range(2):
                        bank = f"P{2 + sp}"
                        pi = P[2 + sp]
                        sl = slice(sp * 512, (sp + 1) * 512)
                        p.op("t", lambda te, pi=pi, sl=sl, er=er: te.matmul(
                            pi[0:1, :], eyef[0:16, er:er + 1], coefT[0:16, sl],
                            start=True, stop=True, skip_group_check=True),
                            waits=p.bank_wwaits(bank) + [("dve", state["coefT_ready"])],
                            inc="pe")
                        p.bank_written(bank)
                        p.op("v", lambda v, pi=pi, sl=sl: v.tensor_copy(mrow[0:1, sl], pi[0:1, :]),
                             waits=p.bank_rwaits(bank), inc="dve")
                        p.bank_read(bank, "dve")
                    row_done = p.c["dve"]
                    for sp in range(2):
                        bank = f"P{2 + sp}"
                        pi = P[2 + sp]
                        sl = slice(sp * 512, (sp + 1) * 512)
                        p.op("t", lambda te, pi=pi, sl=sl: te.matmul(
                            pi[:, :], onescf[0:1, :], mrow[0:1, sl],
                            start=True, stop=True, skip_group_check=True),
                            waits=p.bank_wwaits(bank) + [("dve", row_done)],
                            inc="pe")
                        p.bank_written(bank)
                        p.op("v", lambda v, pi=pi, sl=sl: v.tensor_copy(cB[:, sl], pi[:, :]),
                             waits=p.bank_rwaits(bank), inc="dve")
                        p.bank_read(bank, "dve")
                cb_ready = p.c["dve"]
                hprev = [0, 0]
                for c in range(11):
                    if e == 0 and c == 5 and state.get("rt_pending"):
                        emit_router_transposes()
                    sil = [0, 0]
                    for half in range(2):
                        i = len(ws.consumed)
                        bk = (0, 1) if half == 0 else (2, 3)
                        for sp in range(2):
                            bank = f"P{bk[sp]}"
                            w = []
                            if sp == 0:
                                w += ws.consume_waits(i)
                                w += [("dve", max(state["xn_ready"], cb_ready))]
                            w += p.bank_wwaits(bank)
                            par = i % WStream.NS
                            for kt in range(16):
                                p.op("t", lambda te, bk=bk, sp=sp, par=par, kt=kt: te.matmul(
                                    P[bk[sp]][:, :], wb[par][:, kt * 128:(kt + 1) * 128],
                                    xn[kt][:, sp * 512:(sp + 1) * 512],
                                    start=(kt == 0), stop=(kt == 15), skip_group_check=True),
                                    waits=w if kt == 0 else [], inc="pe" if kt == 15 else None)
                            p.bank_written(bank)
                        ws.mark_consumed(i)
                        if half == 0:
                            for sp in range(2):
                                bank = f"P{sp}"
                                p.op("s", lambda s, sp=sp, c=c: s.activation(
                                    ts1[:, sp * 512:(sp + 1) * 512], P[sp][:, :], AF.Silu),
                                    waits=p.bank_rwaits(bank) +
                                    [("dve", hprev[sp])], inc="act")
                                p.bank_read(bank, "act")
                                sil[sp] = p.c["act"]
                    for sp in range(2):
                        bank = f"P{2 + sp}"
                        sl = slice(sp * 512, (sp + 1) * 512)
                        p.op("v", lambda v, sp=sp, sl=sl, cb=cb: v.tensor_mul(
                            tb0[:, sl], P[2 + sp][:, :], cb[:, sl]),
                            waits=p.bank_rwaits(bank), inc="dve")
                        p.bank_read(bank, "dve")
                        p.op("v", lambda v, c=c, sl=sl: v.tensor_mul(
                            hh[c][:, sl], tb0[:, sl], ts1[:, sl]),
                            waits=[("act", sil[sp]), ("pe", state["w2_end"])], inc="dve")
                        hprev[sp] = p.c["dve"]
                h_done = p.c["dve"]
                if e == 4:
                    dump("cb0", cB[:, :])
                    dump("h00", hh[0][:, :])
                for f in range(16):
                    i = len(ws.consumed)
                    bank = f"P{4 + f % 2}"
                    pi = P[4 + f % 2]
                    par = i % WStream.NS
                    evs = [0, 0]
                    for sp in range(2):
                        w = []
                        if sp == 0:
                            w += ws.consume_waits(i)
                            if f == 0:
                                w += [("dve", h_done)]
                        w += p.bank_wwaits(bank)
                        for kt in range(11):
                            p.op("t", lambda te, pi=pi, par=par, kt=kt, sp=sp: te.matmul(
                                pi[:, :], wb[par][:, kt * 128:(kt + 1) * 128],
                                hh[kt][:, sp * 512:(sp + 1) * 512],
                                start=(kt == 0), stop=(kt == 10), skip_group_check=True),
                                waits=w if kt == 0 else [], inc="pe" if kt == 10 else None)
                        p.bank_written(bank)
                        p.op("v", lambda v, pi=pi, f=f, sp=sp: v.scalar_tensor_tensor(
                            x[f][:, sp * 512:(sp + 1) * 512], pi[:, :], 1.0,
                            x[f][:, sp * 512:(sp + 1) * 512], ALU.mult, ALU.add),
                            waits=p.bank_rwaits(bank), inc="dve")
                        p.bank_read(bank, "dve")
                    ws.mark_consumed(i)
                state["w2_end"] = p.c["pe"]

            def emit_ffn():
                emit_ln()
                emit_sgate_router()
                for e in range(NEXP):
                    emit_expert(e)
                state["x_ready_dve"] = p.c["dve"]
                state["xn_free_pe"] = p.c["pe"]

            def emit_qkv():
                emit_ln()
                tab_consumed = {"c": p.c["dve"], "s": p.c["dve"]}

                def emit_pair(i, dcs, dss, out_lo, out_hi, store):
                    p.op("y", lambda y, i=i, dcs=dcs: y.dma_start(
                        out=tb0[:, :], in_=dcs[i * 128:(i + 1) * 128, :]),
                        waits=[("dve", tab_consumed["c"])], inc="tld0", n=16)
                    p.op("y", lambda y, i=i, dss=dss: y.dma_start(
                        out=cB[:, :], in_=dss[i * 128:(i + 1) * 128, :]),
                        waits=[("dve", tab_consumed["s"])], inc="tld1", n=16)
                    tl0, tl1 = p.c["tld0"], p.c["tld1"]
                    for half in range(2):
                        i_ch = len(ws.consumed)
                        bk = (0, 1) if half == 0 else (2, 3)
                        par = i_ch % WStream.NS
                        for sp in range(2):
                            bank = f"P{bk[sp]}"
                            w = (ws.consume_waits(i_ch) if sp == 0 else []) + p.bank_wwaits(bank)
                            if sp == 0 and half == 0:
                                w += [("dve", state["xn_ready"])]
                            for kt in range(16):
                                p.op("t", lambda te, bk=bk, sp=sp, par=par, kt=kt: te.matmul(
                                    P[bk[sp]][:, :], wb[par][:, kt * 128:(kt + 1) * 128],
                                    xn[kt][:, sp * 512:(sp + 1) * 512],
                                    start=(kt == 0), stop=(kt == 15), skip_group_check=True),
                                    waits=w if kt == 0 else [], inc="pe" if kt == 15 else None)
                            p.bank_written(bank)
                        ws.mark_consumed(i_ch)
                    for sp in range(2):
                        sl = slice(sp * 512, (sp + 1) * 512)
                        blo, bup = f"P{sp}", f"P{2 + sp}"
                        w0 = p.bank_rwaits(blo) + p.bank_rwaits(bup) + \
                            [("tld0", tl0), ("tld1", tl1)]
                        if store is not None:
                            w0 += store["stage_free"]
                        p.op("v", lambda v, sp=sp, sl=sl: v.tensor_mul(
                            ts1[:, sl], P[sp][:, :], tb0[:, sl]), waits=w0, inc="dve")
                        p.op("v", lambda v, sp=sp, sl=sl: v.tensor_mul(
                            sgB[:, sl], P[2 + sp][:, :], cB[:, sl]), inc="dve")
                        p.op("v", lambda v, sl=sl, out_lo=out_lo: v.tensor_sub(
                            out_lo[:, sl], ts1[:, sl], sgB[:, sl]), inc="dve")
                        p.op("v", lambda v, sp=sp, sl=sl: v.tensor_mul(
                            ts1[:, sl], P[2 + sp][:, :], tb0[:, sl]), inc="dve")
                        p.bank_read(bup, "dve")
                        tab_consumed["c"] = p.c["dve"]
                        p.op("v", lambda v, sp=sp, sl=sl: v.tensor_mul(
                            sgB[:, sl], P[sp][:, :], cB[:, sl]), inc="dve")
                        p.bank_read(blo, "dve")
                        tab_consumed["s"] = p.c["dve"]
                        p.op("v", lambda v, sl=sl, out_hi=out_hi: v.tensor_add(
                            out_hi[:, sl], ts1[:, sl], sgB[:, sl]), inc="dve")
                    return p.c["dve"]

                stage_free = [[], []]
                for i in range(8):
                    lo, hi = hh[11 + 2 * (i % 2)], hh[12 + 2 * (i % 2)]
                    dn = emit_pair(i, dcsk, dssk, lo, hi,
                                   {"stage_free": stage_free[i % 2]})
                    ks = "kst%d" % (i % 2)
                    for (tile, ht) in ((lo, i), (hi, i + 8)):
                        p.op("y", lambda y, tile=tile, ht=ht: y.dma_start(
                            out=kbi[:, :, ht, :].rearrange("l p m -> p l m"),
                            in_=tile[:, :].rearrange("p (l m) -> p l m", l=8)),
                            waits=[("dve", dn)], inc=ks, n=16)
                    stage_free[i % 2] = [(ks, p.c[ks])]
                k_store_done = [("kst0", p.c["kst0"]), ("kst1", p.c["kst1"])]
                p.op("g", lambda g: g.collective_compute(
                    "AllGather", ALU.bypass,
                    replica_groups=[[0, 1, 2, 3], [4, 5, 6, 7]],
                    ins=[kbi.ap().rearrange("a p h m -> (a p h) m").opt()],
                    outs=[kbg.ap().opt()]),
                    waits=k_store_done, inc="cc")
                vst = [0]
                for f in range(16):
                    i_ch = len(ws.consumed)
                    par = i_ch % WStream.NS
                    for sp in range(2):
                        bank = f"P{4 + sp}"
                        w = (ws.consume_waits(i_ch) if sp == 0 else []) + p.bank_wwaits(bank)
                        for kt in range(16):
                            p.op("t", lambda te, sp=sp, par=par, kt=kt: te.matmul(
                                P[4 + sp][:, :], wb[par][:, kt * 128:(kt + 1) * 128],
                                xn[kt][:, sp * 512:(sp + 1) * 512],
                                start=(kt == 0), stop=(kt == 15), skip_group_check=True),
                                waits=w if kt == 0 else [], inc="pe" if kt == 15 else None)
                        p.bank_written(bank)
                    ws.mark_consumed(i_ch)
                    for sp in range(2):
                        bank = f"P{4 + sp}"
                        w = p.bank_rwaits(bank)
                        if f > 0 and sp == 0:
                            w += [("vst", vst[0])]
                        p.op("v", lambda v, sp=sp: v.tensor_copy(
                            vtmp[:, sp * 512:(sp + 1) * 512], P[4 + sp][:, :]),
                            waits=w, inc="dve")
                        p.bank_read(bank, "dve")
                    vt_done = p.c["dve"]
                    for m in range(8):
                        pb = p6b if m % 2 == 0 else p7b
                        bname = "p6b" if m % 2 == 0 else "p7b"
                        p.op("t", lambda te, pb=pb, m=m: te.transpose(
                            pb[:, 0:128], vtmp[:, m * 128:(m + 1) * 128], eyeb[:, :]),
                            waits=p.bank_wwaits(bname) +
                            ([("dve", vt_done), ("xld", consts_ld)] if m == 0 else []),
                            inc="pe")
                        p.bank_written(bname)
                        p.op("v", lambda v, pb=pb, m=m: v.tensor_copy(
                            vtg[:, m * 128:(m + 1) * 128], pb[:, 0:128]),
                            waits=p.bank_rwaits(bname), inc="dve")
                        p.bank_read(bname, "dve")
                    p.op("y", lambda y, f=f: y.dma_start(
                        out=vbi[:, f * 128:(f + 1) * 128].rearrange("(m p) c -> p m c", p=128),
                        in_=vtg[:, :].rearrange("p (m c) -> p m c", m=8)),
                        waits=[("dve", p.c["dve"])], inc="vst", n=16)
                    vst[0] = p.c["vst"]
                p.op("g", lambda g: g.collective_compute(
                    "AllGather", ALU.bypass,
                    replica_groups=[[0, 1, 2, 3], [4, 5, 6, 7]],
                    ins=[vbi.ap().opt()],
                    outs=[vbg.ap().opt()]),
                    waits=[("vst", p.c["vst"])], inc="cc")
                for i in range(8):
                    sfree = [("pe", state["w2_end"])]
                    if i in (3, 4, 5, 6):
                        sfree = sfree + k_store_done
                    emit_pair(i, dcsq, dssq, hh[i], hh[i + 8], {"stage_free": sfree})
                state["qkv_end_pe"] = p.c["pe"]
                state["cc_done"] = p.c["cc"]

            def emit_attn():
                kq_consumed = []
                dq_consumed = []
                avg_done = []
                entry_dve = p.c["dve"]

                def load_kblock(q):
                    mb = q % 32
                    r, lb = mb // 8, mb % 8
                    rows = slice((r * 8 + lb) * 2048, (r * 8 + lb) * 2048 + 2048)
                    w = [("cc", state["cc_done"] - 1)]
                    if q >= 2:
                        w = [("pe", kq_consumed[q - 2])]
                    p.op("y", lambda y, rows=rows, q=q: y.dma_start(
                        out=kb[q % 2][:, :],
                        in_=kbg[rows, :].rearrange("(p h) m -> p (h m)", p=128)),
                        waits=w, inc="kld%d" % (q % 2), n=16)

                def load_dtile(q):
                    nc2, mb = q // 32, q % 32
                    w = [("dve", entry_dve)] if q < 2 else [("dve", dq_consumed[q - 2])]
                    sl = slice((q % 2) * 512, (q % 2) * 512 + 512)
                    p.op("y", lambda y, nc2=nc2, mb=mb, sl=sl: y.dma_start(
                        out=tb0[:, sl], in_=ddec[nc2, mb, :, :]),
                        waits=w, inc="dld%d" % (q % 2), n=16)

                def load_vblock(g2, lb):
                    mb = g2 % 8 * 4 + lb
                    nc2 = g2 // 8
                    w = [("cc", state["cc_done"])]
                    if g2 >= 1:
                        w = [("pe", avg_done[g2 - 1])]
                    p.op("y", lambda y, mb=mb, lb=lb: y.dma_start(
                        out=vb[lb][:, :], in_=vbg[mb * 128:(mb + 1) * 128, :]),
                        waits=w, inc="vld%d" % lb, n=16)

                load_kblock(0)
                load_kblock(1)
                load_dtile(0)
                load_dtile(1)
                for lb in range(4):
                    load_vblock(0, lb)

                av_last_dve = 0
                for nc2 in range(2):
                    for g in range(8):
                        g2 = nc2 * 8 + g
                        for lb in range(4):
                            q = g2 * 4 + lb
                            mb = 4 * g + lb
                            bank = f"P{q % 2}"
                            pi = P[q % 2]
                            w = [("kld%d" % (q % 2), 16 * (q // 2 + 1))] + p.bank_wwaits(bank)
                            if q == 0:
                                w += [("pe", state["qkv_end_pe"])]
                            for ht in range(16):
                                p.op("t", lambda te, pi=pi, q=q, ht=ht, nc2=nc2: te.matmul(
                                    pi[:, :], kb[q % 2][:, ht * 128:(ht + 1) * 128],
                                    hh[ht][:, nc2 * 512:(nc2 + 1) * 512],
                                    start=(ht == 0), stop=(ht == 15), skip_group_check=True),
                                    waits=w if ht == 0 else [], inc="pe" if ht == 15 else None)
                            p.bank_written(bank)
                            kq_consumed.append(p.c["pe"])
                            if q + 2 < 64:
                                load_kblock(q + 2)
                            sl = slice((q % 2) * 512, (q % 2) * 512 + 512)
                            w = p.bank_rwaits(bank) + [("dld%d" % (q % 2), 16 * (q // 2 + 1))]
                            if nc2 == 0 and g == 0 and lb == 0:
                                w += [("pe", state["qkv_end_pe"])]
                            if nc2 == 1 and g == 0 and lb == 0:
                                w += [("pe", avg_done[7])]
                            p.op("v", lambda v, pi=pi, mb=mb, sl=sl: v.tensor_mul(
                                xn[mb // 2][:, (mb % 2) * 512:(mb % 2) * 512 + 512],
                                pi[:, :], tb0[:, sl]),
                                waits=w, inc="dve")
                            p.bank_read(bank, "dve")
                            dq_consumed.append(p.c["dve"])
                            if q + 2 < 64:
                                load_dtile(q + 2)
                        attd_done = p.c["dve"]
                        for f in range(16):
                            bank = f"P{4 + f % 2}"
                            pi = P[4 + f % 2]
                            w = p.bank_wwaits(bank) + [("dve", attd_done)]
                            if f == 0:
                                w += [("vld%d" % lb2, 16 * (g2 + 1)) for lb2 in range(4)]
                            for lb in range(4):
                                mb = 4 * g + lb
                                p.op("t", lambda te, pi=pi, f=f, lb=lb, mb=mb: te.matmul(
                                    pi[:, :], vb[lb][:, f * 128:(f + 1) * 128],
                                    xn[mb // 2][:, (mb % 2) * 512:(mb % 2) * 512 + 512],
                                    start=(lb == 0), stop=(lb == 3), skip_group_check=True),
                                    waits=w if lb == 0 else [], inc="pe" if lb == 3 else None)
                            p.bank_written(bank)
                            p.op("v", lambda v, pi=pi, f=f, nc2=nc2: v.scalar_tensor_tensor(
                                x[f][:, nc2 * 512:(nc2 + 1) * 512], pi[:, :], 1.0,
                                x[f][:, nc2 * 512:(nc2 + 1) * 512], ALU.mult, ALU.add),
                                waits=p.bank_rwaits(bank), inc="dve")
                            p.bank_read(bank, "dve")
                        avg_done.append(p.c["pe"])
                        if g2 + 1 < 16:
                            for lb in range(4):
                                load_vblock(g2 + 1, lb)
                state["x_ready_dve"] = p.c["dve"]
                state["xn_free_pe"] = p.c["pe"]
                state["w2_end"] = p.c["pe"]

            emit_ffn()
            if stage >= 2:
                emit_qkv()
                emit_attn()
            if stage >= 3:
                emit_ffn()

            xfin = p.c["dve"]
            for f in range(16):
                p.op("y", lambda y, f=f: y.dma_start(
                    out=dy[f * 128:(f + 1) * 128, :], in_=x[f][:, :]),
                    waits=[("dve", xfin)] if f == 0 else [], inc="st", n=16)
            p.wait("y", [("st", p.c["st"])])

        @block.tensor
        def _(te):
            emit_all(Prog("t", te, sems))

        @block.vector
        def _(v):
            emit_all(Prog("v", v, sems))

        @block.scalar
        def _(s):
            emit_all(Prog("s", s, sems))

        @block.gpsimd
        def _(g):
            emit_all(Prog("g", g, sems))

        @block.sync
        def _(y):
            emit_all(Prog("y", y, sems))

    return nc


_CACHE = {}


def kernel(L0_attn_X, L0_attn_W_Q, L0_attn_W_K, L0_attn_W_V,
           L0_ffn_gate_w, L0_ffn_sgate_w, L0_ffn_w13, L0_ffn_w2,
           L0_ffn_shared_gate_up_w, L0_ffn_shared_down_w):
    X = np.asarray(L0_attn_X)
    shared = _prep_shared(np.asarray(L0_attn_W_Q), np.asarray(L0_attn_W_K),
                          np.asarray(L0_attn_W_V), np.asarray(L0_ffn_gate_w),
                          np.asarray(L0_ffn_sgate_w), np.asarray(L0_ffn_w13),
                          np.asarray(L0_ffn_w2), np.asarray(L0_ffn_shared_gate_up_w),
                          np.asarray(L0_ffn_shared_down_w))
    in_maps = []
    for c in range(8):
        m = dict(shared)
        m.update(_prep_core(X, c))
        in_maps.append(m)
    if "nc" not in _CACHE:
        _CACHE["nc"] = build_nc(STAGE)
    nc = _CACHE["nc"]
    res = run_bass_kernel_spmd(nc, in_maps, core_ids=list(range(8)))
    out = np.empty((2, 4096, 2048), np.float32)
    for c in range(8):
        b, j = c // 4, c % 4
        out[b, j * 1024:(j + 1) * 1024, :] = res.results[c]["y"].T
    return out


if __name__ == "__main__":
    import sys
    if len(sys.argv) > 1 and sys.argv[1] == "buildtest":
        import time
        t0 = time.time()
        nc = build_nc(int(sys.argv[2]) if len(sys.argv) > 2 else STAGE)
        print(f"build ok in {time.time() - t0:.1f}s")


# version 03
# speedup vs baseline: 1.0624x; 1.0491x over previous
import numpy as np
import ml_dtypes
from contextlib import ExitStack

import concourse.bass as bass
import concourse.mybir as mybir
from concourse.bass_utils import run_bass_kernel_spmd

F32 = mybir.dt.float32
BF16 = mybir.dt.bfloat16
AF = mybir.ActivationFunctionType
ALU = mybir.AluOpType
AX = mybir.AxisListType.X
NPBF = ml_dtypes.bfloat16

STAGE = 3
GAMMA = 0.96875
EPS = 1e-5
NEXP = 20


def _lhsT_chunks(W, order):
    K, F = W.shape
    Kt = K // 128
    arr = W.reshape(Kt, 128, F).transpose(1, 0, 2)
    out = np.empty((len(order), 128, Kt * 128), dtype=np.float32)
    for i, f in enumerate(order):
        out[i] = arr[:, :, f * 128:(f + 1) * 128].reshape(128, Kt * 128)
    return out


def _prep_shared(WQ, WK, WV, gate_w, sgate_w, w13, w2, sgu, sd):
    d = {}
    perm = np.concatenate([np.arange(0, 2048, 2), np.arange(1, 2048, 2)])
    pair_order = [x for i in range(8) for x in (i, i + 8)]
    d["wk"] = _lhsT_chunks(np.ascontiguousarray(WK[:, perm]), pair_order)
    d["wq"] = _lhsT_chunks(np.ascontiguousarray(WQ[:, perm]), pair_order)
    d["wv"] = _lhsT_chunks(np.ascontiguousarray(WV), list(range(16)))
    w13s = np.empty((NEXP, 22, 128, 2048), np.float32)
    w2s = np.empty((NEXP, 16, 128, 1408), np.float32)
    inter = [x for c in range(11) for x in (c, 11 + c)]
    SGU = np.ascontiguousarray(sgu.T)
    SD = np.ascontiguousarray(sd.T)
    for ic in range(4):
        order = [x for c in range(11) for x in (ic * 11 + c, 44 + ic * 11 + c)]
        w13s[ic] = _lhsT_chunks(SGU, order)
        w2s[ic] = _lhsT_chunks(
            np.ascontiguousarray(SD[ic * 1408:(ic + 1) * 1408]), list(range(16)))
    for e in range(16):
        w13s[4 + e] = _lhsT_chunks(np.ascontiguousarray(w13[e].T), inter)
        w2s[4 + e] = _lhsT_chunks(np.ascontiguousarray(w2[e].T), list(range(16)))
    d["w13s"] = w13s
    d["w2s"] = w2s
    d["grhs"] = np.ascontiguousarray(
        gate_w.T.reshape(16, 128, 16).transpose(1, 0, 2).reshape(128, 256))
    d["sgw"] = np.ascontiguousarray(sgate_w.reshape(16, 128).T)
    d["eye"] = np.eye(128, dtype=np.float32)
    d["onesc"] = np.ones((1, 128), np.float32)
    d["onesk"] = np.ones((128, 1), np.float32)
    return d


def _prep_core(X, c):
    b, j = c // 4, c % 4
    pos = np.arange(j * 1024, (j + 1) * 1024)
    d = {"x0": np.ascontiguousarray(X[b, pos, :].T).astype(np.float32)}
    i = np.arange(1024, dtype=np.float64)
    sv = (2.0 * i + 0.4 * 2048.0) / (1.4 * 2048.0)
    invf = 10000.0 ** (-i / 1024.0)
    p64 = pos.astype(np.float64)
    scale = sv[:, None] ** (p64[None, :] / 512.0)
    ang = invf[:, None] * p64[None, :]
    d["csq"] = (np.cos(ang) * scale).astype(np.float32)
    d["ssq"] = (np.sin(ang) * scale).astype(np.float32)
    d["csk"] = (np.cos(ang) / scale).astype(np.float32)
    d["ssk"] = (np.sin(ang) / scale).astype(np.float32)
    dec = np.zeros((2, 32, 128, 512), np.float32)
    for nc2 in range(2):
        n_idx = (2 * j + nc2) * 512 + np.arange(512).astype(np.float64)
        for mb in range(32):
            m_idx = mb * 128 + np.arange(128).astype(np.float64)
            dm = n_idx[None, :] - m_idx[:, None]
            dec[nc2, mb] = np.where(dm >= 0, GAMMA ** np.maximum(dm, 0.0), 0.0
                                    ).astype(np.float32)
    d["dec"] = dec
    return d


SEM_NAMES = ["wld0", "wld1", "wld2", "tld0", "tld1", "kld0", "kld1", "dld0", "dld1",
             "vld0", "vld1", "vld2", "vld3", "xld", "pe", "dve", "act", "cc", "st",
             "kst0", "kst1", "vst0", "vst1", "tld2", "tld3"]


class _Bank:
    __slots__ = ("writer", "readers")

    def __init__(self):
        self.writer = 0
        self.readers = {}


class Prog:

    def __init__(self, cur, eng, sems):
        self.cur = cur
        self.eng = eng
        self.sems = sems
        self.c = {n: 0 for n in SEM_NAMES}
        self.banks = {}

    def op(self, key, fn, waits=(), inc=None, n=1):
        if self.cur == key:
            for s, v in waits:
                assert v <= self.c[s], f"wait {s}>={v} but only {self.c[s]} emitted"
                if v > 0:
                    self.eng.wait_ge(self.sems[s], v)
            inst = fn(self.eng)
            if inc is not None:
                inst.then_inc(self.sems[inc], n)
        else:
            for s, v in waits:
                assert v <= self.c[s], f"wait {s}>={v} but only {self.c[s]} emitted"
        if inc is not None:
            self.c[inc] += n
            return self.c[inc]
        return None

    def wait(self, key, waits):
        if self.cur == key:
            for s, v in waits:
                assert v <= self.c[s]
                if v > 0:
                    self.eng.wait_ge(self.sems[s], v)
        else:
            for s, v in waits:
                assert v <= self.c[s]

    def bank(self, name):
        if name not in self.banks:
            self.banks[name] = _Bank()
        return self.banks[name]

    def bank_wwaits(self, name):
        return sorted(self.bank(name).readers.items())

    def bank_written(self, name):
        b = self.bank(name)
        b.writer = self.c["pe"]
        b.readers = {}

    def bank_rwaits(self, name):
        return [("pe", self.bank(name).writer)]

    def bank_read(self, name, sem):
        self.bank(name).readers[sem] = self.c[sem]


class WStream:
    NS = 3

    def __init__(self, p, slots, srcs):
        self.p = p
        self.slots = slots
        self.srcs = srcs
        self.consumed = []
        self.emitted = 0

    def emit_load(self, i):
        assert i == self.emitted
        ap, width = self.srcs[i]
        par = i % self.NS
        waits = []
        if i >= self.NS:
            waits = [("pe", self.consumed[i - self.NS])]
        slot = self.slots[par]
        self.p.op("g", lambda g, ap=ap, slot=slot, w=width:
                  g.dma_start(out=slot(w), in_=ap),
                  waits=waits, inc="wld%d" % par, n=16)
        self.emitted += 1

    def consume_waits(self, i):
        return [("wld%d" % (i % self.NS), 16 * (i // self.NS + 1))]

    def mark_consumed(self, i):
        assert len(self.consumed) == i
        self.consumed.append(self.p.c["pe"])
        if i + self.NS < len(self.srcs):
            self.emit_load(i + self.NS)



def build_nc(stage=STAGE, dumps=()):
    nc = bass.Bass("TRN2", target_bir_lowering=False)
    ddbg = {nm: nc.dram_tensor("dbg_" + nm, [128, 1024], F32, kind="ExternalOutput")
            for nm in dumps}

    dx0 = nc.dram_tensor("x0", [2048, 1024], F32, kind="ExternalInput")
    dw13 = nc.dram_tensor("w13s", [NEXP, 22, 128, 2048], F32, kind="ExternalInput")
    dw2 = nc.dram_tensor("w2s", [NEXP, 16, 128, 1408], F32, kind="ExternalInput")
    dwq = nc.dram_tensor("wq", [16, 128, 2048], F32, kind="ExternalInput")
    dwk = nc.dram_tensor("wk", [16, 128, 2048], F32, kind="ExternalInput")
    dwv = nc.dram_tensor("wv", [16, 128, 2048], F32, kind="ExternalInput")
    dgrhs = nc.dram_tensor("grhs", [128, 256], F32, kind="ExternalInput")
    dsgw = nc.dram_tensor("sgw", [128, 16], F32, kind="ExternalInput")
    deye = nc.dram_tensor("eye", [128, 128], F32, kind="ExternalInput")
    donesc = nc.dram_tensor("onesc", [1, 128], F32, kind="ExternalInput")
    donesk = nc.dram_tensor("onesk", [128, 1], F32, kind="ExternalInput")
    dcsq = nc.dram_tensor("csq", [1024, 1024], F32, kind="ExternalInput")
    dssq = nc.dram_tensor("ssq", [1024, 1024], F32, kind="ExternalInput")
    dcsk = nc.dram_tensor("csk", [1024, 1024], F32, kind="ExternalInput")
    dssk = nc.dram_tensor("ssk", [1024, 1024], F32, kind="ExternalInput")
    ddec = nc.dram_tensor("dec", [2, 32, 128, 512], F32, kind="ExternalInput")
    dy = nc.dram_tensor("y", [2048, 1024], F32, kind="ExternalOutput")
    kbi = nc.dram_tensor("kbi", [8, 128, 16, 128], BF16)
    kbg = nc.dram_tensor("kbg", [4 * 8 * 128 * 16, 128], BF16)
    vbi = nc.dram_tensor("vbi", [1024, 2048], BF16)
    vbg = nc.dram_tensor("vbg", [4096, 2048], BF16)

    ctx = ExitStack()
    with ctx:
        sb = ctx.enter_context
        x = [sb(nc.sbuf_tensor(f"sx{i}", [128, 1024], F32)) for i in range(16)]
        xn = [sb(nc.sbuf_tensor(f"xn{i}", [128, 1024], BF16)) for i in range(16)]
        hh = [sb(nc.sbuf_tensor(f"hh{i}", [128, 1024], BF16)) for i in range(16)]
        wb = [sb(nc.sbuf_tensor(f"wb{i}", [128, 2048], BF16)) for i in range(3)]
        kb = [sb(nc.sbuf_tensor(f"kb{i}", [128, 2048], BF16)) for i in range(2)]
        vb = [sb(nc.sbuf_tensor(f"vb{i}", [128, 2048], BF16)) for i in range(4)]
        tb1 = sb(nc.sbuf_tensor("tb1", [128, 1024], F32))
        tb0 = sb(nc.sbuf_tensor("tb0", [128, 1024], F32))
        dbgt = sb(nc.sbuf_tensor("dbgt", [128, 1024], F32)) if dumps else None
        ts1 = sb(nc.sbuf_tensor("ts1", [128, 1024], F32))
        aB = sb(nc.sbuf_tensor("aB", [128, 1024], F32))
        bB = sb(nc.sbuf_tensor("bB", [128, 1024], F32))
        cB = sb(nc.sbuf_tensor("cB", [128, 1024], F32))
        sgB = sb(nc.sbuf_tensor("sgB", [128, 1024], F32))
        mrow = sb(nc.sbuf_tensor("mrow", [1, 1024], F32))
        vrow = sb(nc.sbuf_tensor("vrow", [1, 1024], F32))
        rwall = sb(nc.sbuf_tensor("rwall", [128, 128], F32))
        rsc = sb(nc.sbuf_tensor("rsc", [128, 8], F32))
        mxs = sb(nc.sbuf_tensor("mxs", [128, 1], F32))
        msks = sb(nc.sbuf_tensor("msks", [128, 16], F32))
        rw2s = sb(nc.sbuf_tensor("rw2s", [128, 16], F32))
        cofs = sb(nc.sbuf_tensor("cofs", [128, 128], F32))
        coefT = sb(nc.sbuf_tensor("coefT", [16, 1024], F32))
        eyef = sb(nc.sbuf_tensor("eyef", [128, 128], F32))
        eyeb = sb(nc.sbuf_tensor("eyeb", [128, 128], BF16))
        onescf = sb(nc.sbuf_tensor("onescf", [1, 128], F32))
        oneskb = sb(nc.sbuf_tensor("oneskb", [128, 1], BF16))
        grhsb = sb(nc.sbuf_tensor("grhsb", [128, 256], BF16))
        sgwb = sb(nc.sbuf_tensor("sgwb", [128, 16], BF16))
        P = [sb(nc.psum_tensor(f"P{i}", [128, 512], F32)) for i in range(6)]
        p6b = sb(nc.psum_tensor("p6b", [128, 1024], BF16))
        p7b = sb(nc.psum_tensor("p7b", [128, 1024], BF16))
        sems = {n: sb(nc.semaphore(n)) for n in SEM_NAMES}
        block = sb(nc.Block())

        srcs = []
        for call in range(2 if stage >= 3 else 1):
            for e in range(NEXP):
                for c in range(22):
                    srcs.append((dw13[e, c, :, :], 2048))
                for f in range(16):
                    srcs.append((dw2[e, f, :, :], 1408))
            if call == 0 and stage >= 2:
                for i in range(16):
                    srcs.append((dwk[i, :, :], 2048))
                for i in range(16):
                    srcs.append((dwv[i, :, :], 2048))
                for i in range(16):
                    srcs.append((dwq[i, :, :], 2048))

        def emit_all(p: Prog):
            ws = WStream(p, [lambda w, s=wb[0]: s[:, 0:w],
                             lambda w, s=wb[1]: s[:, 0:w],
                             lambda w, s=wb[2]: s[:, 0:w]], srcs)
            st_total = [0]

            p.op("g", lambda g: g.dma_start(out=grhsb[:, :], in_=dgrhs[:, :]), inc="xld", n=16)
            p.op("g", lambda g: g.dma_start(out=sgwb[:, :], in_=dsgw[:, :]), inc="xld", n=16)
            p.op("g", lambda g: g.dma_start(out=eyeb[:, :], in_=deye[:, :]), inc="xld", n=16)
            p.op("g", lambda g: g.dma_start(out=oneskb[:, :], in_=donesk[:, :]), inc="xld", n=16)
            ws.emit_load(0)
            ws.emit_load(1)
            ws.emit_load(2)
            p.op("y", lambda y: y.dma_start(out=eyef[:, :], in_=deye[:, :]), inc="xld", n=16)
            p.op("y", lambda y: y.dma_start(out=onescf[:, :], in_=donesc[:, :]), inc="xld", n=16)
            consts_ld = p.c["xld"]
            for t in range(16):
                p.op("y", lambda y, t=t: y.dma_start(out=x[t][:, :], in_=dx0[t * 128:(t + 1) * 128, :]),
                     inc="xld", n=16)
            pro_ld = p.c["xld"]
            consts_ld = pro_ld

            state = {"xn_ready": 0, "w2_end": 0, "xn_free_pe": 0, "x_ready_dve": 0,
                     "x_ready_xld": pro_ld}

            def dump(nm, ap, rows=128, cols=1024):
                if nm not in ddbg:
                    return
                w = [("pe", p.c["pe"]), ("act", p.c["act"]), ("st", p.c["st"])]
                p.op("v", lambda v: v.tensor_copy(dbgt[0:rows, 0:cols], ap), waits=w,
                     inc="dve")
                p.op("y", lambda y, nm=nm: y.dma_start(
                    out=ddbg[nm][0:rows, 0:cols], in_=dbgt[0:rows, 0:cols]),
                    waits=[("dve", p.c["dve"])], inc="st", n=16)

            def emit_ln():
                for t in range(16):
                    w = [("xld", state["x_ready_xld"]), ("pe", state["xn_free_pe"]),
                         ("dve", state["x_ready_dve"])] if t == 0 else []
                    if t == 0:
                        w.append(("xld", consts_ld))
                    p.op("s", lambda s, t=t: s.activation(xn[t][:, :], x[t][:, :], AF.Copy),
                         waits=w, inc="act")
                    a_copy = p.c["act"]
                    for sp in range(2):
                        bank = f"P{sp}"
                        bw = p.bank_wwaits(bank) if t == 0 else []
                        p.op("t", lambda te, t=t, sp=sp: te.matmul(
                            P[sp][0:1, :], oneskb[:, 0:1], xn[t][:, sp * 512:(sp + 1) * 512],
                            start=(t == 0), stop=(t == 15), skip_group_check=True),
                            waits=bw + [("act", a_copy)], inc="pe")
                        if t == 15:
                            p.bank_written(bank)
                sum_done = p.c["pe"]
                for t in range(16):
                    p.op("s", lambda s, t=t: s.activation(xn[t][:, :], x[t][:, :], AF.Square),
                         waits=[("pe", sum_done)] if t == 0 else [], inc="act")
                    a_sq = p.c["act"]
                    for sp in range(2):
                        bank = f"P{2 + sp}"
                        bw = p.bank_wwaits(bank) if t == 0 else []
                        p.op("t", lambda te, t=t, sp=sp: te.matmul(
                            P[2 + sp][0:1, :], oneskb[:, 0:1], xn[t][:, sp * 512:(sp + 1) * 512],
                            start=(t == 0), stop=(t == 15), skip_group_check=True),
                            waits=bw + [("act", a_sq)], inc="pe")
                        if t == 15:
                            p.bank_written(bank)
                sq_done = p.c["pe"]
                for sp in range(2):
                    sl = slice(sp * 512, (sp + 1) * 512)
                    p.op("v", lambda v, sp=sp, sl=sl: v.tensor_scalar_mul(
                        mrow[0:1, sl], P[sp][0:1, :], 1.0 / 2048.0),
                        waits=[("pe", sq_done)], inc="dve")
                    p.bank_read(f"P{sp}", "dve")
                    p.op("v", lambda v, sl=sl: v.tensor_mul(
                        vrow[0:1, sl], mrow[0:1, sl], mrow[0:1, sl]), inc="dve")
                    p.op("v", lambda v, sp=sp, sl=sl: v.scalar_tensor_tensor(
                        vrow[0:1, sl], P[2 + sp][0:1, :], 1.0 / 2048.0, vrow[0:1, sl],
                        ALU.mult, ALU.subtract), inc="dve")
                    p.bank_read(f"P{2 + sp}", "dve")
                for sp in range(2):
                    sl = slice(sp * 512, (sp + 1) * 512)
                    p.op("v", lambda v, sl=sl: v.tensor_scalar_add(
                        vrow[0:1, sl], vrow[0:1, sl], EPS), inc="dve")
                row_v = p.c["dve"]
                for sp in range(2):
                    sl = slice(sp * 512, (sp + 1) * 512)
                    p.op("s", lambda s, sl=sl: s.activation(
                        vrow[0:1, sl], vrow[0:1, sl], AF.Ln),
                        waits=[("dve", row_v)] if sp == 0 else [], inc="act")
                    p.op("s", lambda s, sl=sl: s.activation(
                        vrow[0:1, sl], vrow[0:1, sl], AF.Exp, scale=-0.5), inc="act")
                row_r = p.c["act"]
                for sp in range(2):
                    sl = slice(sp * 512, (sp + 1) * 512)
                    p.op("v", lambda v, sl=sl: v.scalar_tensor_tensor(
                        mrow[0:1, sl], mrow[0:1, sl], -1.0, vrow[0:1, sl],
                        ALU.mult, ALU.mult),
                        waits=[("act", row_r)] if sp == 0 else [], inc="dve")
                row_b = p.c["dve"]
                for sp in range(2):
                    sl = slice(sp * 512, (sp + 1) * 512)
                    for (dst, src, rw) in ((aB, vrow, row_r), (bB, mrow, row_b)):
                        bank = "P4" if dst is aB else "P5"
                        pi = P[4] if dst is aB else P[5]
                        p.op("t", lambda te, pi=pi, src=src, sl=sl: te.matmul(
                            pi[:, :], onescf[0:1, :], src[0:1, sl],
                            start=True, stop=True, skip_group_check=True),
                            waits=p.bank_wwaits(bank) + [("dve", row_b), ("act", row_r)],
                            inc="pe")
                        p.bank_written(bank)
                        p.op("v", lambda v, dst=dst, pi=pi, sl=sl: v.tensor_copy(
                            dst[:, sl], pi[:, :]),
                            waits=p.bank_rwaits(bank), inc="dve")
                        p.bank_read(bank, "dve")
                ab_done = p.c["dve"]
                for t in range(16):
                    w = [("dve", ab_done), ("pe", sq_done)] if t == 0 else []
                    p.op("v", lambda v, t=t: v.tensor_mul(ts1[:, :], x[t][:, :], aB[:, :]),
                         waits=w, inc="dve")
                    p.op("v", lambda v, t=t: v.tensor_add(xn[t][:, :], ts1[:, :], bB[:, :]),
                         inc="dve")
                state["xn_ready"] = p.c["dve"]
                dump("xn0", xn[0][:, :])
                dump("ab", aB[:, :])
                dump("bb", bB[:, :])

            def emit_sgate_router():
                for sp in range(2):
                    bank = f"P{4 + sp}"
                    pi = P[4 + sp]
                    for kt in range(16):
                        w = p.bank_wwaits(bank) + [("dve", state["xn_ready"])] if kt == 0 else []
                        p.op("t", lambda te, pi=pi, kt=kt, sp=sp: te.matmul(
                            pi[0:1, :], sgwb[:, kt:kt + 1],
                            xn[kt][:, sp * 512:(sp + 1) * 512],
                            start=(kt == 0), stop=(kt == 15), skip_group_check=True),
                            waits=w, inc="pe")
                    p.bank_written(bank)
                    p.op("s", lambda s, pi=pi, sp=sp: s.activation(
                        mrow[0:1, sp * 512:(sp + 1) * 512], pi[0:1, :], AF.Sigmoid),
                        waits=p.bank_rwaits(bank), inc="act")
                    p.bank_read(bank, "act")
                sg_row = p.c["act"]
                for sp in range(2):
                    bank = f"P{4 + sp}"
                    pi = P[4 + sp]
                    sl = slice(sp * 512, (sp + 1) * 512)
                    p.op("t", lambda te, pi=pi, sl=sl: te.matmul(
                        pi[:, :], onescf[0:1, :], mrow[0:1, sl],
                        start=True, stop=True, skip_group_check=True),
                        waits=p.bank_wwaits(bank) + [("act", sg_row)], inc="pe")
                    p.bank_written(bank)
                    p.op("v", lambda v, pi=pi, sl=sl: v.tensor_scalar_mul(
                        sgB[:, sl], pi[:, :], 0.5),
                        waits=p.bank_rwaits(bank), inc="dve")
                    p.bank_read(bank, "dve")
                aexp_hist = [0, 0]
                lnx_hist = [0, 0]
                rinv_hist = [0, 0]

                def ved(fn, waits=()):
                    val = p.op("v", fn, waits=waits, inc="dve")
                    p.wait("v", [("dve", val)])
                    return val

                lg_pe = []
                for t in range(8):
                    bank = "P4" if t < 4 else "P5"
                    pi = P[4] if t < 4 else P[5]
                    cols = slice((t % 4) * 16, (t % 4) * 16 + 16)
                    for kt in range(16):
                        w = p.bank_wwaits(bank) + ([("dve", state["xn_ready"])] if t == 0 else []) \
                            if kt == 0 and t % 4 == 0 else []
                        p.op("t", lambda te, pi=pi, kt=kt, t=t, cols=cols: te.matmul(
                            pi[:, cols], xn[kt][:, t * 128:(t + 1) * 128],
                            grhsb[:, kt * 16:(kt + 1) * 16],
                            start=(kt == 0), stop=(kt == 15), skip_group_check=True),
                            waits=w, inc="pe")
                    p.bank_written(bank)
                    lg_pe.append(p.c["pe"])
                coef_dve = []
                for t in range(8):
                    par = t % 2
                    bank = "P4" if t < 4 else "P5"
                    pi = P[4] if t < 4 else P[5]
                    cols = slice((t % 4) * 16, (t % 4) * 16 + 16)
                    nmc = rsc[:, par:par + 1]
                    sumc = rsc[:, 2 + par:3 + par]
                    rinvc = rsc[:, 4 + par:5 + par]
                    coefc = cofs[:, t * 16:(t + 1) * 16]
                    bank_done = lg_pe[3] if t < 4 else lg_pe[7]
                    nm = p.op("v", lambda v, pi=pi, nmc=nmc, cols=cols: v.reduce_max(
                        nmc, pi[:, cols], axis=AX, negate=True),
                        waits=[("pe", bank_done), ("act", aexp_hist[par])], inc="dve")
                    rsl = slice(t * 16, (t + 1) * 16)
                    p.op("s", lambda s, pi=pi, rsl=rsl, nmc=nmc, cols=cols: s.activation(
                        rwall[:, rsl], pi[:, cols], AF.Exp, bias=nmc[:, 0:1]),
                        waits=[("pe", bank_done), ("dve", nm)], inc="act")
                    p.bank_read(bank, "act")
                    p.bank_read(bank, "dve")
                    aexp = p.c["act"]
                    aexp_hist[par] = aexp
                    sm = p.op("v", lambda v, rsl=rsl, sumc=sumc: v.reduce_sum(
                        sumc, rwall[:, rsl], axis=AX),
                        waits=[("act", max(aexp, lnx_hist[par]))], inc="dve")
                    p.op("s", lambda s, sumc=sumc, rinvc=rinvc: s.activation(
                        rinvc, sumc, AF.Ln),
                        waits=[("dve", max(sm, rinv_hist[par]))], inc="act")
                    p.op("s", lambda s, rinvc=rinvc: s.activation(
                        rinvc, rinvc, AF.Exp, scale=-1.0), inc="act")
                    arinv = p.c["act"]
                    rinv_hist[par] = ved(lambda v, rsl=rsl, rinvc=rinvc: v.tensor_scalar_mul(
                        rwall[:, rsl], rwall[:, rsl], rinvc), waits=[("act", arinv)])
                    for it in range(4):
                        srcv = rwall[:, rsl] if it == 0 else rw2s[:, :]
                        ved(lambda v, srcv=srcv: v.reduce_max(mxs[:, 0:1], srcv, axis=AX))
                        if it < 3:
                            ved(lambda v, srcv=srcv: v.tensor_scalar(
                                msks[:, :], srcv, mxs[:, 0:1], None, ALU.is_lt))
                            ved(lambda v, srcv=srcv: v.tensor_mul(
                                rw2s[:, :], srcv, msks[:, :]))
                    ved(lambda v, rsl=rsl: v.tensor_scalar(
                        msks[:, :], rwall[:, rsl], mxs[:, 0:1], None, ALU.is_ge))
                    cd = ved(lambda v, rsl=rsl, coefc=coefc: v.scalar_tensor_tensor(
                        coefc, rwall[:, rsl], 0.5, msks[:, :],
                        ALU.mult, ALU.mult))
                    coef_dve.append(cd)
                state["rt_pending"] = coef_dve
                state["coefT_ready"] = p.c["dve"]
                dump("sgb", sgB[:, :])
                dump("rw", rwall[:, :], cols=128)
                dump("coeft", coefT[:, :], rows=16)

            def emit_router_transposes():
                coef_dve = state.pop("rt_pending")
                for t in range(8):
                    coefc = cofs[:, t * 16:(t + 1) * 16]
                    p.op("t", lambda te, coefc=coefc: te.transpose(
                        P[4][0:16, 0:128], coefc, eyef[:, :]),
                        waits=p.bank_wwaits("P4") + [("dve", coef_dve[t]),
                                                     ("xld", consts_ld)],
                        inc="pe")
                    p.bank_written("P4")
                    p.op("v", lambda v, t=t: v.tensor_copy(
                        coefT[:, t * 128:(t + 1) * 128], P[4][0:16, 0:128]),
                        waits=p.bank_rwaits("P4"), inc="dve")
                    p.bank_read("P4", "dve")
                state["coefT_ready"] = p.c["dve"]

            def emit_expert(e):
                routed = e >= 4
                er = e - 4
                cb = cB if routed else sgB
                if routed:
                    for sp in range(2):
                        bank = f"P{2 + sp}"
                        pi = P[2 + sp]
                        sl = slice(sp * 512, (sp + 1) * 512)
                        p.op("t", lambda te, pi=pi, sl=sl, er=er: te.matmul(
                            pi[0:1, :], eyef[0:16, er:er + 1], coefT[0:16, sl],
                            start=True, stop=True, skip_group_check=True),
                            waits=p.bank_wwaits(bank) + [("dve", state["coefT_ready"])],
                            inc="pe")
                        p.bank_written(bank)
                        p.op("v", lambda v, pi=pi, sl=sl: v.tensor_copy(mrow[0:1, sl], pi[0:1, :]),
                             waits=p.bank_rwaits(bank), inc="dve")
                        p.bank_read(bank, "dve")
                    row_done = p.c["dve"]
                    for sp in range(2):
                        bank = f"P{2 + sp}"
                        pi = P[2 + sp]
                        sl = slice(sp * 512, (sp + 1) * 512)
                        p.op("t", lambda te, pi=pi, sl=sl: te.matmul(
                            pi[:, :], onescf[0:1, :], mrow[0:1, sl],
                            start=True, stop=True, skip_group_check=True),
                            waits=p.bank_wwaits(bank) + [("dve", row_done)],
                            inc="pe")
                        p.bank_written(bank)
                        p.op("v", lambda v, pi=pi, sl=sl: v.tensor_copy(cB[:, sl], pi[:, :]),
                             waits=p.bank_rwaits(bank), inc="dve")
                        p.bank_read(bank, "dve")
                cb_ready = p.c["dve"]
                hprev = [0, 0]
                for c in range(11):
                    if e == 0 and c == 5 and state.get("rt_pending"):
                        emit_router_transposes()
                    sil = [0, 0]
                    for half in range(2):
                        i = len(ws.consumed)
                        bk = (0, 1) if half == 0 else (2, 3)
                        for sp in range(2):
                            bank = f"P{bk[sp]}"
                            w = []
                            if sp == 0:
                                w += ws.consume_waits(i)
                                w += [("dve", max(state["xn_ready"], cb_ready))]
                            w += p.bank_wwaits(bank)
                            par = i % WStream.NS
                            for kt in range(16):
                                p.op("t", lambda te, bk=bk, sp=sp, par=par, kt=kt: te.matmul(
                                    P[bk[sp]][:, :], wb[par][:, kt * 128:(kt + 1) * 128],
                                    xn[kt][:, sp * 512:(sp + 1) * 512],
                                    start=(kt == 0), stop=(kt == 15), skip_group_check=True),
                                    waits=w if kt == 0 else [], inc="pe" if kt == 15 else None)
                            p.bank_written(bank)
                        ws.mark_consumed(i)
                        if half == 0:
                            for sp in range(2):
                                bank = f"P{sp}"
                                p.op("s", lambda s, sp=sp, c=c: s.activation(
                                    ts1[:, sp * 512:(sp + 1) * 512], P[sp][:, :], AF.Silu),
                                    waits=p.bank_rwaits(bank) +
                                    [("dve", hprev[sp])], inc="act")
                                p.bank_read(bank, "act")
                                sil[sp] = p.c["act"]
                    for sp in range(2):
                        bank = f"P{2 + sp}"
                        sl = slice(sp * 512, (sp + 1) * 512)
                        p.op("v", lambda v, sp=sp, sl=sl, cb=cb: v.tensor_mul(
                            tb0[:, sl], P[2 + sp][:, :], cb[:, sl]),
                            waits=p.bank_rwaits(bank), inc="dve")
                        p.bank_read(bank, "dve")
                        p.op("v", lambda v, c=c, sl=sl: v.tensor_mul(
                            hh[c][:, sl], tb0[:, sl], ts1[:, sl]),
                            waits=[("act", sil[sp]), ("pe", state["w2_end"])], inc="dve")
                        hprev[sp] = p.c["dve"]
                h_done = p.c["dve"]
                if e == 4:
                    dump("cb0", cB[:, :])
                    dump("h00", hh[0][:, :])
                for f in range(16):
                    i = len(ws.consumed)
                    bks = (4, 5) if f % 2 == 0 else (2, 3)
                    par = i % WStream.NS
                    for sp in range(2):
                        bank = f"P{bks[sp]}"
                        pi = P[bks[sp]]
                        w = []
                        if sp == 0:
                            w += ws.consume_waits(i)
                            if f == 0:
                                w += [("dve", h_done)]
                        w += p.bank_wwaits(bank)
                        for kt in range(11):
                            p.op("t", lambda te, pi=pi, par=par, kt=kt, sp=sp: te.matmul(
                                pi[:, :], wb[par][:, kt * 128:(kt + 1) * 128],
                                hh[kt][:, sp * 512:(sp + 1) * 512],
                                start=(kt == 0), stop=(kt == 10), skip_group_check=True),
                                waits=w if kt == 0 else [], inc="pe" if kt == 10 else None)
                        p.bank_written(bank)
                        p.op("v", lambda v, pi=pi, f=f, sp=sp: v.scalar_tensor_tensor(
                            x[f][:, sp * 512:(sp + 1) * 512], pi[:, :], 1.0,
                            x[f][:, sp * 512:(sp + 1) * 512], ALU.mult, ALU.add),
                            waits=p.bank_rwaits(bank), inc="dve")
                        p.bank_read(bank, "dve")
                    ws.mark_consumed(i)
                state["w2_end"] = p.c["pe"]

            def emit_ffn():
                emit_ln()
                emit_sgate_router()
                for e in range(NEXP):
                    emit_expert(e)
                state["x_ready_dve"] = p.c["dve"]
                state["xn_free_pe"] = p.c["pe"]

            def emit_qkv():
                emit_ln()
                tab_consumed = {k: p.c["dve"] for k in ("c0", "c1", "s0", "s1")}

                def emit_pair(i, dcs, dss, out_lo, out_hi, store):
                    cs_slot = tb0 if i % 2 == 0 else tb1
                    ss_slot = aB if i % 2 == 0 else bB
                    cs_sem = "tld%d" % (i % 2)
                    ss_sem = "tld%d" % (2 + i % 2)
                    p.op("y", lambda y, i=i, dcs=dcs, cs_slot=cs_slot: y.dma_start(
                        out=cs_slot[:, :], in_=dcs[i * 128:(i + 1) * 128, :]),
                        waits=[("dve", tab_consumed["c%d" % (i % 2)])], inc=cs_sem, n=16)
                    p.op("y", lambda y, i=i, dss=dss, ss_slot=ss_slot: y.dma_start(
                        out=ss_slot[:, :], in_=dss[i * 128:(i + 1) * 128, :]),
                        waits=[("dve", tab_consumed["s%d" % (i % 2)])], inc=ss_sem, n=16)
                    tlw = [(cs_sem, p.c[cs_sem]), (ss_sem, p.c[ss_sem])]
                    i_lo = len(ws.consumed)
                    i_up = i_lo + 1
                    par_lo = i_lo % WStream.NS
                    par_up = i_up % WStream.NS
                    for sp in range(2):
                        blo, bup = f"P{2 * sp}", f"P{2 * sp + 1}"
                        for half in range(2):
                            bank = blo if half == 0 else bup
                            pi = P[2 * sp + half]
                            par = par_lo if half == 0 else par_up
                            i_ch = i_lo if half == 0 else i_up
                            w = p.bank_wwaits(bank)
                            if sp == 0:
                                w += ws.consume_waits(i_ch)
                                if half == 0:
                                    w += [("dve", state["xn_ready"])]
                            for kt in range(16):
                                p.op("t", lambda te, pi=pi, sp=sp, par=par, kt=kt: te.matmul(
                                    pi[:, :], wb[par][:, kt * 128:(kt + 1) * 128],
                                    xn[kt][:, sp * 512:(sp + 1) * 512],
                                    start=(kt == 0), stop=(kt == 15), skip_group_check=True),
                                    waits=w if kt == 0 else [], inc="pe" if kt == 15 else None)
                            p.bank_written(bank)
                        sl = slice(sp * 512, (sp + 1) * 512)
                        w0 = p.bank_rwaits(blo) + p.bank_rwaits(bup) + tlw
                        if sp == 0 and store is not None:
                            w0 += store["stage_free"]
                        p.op("v", lambda v, sp=sp, sl=sl, cs_slot=cs_slot: v.tensor_mul(
                            ts1[:, sl], P[2 * sp][:, :], cs_slot[:, sl]), waits=w0, inc="dve")
                        p.op("v", lambda v, sp=sp, sl=sl, ss_slot=ss_slot: v.tensor_mul(
                            sgB[:, sl], P[2 * sp + 1][:, :], ss_slot[:, sl]), inc="dve")
                        p.op("v", lambda v, sl=sl, out_lo=out_lo: v.tensor_sub(
                            out_lo[:, sl], ts1[:, sl], sgB[:, sl]), inc="dve")
                        p.op("v", lambda v, sp=sp, sl=sl, cs_slot=cs_slot: v.tensor_mul(
                            ts1[:, sl], P[2 * sp + 1][:, :], cs_slot[:, sl]), inc="dve")
                        p.bank_read(bup, "dve")
                        p.op("v", lambda v, sp=sp, sl=sl, ss_slot=ss_slot: v.tensor_mul(
                            sgB[:, sl], P[2 * sp][:, :], ss_slot[:, sl]), inc="dve")
                        p.bank_read(blo, "dve")
                        p.op("v", lambda v, sl=sl, out_hi=out_hi: v.tensor_add(
                            out_hi[:, sl], ts1[:, sl], sgB[:, sl]), inc="dve")
                    tab_consumed["c%d" % (i % 2)] = p.c["dve"]
                    tab_consumed["s%d" % (i % 2)] = p.c["dve"]
                    ws.mark_consumed(i_lo)
                    ws.mark_consumed(i_up)
                    return p.c["dve"]

                stage_free = [[], []]
                for i in range(8):
                    lo, hi = hh[11 + 2 * (i % 2)], hh[12 + 2 * (i % 2)]
                    dn = emit_pair(i, dcsk, dssk, lo, hi,
                                   {"stage_free": stage_free[i % 2]})
                    ks = "kst%d" % (i % 2)
                    for (tile, ht) in ((lo, i), (hi, i + 8)):
                        p.op("y", lambda y, tile=tile, ht=ht: y.dma_start(
                            out=kbi[:, :, ht, :].rearrange("l p m -> p l m"),
                            in_=tile[:, :].rearrange("p (l m) -> p l m", l=8)),
                            waits=[("dve", dn)], inc=ks, n=16)
                    stage_free[i % 2] = [(ks, p.c[ks])]
                k_store_done = [("kst0", p.c["kst0"]), ("kst1", p.c["kst1"])]
                p.op("g", lambda g: g.collective_compute(
                    "AllGather", ALU.bypass,
                    replica_groups=[[0, 1, 2, 3], [4, 5, 6, 7]],
                    ins=[kbi.ap().rearrange("a p h m -> (a p h) m").opt()],
                    outs=[kbg.ap().opt()]),
                    waits=k_store_done, inc="cc")
                for f in range(16):
                    vtmp = hh[2 * (f % 2)]
                    vtg = hh[2 * (f % 2) + 1]
                    vsem = "vst%d" % (f % 2)
                    i_ch = len(ws.consumed)
                    par = i_ch % WStream.NS
                    for sp in range(2):
                        bank = f"P{4 + sp}"
                        w = (ws.consume_waits(i_ch) if sp == 0 else []) + p.bank_wwaits(bank)
                        for kt in range(16):
                            p.op("t", lambda te, sp=sp, par=par, kt=kt: te.matmul(
                                P[4 + sp][:, :], wb[par][:, kt * 128:(kt + 1) * 128],
                                xn[kt][:, sp * 512:(sp + 1) * 512],
                                start=(kt == 0), stop=(kt == 15), skip_group_check=True),
                                waits=w if kt == 0 else [], inc="pe" if kt == 15 else None)
                        p.bank_written(bank)
                    ws.mark_consumed(i_ch)
                    for sp in range(2):
                        bank = f"P{4 + sp}"
                        w = p.bank_rwaits(bank)
                        if f >= 2 and sp == 0:
                            w += [(vsem, 16 * (f // 2))]
                        p.op("v", lambda v, sp=sp, vtmp=vtmp: v.tensor_copy(
                            vtmp[:, sp * 512:(sp + 1) * 512], P[4 + sp][:, :]),
                            waits=w, inc="dve")
                        p.bank_read(bank, "dve")
                    vt_done = p.c["dve"]
                    for m in range(8):
                        pb = p6b if m % 2 == 0 else p7b
                        bname = "p6b" if m % 2 == 0 else "p7b"
                        p.op("t", lambda te, pb=pb, m=m, vtmp=vtmp: te.transpose(
                            pb[:, 0:128], vtmp[:, m * 128:(m + 1) * 128], eyeb[:, :]),
                            waits=p.bank_wwaits(bname) +
                            ([("dve", vt_done), ("xld", consts_ld)] if m == 0 else []),
                            inc="pe")
                        p.bank_written(bname)
                        p.op("v", lambda v, pb=pb, m=m, vtg=vtg: v.tensor_copy(
                            vtg[:, m * 128:(m + 1) * 128], pb[:, 0:128]),
                            waits=p.bank_rwaits(bname), inc="dve")
                        p.bank_read(bname, "dve")
                    p.op("y", lambda y, f=f, vtg=vtg: y.dma_start(
                        out=vbi[:, f * 128:(f + 1) * 128].rearrange("(m p) c -> p m c", p=128),
                        in_=vtg[:, :].rearrange("p (m c) -> p m c", m=8)),
                        waits=[("dve", p.c["dve"])], inc=vsem, n=16)
                v_store_done = [("vst0", p.c["vst0"]), ("vst1", p.c["vst1"])]
                p.op("g", lambda g: g.collective_compute(
                    "AllGather", ALU.bypass,
                    replica_groups=[[0, 1, 2, 3], [4, 5, 6, 7]],
                    ins=[vbi.ap().opt()],
                    outs=[vbg.ap().opt()]),
                    waits=v_store_done, inc="cc")
                for i in range(8):
                    sfree = [("pe", state["w2_end"])]
                    if i in (3, 4, 5, 6):
                        sfree = sfree + k_store_done
                    if i < 4:
                        sfree = sfree + v_store_done
                    emit_pair(i, dcsq, dssq, hh[i], hh[i + 8], {"stage_free": sfree})
                state["qkv_end_pe"] = p.c["pe"]
                state["cc_done"] = p.c["cc"]

            def emit_attn():
                kq_consumed = []
                dq_consumed = []
                avg_done = []
                entry_dve = p.c["dve"]

                def load_kblock(q):
                    mb = q % 32
                    r, lb = mb // 8, mb % 8
                    rows = slice((r * 8 + lb) * 2048, (r * 8 + lb) * 2048 + 2048)
                    w = [("cc", state["cc_done"] - 1)]
                    if q >= 2:
                        w = [("pe", kq_consumed[q - 2])]
                    p.op("y", lambda y, rows=rows, q=q: y.dma_start(
                        out=kb[q % 2][:, :],
                        in_=kbg[rows, :].rearrange("(p h) m -> p (h m)", p=128)),
                        waits=w, inc="kld%d" % (q % 2), n=16)

                def load_dtile(q):
                    nc2, mb = q // 32, q % 32
                    w = [("dve", entry_dve)] if q < 2 else [("dve", dq_consumed[q - 2])]
                    sl = slice((q % 2) * 512, (q % 2) * 512 + 512)
                    p.op("y", lambda y, nc2=nc2, mb=mb, sl=sl: y.dma_start(
                        out=tb0[:, sl], in_=ddec[nc2, mb, :, :]),
                        waits=w, inc="dld%d" % (q % 2), n=16)

                def load_vblock(g2, lb):
                    mb = g2 % 8 * 4 + lb
                    nc2 = g2 // 8
                    w = [("cc", state["cc_done"])]
                    if g2 >= 1:
                        w = [("pe", avg_done[g2 - 1])]
                    p.op("y", lambda y, mb=mb, lb=lb: y.dma_start(
                        out=vb[lb][:, :], in_=vbg[mb * 128:(mb + 1) * 128, :]),
                        waits=w, inc="vld%d" % lb, n=16)

                load_kblock(0)
                load_kblock(1)
                load_dtile(0)
                load_dtile(1)
                for lb in range(4):
                    load_vblock(0, lb)

                av_last_dve = 0
                for nc2 in range(2):
                    for g in range(8):
                        g2 = nc2 * 8 + g
                        for lb in range(4):
                            q = g2 * 4 + lb
                            mb = 4 * g + lb
                            bank = f"P{q % 2}"
                            pi = P[q % 2]
                            w = [("kld%d" % (q % 2), 16 * (q // 2 + 1))] + p.bank_wwaits(bank)
                            if q == 0:
                                w += [("pe", state["qkv_end_pe"])]
                            for ht in range(16):
                                p.op("t", lambda te, pi=pi, q=q, ht=ht, nc2=nc2: te.matmul(
                                    pi[:, :], kb[q % 2][:, ht * 128:(ht + 1) * 128],
                                    hh[ht][:, nc2 * 512:(nc2 + 1) * 512],
                                    start=(ht == 0), stop=(ht == 15), skip_group_check=True),
                                    waits=w if ht == 0 else [], inc="pe" if ht == 15 else None)
                            p.bank_written(bank)
                            kq_consumed.append(p.c["pe"])
                            if q + 2 < 64:
                                load_kblock(q + 2)
                            sl = slice((q % 2) * 512, (q % 2) * 512 + 512)
                            w = p.bank_rwaits(bank) + [("dld%d" % (q % 2), 16 * (q // 2 + 1))]
                            if nc2 == 0 and g == 0 and lb == 0:
                                w += [("pe", state["qkv_end_pe"])]
                            if nc2 == 1 and g == 0 and lb == 0:
                                w += [("pe", avg_done[7])]
                            p.op("v", lambda v, pi=pi, mb=mb, sl=sl: v.tensor_mul(
                                xn[mb // 2][:, (mb % 2) * 512:(mb % 2) * 512 + 512],
                                pi[:, :], tb0[:, sl]),
                                waits=w, inc="dve")
                            p.bank_read(bank, "dve")
                            dq_consumed.append(p.c["dve"])
                            if q + 2 < 64:
                                load_dtile(q + 2)
                        attd_done = p.c["dve"]
                        for f in range(16):
                            bank = f"P{4 + f % 2}"
                            pi = P[4 + f % 2]
                            w = p.bank_wwaits(bank) + [("dve", attd_done)]
                            if f == 0:
                                w += [("vld%d" % lb2, 16 * (g2 + 1)) for lb2 in range(4)]
                            for lb in range(4):
                                mb = 4 * g + lb
                                p.op("t", lambda te, pi=pi, f=f, lb=lb, mb=mb: te.matmul(
                                    pi[:, :], vb[lb][:, f * 128:(f + 1) * 128],
                                    xn[mb // 2][:, (mb % 2) * 512:(mb % 2) * 512 + 512],
                                    start=(lb == 0), stop=(lb == 3), skip_group_check=True),
                                    waits=w if lb == 0 else [], inc="pe" if lb == 3 else None)
                            p.bank_written(bank)
                            p.op("v", lambda v, pi=pi, f=f, nc2=nc2: v.scalar_tensor_tensor(
                                x[f][:, nc2 * 512:(nc2 + 1) * 512], pi[:, :], 1.0,
                                x[f][:, nc2 * 512:(nc2 + 1) * 512], ALU.mult, ALU.add),
                                waits=p.bank_rwaits(bank), inc="dve")
                            p.bank_read(bank, "dve")
                        avg_done.append(p.c["pe"])
                        if g2 + 1 < 16:
                            for lb in range(4):
                                load_vblock(g2 + 1, lb)
                state["x_ready_dve"] = p.c["dve"]
                state["xn_free_pe"] = p.c["pe"]
                state["w2_end"] = p.c["pe"]

            emit_ffn()
            if stage >= 2:
                emit_qkv()
                emit_attn()
            if stage >= 3:
                emit_ffn()

            xfin = p.c["dve"]
            for f in range(16):
                p.op("y", lambda y, f=f: y.dma_start(
                    out=dy[f * 128:(f + 1) * 128, :], in_=x[f][:, :]),
                    waits=[("dve", xfin)] if f == 0 else [], inc="st", n=16)
            p.wait("y", [("st", p.c["st"])])

        @block.tensor
        def _(te):
            emit_all(Prog("t", te, sems))

        @block.vector
        def _(v):
            emit_all(Prog("v", v, sems))

        @block.scalar
        def _(s):
            emit_all(Prog("s", s, sems))

        @block.gpsimd
        def _(g):
            emit_all(Prog("g", g, sems))

        @block.sync
        def _(y):
            emit_all(Prog("y", y, sems))

    return nc


_CACHE = {}


def kernel(L0_attn_X, L0_attn_W_Q, L0_attn_W_K, L0_attn_W_V,
           L0_ffn_gate_w, L0_ffn_sgate_w, L0_ffn_w13, L0_ffn_w2,
           L0_ffn_shared_gate_up_w, L0_ffn_shared_down_w):
    X = np.asarray(L0_attn_X)
    shared = _prep_shared(np.asarray(L0_attn_W_Q), np.asarray(L0_attn_W_K),
                          np.asarray(L0_attn_W_V), np.asarray(L0_ffn_gate_w),
                          np.asarray(L0_ffn_sgate_w), np.asarray(L0_ffn_w13),
                          np.asarray(L0_ffn_w2), np.asarray(L0_ffn_shared_gate_up_w),
                          np.asarray(L0_ffn_shared_down_w))
    in_maps = []
    for c in range(8):
        m = dict(shared)
        m.update(_prep_core(X, c))
        in_maps.append(m)
    if "nc" not in _CACHE:
        _CACHE["nc"] = build_nc(STAGE)
    nc = _CACHE["nc"]
    res = run_bass_kernel_spmd(nc, in_maps, core_ids=list(range(8)))
    out = np.empty((2, 4096, 2048), np.float32)
    for c in range(8):
        b, j = c // 4, c % 4
        out[b, j * 1024:(j + 1) * 1024, :] = res.results[c]["y"].T
    return out


if __name__ == "__main__":
    import sys
    if len(sys.argv) > 1 and sys.argv[1] == "buildtest":
        import time
        t0 = time.time()
        nc = build_nc(int(sys.argv[2]) if len(sys.argv) > 2 else STAGE)
        print(f"build ok in {time.time() - t0:.1f}s")


# version 05
# speedup vs baseline: 1.0759x; 1.0127x over previous
import numpy as np
import ml_dtypes
from contextlib import ExitStack

import concourse.bass as bass
import concourse.mybir as mybir
from concourse.bass_utils import run_bass_kernel_spmd

F32 = mybir.dt.float32
BF16 = mybir.dt.bfloat16
AF = mybir.ActivationFunctionType
ALU = mybir.AluOpType
AX = mybir.AxisListType.X
NPBF = ml_dtypes.bfloat16

STAGE = 3
GAMMA = 0.96875
EPS = 1e-5
NEXP = 20


def _lhsT_chunks(W, order):
    K, F = W.shape
    Kt = K // 128
    arr = W.reshape(Kt, 128, F).transpose(1, 0, 2)
    out = np.empty((len(order), 128, Kt * 128), dtype=np.float32)
    for i, f in enumerate(order):
        out[i] = arr[:, :, f * 128:(f + 1) * 128].reshape(128, Kt * 128)
    return out


def _prep_shared(WQ, WK, WV, gate_w, sgate_w, w13, w2, sgu, sd):
    d = {}
    perm = np.concatenate([np.arange(0, 2048, 2), np.arange(1, 2048, 2)])
    pair_order = [x for i in range(8) for x in (i, i + 8)]
    d["wk"] = _lhsT_chunks(np.ascontiguousarray(WK[:, perm]), pair_order)
    d["wq"] = _lhsT_chunks(np.ascontiguousarray(WQ[:, perm]), pair_order)
    d["wv"] = _lhsT_chunks(np.ascontiguousarray(WV), list(range(16)))
    w13s = np.empty((NEXP, 22, 128, 2048), np.float32)
    w2s = np.empty((NEXP, 16, 128, 1408), np.float32)
    inter = [x for c in range(11) for x in (c, 11 + c)]
    SGU = np.ascontiguousarray(sgu.T)
    SD = np.ascontiguousarray(sd.T)
    for ic in range(4):
        order = [x for c in range(11) for x in (ic * 11 + c, 44 + ic * 11 + c)]
        w13s[ic] = _lhsT_chunks(SGU, order)
        w2s[ic] = _lhsT_chunks(
            np.ascontiguousarray(SD[ic * 1408:(ic + 1) * 1408]), list(range(16)))
    for e in range(16):
        w13s[4 + e] = _lhsT_chunks(np.ascontiguousarray(w13[e].T), inter)
        w2s[4 + e] = _lhsT_chunks(np.ascontiguousarray(w2[e].T), list(range(16)))
    d["w13s"] = w13s
    d["w2s"] = w2s
    d["grhs"] = np.ascontiguousarray(
        gate_w.T.reshape(16, 128, 16).transpose(1, 0, 2).reshape(128, 256))
    d["sgw"] = np.ascontiguousarray(sgate_w.reshape(16, 128).T)
    d["eye"] = np.eye(128, dtype=np.float32)
    d["onesc"] = np.ones((1, 128), np.float32)
    d["onesk"] = np.ones((128, 1), np.float32)
    return d


def _prep_core(X, c):
    b, j = c // 4, c % 4
    pos = np.concatenate([np.arange(j * 512, (j + 1) * 512),
                          np.arange((7 - j) * 512, (8 - j) * 512)])
    d = {"x0": np.ascontiguousarray(X[b, pos, :].T).astype(np.float32)}
    i = np.arange(1024, dtype=np.float64)
    sv = (2.0 * i + 0.4 * 2048.0) / (1.4 * 2048.0)
    invf = 10000.0 ** (-i / 1024.0)
    p64 = pos.astype(np.float64)
    scale = sv[:, None] ** (p64[None, :] / 512.0)
    ang = invf[:, None] * p64[None, :]
    d["csq"] = (np.cos(ang) * scale).astype(np.float32)
    d["ssq"] = (np.sin(ang) * scale).astype(np.float32)
    d["csk"] = (np.cos(ang) / scale).astype(np.float32)
    d["ssk"] = (np.sin(ang) / scale).astype(np.float32)
    dec = np.zeros((2, 32, 128, 512), np.float32)
    n0s = (j * 512, (7 - j) * 512)
    for nc2 in range(2):
        n_idx = n0s[nc2] + np.arange(512).astype(np.float64)
        for mb in range(32):
            m_idx = mb * 128 + np.arange(128).astype(np.float64)
            dm = n_idx[None, :] - m_idx[:, None]
            dec[nc2, mb] = np.where(dm >= 0, GAMMA ** np.maximum(dm, 0.0), 0.0
                                    ).astype(np.float32)
    d["dec"] = dec
    return d


SEM_NAMES = ["wld0", "wld1", "wld2", "tld0", "tld1", "kld0", "kld1", "dld0", "dld1",
             "vld0", "vld1", "vld2", "vld3", "xld", "pe", "dve", "act", "cc", "st",
             "kst0", "kst1", "vst0", "vst1", "tld2", "tld3"]


class _Bank:
    __slots__ = ("writer", "readers")

    def __init__(self):
        self.writer = 0
        self.readers = {}


class Prog:

    def __init__(self, cur, eng, sems):
        self.cur = cur
        self.eng = eng
        self.sems = sems
        self.c = {n: 0 for n in SEM_NAMES}
        self.banks = {}

    def op(self, key, fn, waits=(), inc=None, n=1):
        if self.cur == key:
            for s, v in waits:
                assert v <= self.c[s], f"wait {s}>={v} but only {self.c[s]} emitted"
                if v > 0:
                    self.eng.wait_ge(self.sems[s], v)
            inst = fn(self.eng)
            if inc is not None:
                inst.then_inc(self.sems[inc], n)
        else:
            for s, v in waits:
                assert v <= self.c[s], f"wait {s}>={v} but only {self.c[s]} emitted"
        if inc is not None:
            self.c[inc] += n
            return self.c[inc]
        return None

    def wait(self, key, waits):
        if self.cur == key:
            for s, v in waits:
                assert v <= self.c[s]
                if v > 0:
                    self.eng.wait_ge(self.sems[s], v)
        else:
            for s, v in waits:
                assert v <= self.c[s]

    def bank(self, name):
        if name not in self.banks:
            self.banks[name] = _Bank()
        return self.banks[name]

    def bank_wwaits(self, name):
        return sorted(self.bank(name).readers.items())

    def bank_written(self, name):
        b = self.bank(name)
        b.writer = self.c["pe"]
        b.readers = {}

    def bank_rwaits(self, name):
        return [("pe", self.bank(name).writer)]

    def bank_read(self, name, sem):
        self.bank(name).readers[sem] = self.c[sem]


class WStream:
    NS = 3

    def __init__(self, p, slots, srcs):
        self.p = p
        self.slots = slots
        self.srcs = srcs
        self.consumed = []
        self.emitted = 0

    def emit_load(self, i):
        assert i == self.emitted
        ap, width = self.srcs[i]
        par = i % self.NS
        waits = []
        if i >= self.NS:
            waits = [("pe", self.consumed[i - self.NS])]
        slot = self.slots[par]
        self.p.op("g", lambda g, ap=ap, slot=slot, w=width:
                  g.dma_start(out=slot(w), in_=ap),
                  waits=waits, inc="wld%d" % par, n=16)
        self.emitted += 1

    def consume_waits(self, i):
        return [("wld%d" % (i % self.NS), 16 * (i // self.NS + 1))]

    def mark_consumed(self, i):
        assert len(self.consumed) == i
        self.consumed.append(self.p.c["pe"])
        if i + self.NS < len(self.srcs):
            self.emit_load(i + self.NS)



def build_nc(stage=STAGE, dumps=()):
    nc = bass.Bass("TRN2", target_bir_lowering=False)
    ddbg = {nm: nc.dram_tensor("dbg_" + nm, [128, 1024], F32, kind="ExternalOutput")
            for nm in dumps}

    dx0 = nc.dram_tensor("x0", [2048, 1024], F32, kind="ExternalInput")
    dw13 = nc.dram_tensor("w13s", [NEXP, 22, 128, 2048], F32, kind="ExternalInput")
    dw2 = nc.dram_tensor("w2s", [NEXP, 16, 128, 1408], F32, kind="ExternalInput")
    dwq = nc.dram_tensor("wq", [16, 128, 2048], F32, kind="ExternalInput")
    dwk = nc.dram_tensor("wk", [16, 128, 2048], F32, kind="ExternalInput")
    dwv = nc.dram_tensor("wv", [16, 128, 2048], F32, kind="ExternalInput")
    dgrhs = nc.dram_tensor("grhs", [128, 256], F32, kind="ExternalInput")
    dsgw = nc.dram_tensor("sgw", [128, 16], F32, kind="ExternalInput")
    deye = nc.dram_tensor("eye", [128, 128], F32, kind="ExternalInput")
    donesc = nc.dram_tensor("onesc", [1, 128], F32, kind="ExternalInput")
    donesk = nc.dram_tensor("onesk", [128, 1], F32, kind="ExternalInput")
    dcsq = nc.dram_tensor("csq", [1024, 1024], F32, kind="ExternalInput")
    dssq = nc.dram_tensor("ssq", [1024, 1024], F32, kind="ExternalInput")
    dcsk = nc.dram_tensor("csk", [1024, 1024], F32, kind="ExternalInput")
    dssk = nc.dram_tensor("ssk", [1024, 1024], F32, kind="ExternalInput")
    ddec = nc.dram_tensor("dec", [2, 32, 128, 512], F32, kind="ExternalInput")
    dy = nc.dram_tensor("y", [2048, 1024], F32, kind="ExternalOutput")
    kbi = nc.dram_tensor("kbi", [8, 128, 16, 128], BF16)
    kbg = nc.dram_tensor("kbg", [4 * 8 * 128 * 16, 128], BF16)
    vbi = nc.dram_tensor("vbi", [1024, 2048], BF16)
    vbg = nc.dram_tensor("vbg", [4096, 2048], BF16)

    ctx = ExitStack()
    with ctx:
        sb = ctx.enter_context
        x = [sb(nc.sbuf_tensor(f"sx{i}", [128, 1024], F32)) for i in range(16)]
        xn = [sb(nc.sbuf_tensor(f"xn{i}", [128, 1024], BF16)) for i in range(16)]
        hh = [sb(nc.sbuf_tensor(f"hh{i}", [128, 1024], BF16)) for i in range(16)]
        wb = [sb(nc.sbuf_tensor(f"wb{i}", [128, 2048], BF16)) for i in range(3)]
        kb = [sb(nc.sbuf_tensor(f"kb{i}", [128, 2048], BF16)) for i in range(2)]
        vb = [sb(nc.sbuf_tensor(f"vb{i}", [128, 2048], BF16)) for i in range(4)]
        tb1 = sb(nc.sbuf_tensor("tb1", [128, 1024], F32))
        tb0 = sb(nc.sbuf_tensor("tb0", [128, 1024], F32))
        dbgt = sb(nc.sbuf_tensor("dbgt", [128, 1024], F32)) if dumps else None
        ts1 = sb(nc.sbuf_tensor("ts1", [128, 1024], F32))
        aB = sb(nc.sbuf_tensor("aB", [128, 1024], F32))
        bB = sb(nc.sbuf_tensor("bB", [128, 1024], F32))
        cB = sb(nc.sbuf_tensor("cB", [128, 1024], F32))
        sgB = sb(nc.sbuf_tensor("sgB", [128, 1024], F32))
        mrow = sb(nc.sbuf_tensor("mrow", [1, 1024], F32))
        vrow = sb(nc.sbuf_tensor("vrow", [1, 1024], F32))
        rwall = sb(nc.sbuf_tensor("rwall", [128, 128], F32))
        rsc = sb(nc.sbuf_tensor("rsc", [128, 8], F32))
        mxs = sb(nc.sbuf_tensor("mxs", [128, 1], F32))
        msks = sb(nc.sbuf_tensor("msks", [128, 16], F32))
        rw2s = sb(nc.sbuf_tensor("rw2s", [128, 16], F32))
        cofs = sb(nc.sbuf_tensor("cofs", [128, 128], F32))
        coefT = sb(nc.sbuf_tensor("coefT", [16, 1024], F32))
        eyef = sb(nc.sbuf_tensor("eyef", [128, 128], F32))
        eyeb = sb(nc.sbuf_tensor("eyeb", [128, 128], BF16))
        onescf = sb(nc.sbuf_tensor("onescf", [1, 128], F32))
        oneskb = sb(nc.sbuf_tensor("oneskb", [128, 1], BF16))
        grhsb = sb(nc.sbuf_tensor("grhsb", [128, 256], BF16))
        sgwb = sb(nc.sbuf_tensor("sgwb", [128, 16], BF16))
        P = [sb(nc.psum_tensor(f"P{i}", [128, 512], F32)) for i in range(6)]
        p6b = sb(nc.psum_tensor("p6b", [128, 1024], BF16))
        p7b = sb(nc.psum_tensor("p7b", [128, 1024], BF16))
        sems = {n: sb(nc.semaphore(n)) for n in SEM_NAMES}
        block = sb(nc.Block())

        srcs = []
        for call in range(2 if stage >= 3 else 1):
            for e in range(NEXP):
                for c in range(22):
                    srcs.append((dw13[e, c, :, :], 2048))
                for f in range(16):
                    srcs.append((dw2[e, f, :, :], 1408))
            if call == 0 and stage >= 2:
                for i in range(16):
                    srcs.append((dwk[i, :, :], 2048))
                for i in range(16):
                    srcs.append((dwv[i, :, :], 2048))
                for i in range(16):
                    srcs.append((dwq[i, :, :], 2048))

        def emit_all(p: Prog):
            ws = WStream(p, [lambda w, s=wb[0]: s[:, 0:w],
                             lambda w, s=wb[1]: s[:, 0:w],
                             lambda w, s=wb[2]: s[:, 0:w]], srcs)
            st_total = [0]

            p.op("g", lambda g: g.dma_start(out=grhsb[:, :], in_=dgrhs[:, :]), inc="xld", n=16)
            p.op("g", lambda g: g.dma_start(out=sgwb[:, :], in_=dsgw[:, :]), inc="xld", n=16)
            p.op("g", lambda g: g.dma_start(out=eyeb[:, :], in_=deye[:, :]), inc="xld", n=16)
            p.op("g", lambda g: g.dma_start(out=oneskb[:, :], in_=donesk[:, :]), inc="xld", n=16)
            ws.emit_load(0)
            ws.emit_load(1)
            ws.emit_load(2)
            p.op("y", lambda y: y.dma_start(out=eyef[:, :], in_=deye[:, :]), inc="xld", n=16)
            p.op("y", lambda y: y.dma_start(out=onescf[:, :], in_=donesc[:, :]), inc="xld", n=16)
            consts_ld = p.c["xld"]
            for t in range(16):
                qk = "y" if t % 2 == 0 else "s"
                p.op(qk, lambda e, t=t: e.dma_start(out=x[t][:, :], in_=dx0[t * 128:(t + 1) * 128, :]),
                     inc="xld", n=16)
            pro_ld = p.c["xld"]
            consts_ld = pro_ld

            state = {"xn_ready": 0, "w2_end": 0, "xn_free_pe": 0, "x_ready_dve": 0,
                     "x_ready_xld": pro_ld, "x_tile_dve": [0] * 16}

            def dump(nm, ap, rows=128, cols=1024):
                if nm not in ddbg:
                    return
                w = [("pe", p.c["pe"]), ("act", p.c["act"]), ("st", p.c["st"])]
                p.op("v", lambda v: v.tensor_copy(dbgt[0:rows, 0:cols], ap), waits=w,
                     inc="dve")
                p.op("y", lambda y, nm=nm: y.dma_start(
                    out=ddbg[nm][0:rows, 0:cols], in_=dbgt[0:rows, 0:cols]),
                    waits=[("dve", p.c["dve"])], inc="st", n=16)

            def emit_ln():
                for t in range(16):
                    w = [("dve", max(state["x_ready_dve"], state["x_tile_dve"][t]))]
                    if t == 0:
                        w += [("xld", state["x_ready_xld"]), ("pe", state["xn_free_pe"]),
                              ("xld", consts_ld)]
                    p.op("s", lambda s, t=t: s.activation(xn[t][:, :], x[t][:, :], AF.Copy),
                         waits=w, inc="act")
                    a_copy = p.c["act"]
                    for sp in range(2):
                        bank = f"P{sp}"
                        bw = p.bank_wwaits(bank) if t == 0 else []
                        p.op("t", lambda te, t=t, sp=sp: te.matmul(
                            P[sp][0:1, :], oneskb[:, 0:1], xn[t][:, sp * 512:(sp + 1) * 512],
                            start=(t == 0), stop=(t == 15), skip_group_check=True),
                            waits=bw + [("act", a_copy)], inc="pe")
                        if t == 15:
                            p.bank_written(bank)
                sum_done = p.c["pe"]
                for t in range(16):
                    p.op("s", lambda s, t=t: s.activation(xn[t][:, :], x[t][:, :], AF.Square),
                         waits=[("pe", sum_done)] if t == 0 else [], inc="act")
                    a_sq = p.c["act"]
                    for sp in range(2):
                        bank = f"P{2 + sp}"
                        bw = p.bank_wwaits(bank) if t == 0 else []
                        p.op("t", lambda te, t=t, sp=sp: te.matmul(
                            P[2 + sp][0:1, :], oneskb[:, 0:1], xn[t][:, sp * 512:(sp + 1) * 512],
                            start=(t == 0), stop=(t == 15), skip_group_check=True),
                            waits=bw + [("act", a_sq)], inc="pe")
                        if t == 15:
                            p.bank_written(bank)
                sq_done = p.c["pe"]
                for sp in range(2):
                    sl = slice(sp * 512, (sp + 1) * 512)
                    p.op("v", lambda v, sp=sp, sl=sl: v.tensor_scalar_mul(
                        mrow[0:1, sl], P[sp][0:1, :], 1.0 / 2048.0),
                        waits=[("pe", sq_done)], inc="dve")
                    p.bank_read(f"P{sp}", "dve")
                    p.op("v", lambda v, sl=sl: v.tensor_mul(
                        vrow[0:1, sl], mrow[0:1, sl], mrow[0:1, sl]), inc="dve")
                    p.op("v", lambda v, sp=sp, sl=sl: v.scalar_tensor_tensor(
                        vrow[0:1, sl], P[2 + sp][0:1, :], 1.0 / 2048.0, vrow[0:1, sl],
                        ALU.mult, ALU.subtract), inc="dve")
                    p.bank_read(f"P{2 + sp}", "dve")
                for sp in range(2):
                    sl = slice(sp * 512, (sp + 1) * 512)
                    p.op("v", lambda v, sl=sl: v.tensor_scalar_add(
                        vrow[0:1, sl], vrow[0:1, sl], EPS), inc="dve")
                row_v = p.c["dve"]
                for sp in range(2):
                    sl = slice(sp * 512, (sp + 1) * 512)
                    p.op("s", lambda s, sl=sl: s.activation(
                        vrow[0:1, sl], vrow[0:1, sl], AF.Ln),
                        waits=[("dve", row_v)] if sp == 0 else [], inc="act")
                    p.op("s", lambda s, sl=sl: s.activation(
                        vrow[0:1, sl], vrow[0:1, sl], AF.Exp, scale=-0.5), inc="act")
                row_r = p.c["act"]
                for sp in range(2):
                    sl = slice(sp * 512, (sp + 1) * 512)
                    p.op("v", lambda v, sl=sl: v.scalar_tensor_tensor(
                        mrow[0:1, sl], mrow[0:1, sl], -1.0, vrow[0:1, sl],
                        ALU.mult, ALU.mult),
                        waits=[("act", row_r)] if sp == 0 else [], inc="dve")
                row_b = p.c["dve"]
                for sp in range(2):
                    sl = slice(sp * 512, (sp + 1) * 512)
                    for (dst, src, rw) in ((aB, vrow, row_r), (bB, mrow, row_b)):
                        bank = "P4" if dst is aB else "P5"
                        pi = P[4] if dst is aB else P[5]
                        p.op("t", lambda te, pi=pi, src=src, sl=sl: te.matmul(
                            pi[:, :], onescf[0:1, :], src[0:1, sl],
                            start=True, stop=True, skip_group_check=True),
                            waits=p.bank_wwaits(bank) + [("dve", row_b), ("act", row_r)],
                            inc="pe")
                        p.bank_written(bank)
                        p.op("v", lambda v, dst=dst, pi=pi, sl=sl: v.tensor_copy(
                            dst[:, sl], pi[:, :]),
                            waits=p.bank_rwaits(bank), inc="dve")
                        p.bank_read(bank, "dve")
                ab_done = p.c["dve"]
                xn_tile = [0] * 16
                for t in range(16):
                    w = [("dve", ab_done), ("pe", sq_done)] if t == 0 else []
                    p.op("v", lambda v, t=t: v.tensor_mul(ts1[:, :], x[t][:, :], aB[:, :]),
                         waits=w, inc="dve")
                    p.op("v", lambda v, t=t: v.tensor_add(xn[t][:, :], ts1[:, :], bB[:, :]),
                         inc="dve")
                    xn_tile[t] = p.c["dve"]
                state["xn_ready"] = p.c["dve"]
                state["xn_tile"] = xn_tile
                dump("xn0", xn[0][:, :])
                dump("ab", aB[:, :])
                dump("bb", bB[:, :])

            def emit_sgate_router():
                for sp in range(2):
                    bank = f"P{4 + sp}"
                    pi = P[4 + sp]
                    for kt in range(16):
                        w = (p.bank_wwaits(bank) if kt == 0 else []) + \
                            ([("dve", state["xn_tile"][kt])] if sp == 0 else [])
                        p.op("t", lambda te, pi=pi, kt=kt, sp=sp: te.matmul(
                            pi[0:1, :], sgwb[:, kt:kt + 1],
                            xn[kt][:, sp * 512:(sp + 1) * 512],
                            start=(kt == 0), stop=(kt == 15), skip_group_check=True),
                            waits=w, inc="pe")
                    p.bank_written(bank)
                    p.op("s", lambda s, pi=pi, sp=sp: s.activation(
                        mrow[0:1, sp * 512:(sp + 1) * 512], pi[0:1, :], AF.Sigmoid),
                        waits=p.bank_rwaits(bank), inc="act")
                    p.bank_read(bank, "act")
                sg_row = p.c["act"]
                for sp in range(2):
                    bank = f"P{4 + sp}"
                    pi = P[4 + sp]
                    sl = slice(sp * 512, (sp + 1) * 512)
                    p.op("t", lambda te, pi=pi, sl=sl: te.matmul(
                        pi[:, :], onescf[0:1, :], mrow[0:1, sl],
                        start=True, stop=True, skip_group_check=True),
                        waits=p.bank_wwaits(bank) + [("act", sg_row)], inc="pe")
                    p.bank_written(bank)
                    p.op("v", lambda v, pi=pi, sl=sl: v.tensor_scalar_mul(
                        sgB[:, sl], pi[:, :], 0.5),
                        waits=p.bank_rwaits(bank), inc="dve")
                    p.bank_read(bank, "dve")
                aexp_hist = [0, 0]
                lnx_hist = [0, 0]
                rinv_hist = [0, 0]

                def ved(fn, waits=()):
                    val = p.op("v", fn, waits=waits, inc="dve")
                    p.wait("v", [("dve", val)])
                    return val

                lg_pe = []
                for t in range(8):
                    bank = "P4" if t < 4 else "P5"
                    pi = P[4] if t < 4 else P[5]
                    cols = slice((t % 4) * 16, (t % 4) * 16 + 16)
                    for kt in range(16):
                        w = (p.bank_wwaits(bank) if kt == 0 and t % 4 == 0 else []) + \
                            ([("dve", state["xn_tile"][kt])] if t == 0 else [])
                        p.op("t", lambda te, pi=pi, kt=kt, t=t, cols=cols: te.matmul(
                            pi[:, cols], xn[kt][:, t * 128:(t + 1) * 128],
                            grhsb[:, kt * 16:(kt + 1) * 16],
                            start=(kt == 0), stop=(kt == 15), skip_group_check=True),
                            waits=w, inc="pe")
                    p.bank_written(bank)
                    lg_pe.append(p.c["pe"])
                coef_dve = []
                for t in range(8):
                    par = t % 2
                    bank = "P4" if t < 4 else "P5"
                    pi = P[4] if t < 4 else P[5]
                    cols = slice((t % 4) * 16, (t % 4) * 16 + 16)
                    nmc = rsc[:, par:par + 1]
                    sumc = rsc[:, 2 + par:3 + par]
                    rinvc = rsc[:, 4 + par:5 + par]
                    coefc = cofs[:, t * 16:(t + 1) * 16]
                    bank_done = lg_pe[3] if t < 4 else lg_pe[7]
                    nm = p.op("v", lambda v, pi=pi, nmc=nmc, cols=cols: v.reduce_max(
                        nmc, pi[:, cols], axis=AX, negate=True),
                        waits=[("pe", bank_done), ("act", aexp_hist[par])], inc="dve")
                    rsl = slice(t * 16, (t + 1) * 16)
                    p.op("s", lambda s, pi=pi, rsl=rsl, nmc=nmc, cols=cols: s.activation(
                        rwall[:, rsl], pi[:, cols], AF.Exp, bias=nmc[:, 0:1]),
                        waits=[("pe", bank_done), ("dve", nm)], inc="act")
                    p.bank_read(bank, "act")
                    p.bank_read(bank, "dve")
                    aexp = p.c["act"]
                    aexp_hist[par] = aexp
                    sm = p.op("v", lambda v, rsl=rsl, sumc=sumc: v.reduce_sum(
                        sumc, rwall[:, rsl], axis=AX),
                        waits=[("act", max(aexp, lnx_hist[par]))], inc="dve")
                    p.op("s", lambda s, sumc=sumc, rinvc=rinvc: s.activation(
                        rinvc, sumc, AF.Ln),
                        waits=[("dve", max(sm, rinv_hist[par]))], inc="act")
                    p.op("s", lambda s, rinvc=rinvc: s.activation(
                        rinvc, rinvc, AF.Exp, scale=-1.0), inc="act")
                    arinv = p.c["act"]
                    rinv_hist[par] = ved(lambda v, rsl=rsl, rinvc=rinvc: v.tensor_scalar_mul(
                        rwall[:, rsl], rwall[:, rsl], rinvc), waits=[("act", arinv)])
                    for it in range(4):
                        srcv = rwall[:, rsl] if it == 0 else rw2s[:, :]
                        ved(lambda v, srcv=srcv: v.reduce_max(mxs[:, 0:1], srcv, axis=AX))
                        if it < 3:
                            ved(lambda v, srcv=srcv: v.tensor_scalar(
                                msks[:, :], srcv, mxs[:, 0:1], None, ALU.is_lt))
                            ved(lambda v, srcv=srcv: v.tensor_mul(
                                rw2s[:, :], srcv, msks[:, :]))
                    ved(lambda v, rsl=rsl: v.tensor_scalar(
                        msks[:, :], rwall[:, rsl], mxs[:, 0:1], None, ALU.is_ge))
                    cd = ved(lambda v, rsl=rsl, coefc=coefc: v.scalar_tensor_tensor(
                        coefc, rwall[:, rsl], 0.5, msks[:, :],
                        ALU.mult, ALU.mult))
                    coef_dve.append(cd)
                state["rt_pending"] = coef_dve
                state["coefT_ready"] = p.c["dve"]
                dump("sgb", sgB[:, :])
                dump("rw", rwall[:, :], cols=128)
                dump("coeft", coefT[:, :], rows=16)

            def emit_router_transposes():
                coef_dve = state.pop("rt_pending")
                for t in range(8):
                    coefc = cofs[:, t * 16:(t + 1) * 16]
                    p.op("t", lambda te, coefc=coefc: te.transpose(
                        P[4][0:16, 0:128], coefc, eyef[:, :]),
                        waits=p.bank_wwaits("P4") + [("dve", coef_dve[t]),
                                                     ("xld", consts_ld)],
                        inc="pe")
                    p.bank_written("P4")
                    p.op("v", lambda v, t=t: v.tensor_copy(
                        coefT[:, t * 128:(t + 1) * 128], P[4][0:16, 0:128]),
                        waits=p.bank_rwaits("P4"), inc="dve")
                    p.bank_read("P4", "dve")
                state["coefT_ready"] = p.c["dve"]

            def emit_expert(e):
                routed = e >= 4
                er = e - 4
                cb = cB if routed else sgB
                if routed:
                    for sp in range(2):
                        bank = f"P{2 + sp}"
                        pi = P[2 + sp]
                        sl = slice(sp * 512, (sp + 1) * 512)
                        p.op("t", lambda te, pi=pi, sl=sl, er=er: te.matmul(
                            pi[0:1, :], eyef[0:16, er:er + 1], coefT[0:16, sl],
                            start=True, stop=True, skip_group_check=True),
                            waits=p.bank_wwaits(bank) + [("dve", state["coefT_ready"])],
                            inc="pe")
                        p.bank_written(bank)
                        p.op("v", lambda v, pi=pi, sl=sl: v.tensor_copy(mrow[0:1, sl], pi[0:1, :]),
                             waits=p.bank_rwaits(bank), inc="dve")
                        p.bank_read(bank, "dve")
                    row_done = p.c["dve"]
                    for sp in range(2):
                        bank = f"P{2 + sp}"
                        pi = P[2 + sp]
                        sl = slice(sp * 512, (sp + 1) * 512)
                        p.op("t", lambda te, pi=pi, sl=sl: te.matmul(
                            pi[:, :], onescf[0:1, :], mrow[0:1, sl],
                            start=True, stop=True, skip_group_check=True),
                            waits=p.bank_wwaits(bank) + [("dve", row_done)],
                            inc="pe")
                        p.bank_written(bank)
                        p.op("v", lambda v, pi=pi, sl=sl: v.tensor_copy(cB[:, sl], pi[:, :]),
                             waits=p.bank_rwaits(bank), inc="dve")
                        p.bank_read(bank, "dve")
                cb_ready = p.c["dve"]
                hprev = [0, 0]
                for c in range(11):
                    if e == 0 and c == 5 and state.get("rt_pending"):
                        emit_router_transposes()
                    sil = [0, 0]
                    for half in range(2):
                        i = len(ws.consumed)
                        bk = (0, 1) if half == 0 else (2, 3)
                        for sp in range(2):
                            bank = f"P{bk[sp]}"
                            w = []
                            if sp == 0:
                                w += ws.consume_waits(i)
                                w += [("dve", cb_ready)]
                            w += p.bank_wwaits(bank)
                            par = i % WStream.NS
                            fine = (c == 0 and half == 0 and sp == 0)
                            for kt in range(16):
                                wk_ = (w if kt == 0 else [])
                                if fine:
                                    wk_ = wk_ + [("dve", state["xn_tile"][kt])]
                                p.op("t", lambda te, bk=bk, sp=sp, par=par, kt=kt: te.matmul(
                                    P[bk[sp]][:, :], wb[par][:, kt * 128:(kt + 1) * 128],
                                    xn[kt][:, sp * 512:(sp + 1) * 512],
                                    start=(kt == 0), stop=(kt == 15), skip_group_check=True),
                                    waits=wk_, inc="pe" if kt == 15 else None)
                            p.bank_written(bank)
                        ws.mark_consumed(i)
                        if half == 0:
                            for sp in range(2):
                                bank = f"P{sp}"
                                p.op("s", lambda s, sp=sp, c=c: s.activation(
                                    ts1[:, sp * 512:(sp + 1) * 512], P[sp][:, :], AF.Silu),
                                    waits=p.bank_rwaits(bank) +
                                    [("dve", hprev[sp])], inc="act")
                                p.bank_read(bank, "act")
                                sil[sp] = p.c["act"]
                    for sp in range(2):
                        bank = f"P{2 + sp}"
                        sl = slice(sp * 512, (sp + 1) * 512)
                        p.op("v", lambda v, sp=sp, sl=sl, cb=cb: v.tensor_mul(
                            tb0[:, sl], P[2 + sp][:, :], cb[:, sl]),
                            waits=p.bank_rwaits(bank), inc="dve")
                        p.bank_read(bank, "dve")
                        p.op("v", lambda v, c=c, sl=sl: v.tensor_mul(
                            hh[c][:, sl], tb0[:, sl], ts1[:, sl]),
                            waits=[("act", sil[sp]), ("pe", state["w2_end"])], inc="dve")
                        hprev[sp] = p.c["dve"]
                h_done = p.c["dve"]
                if e == 4:
                    dump("cb0", cB[:, :])
                    dump("h00", hh[0][:, :])
                for f in range(16):
                    i = len(ws.consumed)
                    bks = (4, 5) if f % 2 == 0 else (2, 3)
                    par = i % WStream.NS
                    for sp in range(2):
                        bank = f"P{bks[sp]}"
                        pi = P[bks[sp]]
                        w = []
                        if sp == 0:
                            w += ws.consume_waits(i)
                            if f == 0:
                                w += [("dve", h_done)]
                        w += p.bank_wwaits(bank)
                        for kt in range(11):
                            p.op("t", lambda te, pi=pi, par=par, kt=kt, sp=sp: te.matmul(
                                pi[:, :], wb[par][:, kt * 128:(kt + 1) * 128],
                                hh[kt][:, sp * 512:(sp + 1) * 512],
                                start=(kt == 0), stop=(kt == 10), skip_group_check=True),
                                waits=w if kt == 0 else [], inc="pe" if kt == 10 else None)
                        p.bank_written(bank)
                        p.op("v", lambda v, pi=pi, f=f, sp=sp: v.scalar_tensor_tensor(
                            x[f][:, sp * 512:(sp + 1) * 512], pi[:, :], 1.0,
                            x[f][:, sp * 512:(sp + 1) * 512], ALU.mult, ALU.add),
                            waits=p.bank_rwaits(bank), inc="dve")
                        p.bank_read(bank, "dve")
                        if e == NEXP - 1 and sp == 1:
                            state["x_tile_dve"][f] = p.c["dve"]
                    ws.mark_consumed(i)
                state["w2_end"] = p.c["pe"]

            def emit_ffn():
                emit_ln()
                emit_sgate_router()
                for e in range(NEXP):
                    emit_expert(e)
                state["x_ready_dve"] = p.c["dve"]
                state["xn_free_pe"] = p.c["pe"]

            def emit_qkv():
                emit_ln()
                tab_consumed = {k: p.c["dve"] for k in ("c0", "c1", "s0", "s1")}

                def emit_pair(i, dcs, dss, out_lo, out_hi, store):
                    cs_slot = tb0 if i % 2 == 0 else tb1
                    ss_slot = aB if i % 2 == 0 else bB
                    cs_sem = "tld%d" % (i % 2)
                    ss_sem = "tld%d" % (2 + i % 2)
                    p.op("y", lambda y, i=i, dcs=dcs, cs_slot=cs_slot: y.dma_start(
                        out=cs_slot[:, :], in_=dcs[i * 128:(i + 1) * 128, :]),
                        waits=[("dve", tab_consumed["c%d" % (i % 2)])], inc=cs_sem, n=16)
                    p.op("y", lambda y, i=i, dss=dss, ss_slot=ss_slot: y.dma_start(
                        out=ss_slot[:, :], in_=dss[i * 128:(i + 1) * 128, :]),
                        waits=[("dve", tab_consumed["s%d" % (i % 2)])], inc=ss_sem, n=16)
                    tlw = [(cs_sem, p.c[cs_sem]), (ss_sem, p.c[ss_sem])]
                    i_lo = len(ws.consumed)
                    i_up = i_lo + 1
                    par_lo = i_lo % WStream.NS
                    par_up = i_up % WStream.NS
                    for sp in range(2):
                        blo, bup = f"P{2 * sp}", f"P{2 * sp + 1}"
                        for half in range(2):
                            bank = blo if half == 0 else bup
                            pi = P[2 * sp + half]
                            par = par_lo if half == 0 else par_up
                            i_ch = i_lo if half == 0 else i_up
                            w = p.bank_wwaits(bank)
                            if sp == 0:
                                w += ws.consume_waits(i_ch)
                            fine = (i == 0 and sp == 0 and half == 0)
                            for kt in range(16):
                                wk_ = (w if kt == 0 else [])
                                if fine:
                                    wk_ = wk_ + [("dve", state["xn_tile"][kt])]
                                p.op("t", lambda te, pi=pi, sp=sp, par=par, kt=kt: te.matmul(
                                    pi[:, :], wb[par][:, kt * 128:(kt + 1) * 128],
                                    xn[kt][:, sp * 512:(sp + 1) * 512],
                                    start=(kt == 0), stop=(kt == 15), skip_group_check=True),
                                    waits=wk_, inc="pe" if kt == 15 else None)
                            p.bank_written(bank)
                        sl = slice(sp * 512, (sp + 1) * 512)
                        w0 = p.bank_rwaits(blo) + p.bank_rwaits(bup) + tlw
                        if sp == 0 and store is not None:
                            w0 += store["stage_free"]
                        p.op("v", lambda v, sp=sp, sl=sl, cs_slot=cs_slot: v.tensor_mul(
                            ts1[:, sl], P[2 * sp][:, :], cs_slot[:, sl]), waits=w0, inc="dve")
                        p.op("v", lambda v, sp=sp, sl=sl, ss_slot=ss_slot: v.tensor_mul(
                            sgB[:, sl], P[2 * sp + 1][:, :], ss_slot[:, sl]), inc="dve")
                        p.op("v", lambda v, sl=sl, out_lo=out_lo: v.tensor_sub(
                            out_lo[:, sl], ts1[:, sl], sgB[:, sl]), inc="dve")
                        p.op("v", lambda v, sp=sp, sl=sl, cs_slot=cs_slot: v.tensor_mul(
                            ts1[:, sl], P[2 * sp + 1][:, :], cs_slot[:, sl]), inc="dve")
                        p.bank_read(bup, "dve")
                        p.op("v", lambda v, sp=sp, sl=sl, ss_slot=ss_slot: v.tensor_mul(
                            sgB[:, sl], P[2 * sp][:, :], ss_slot[:, sl]), inc="dve")
                        p.bank_read(blo, "dve")
                        p.op("v", lambda v, sl=sl, out_hi=out_hi: v.tensor_add(
                            out_hi[:, sl], ts1[:, sl], sgB[:, sl]), inc="dve")
                    tab_consumed["c%d" % (i % 2)] = p.c["dve"]
                    tab_consumed["s%d" % (i % 2)] = p.c["dve"]
                    ws.mark_consumed(i_lo)
                    ws.mark_consumed(i_up)
                    return p.c["dve"]

                stage_free = [[], []]
                for i in range(8):
                    lo, hi = hh[11 + 2 * (i % 2)], hh[12 + 2 * (i % 2)]
                    dn = emit_pair(i, dcsk, dssk, lo, hi,
                                   {"stage_free": stage_free[i % 2]})
                    ks = "kst%d" % (i % 2)
                    for (tile, ht) in ((lo, i), (hi, i + 8)):
                        p.op("y", lambda y, tile=tile, ht=ht: y.dma_start(
                            out=kbi[:, :, ht, :].rearrange("l p m -> p l m"),
                            in_=tile[:, :].rearrange("p (l m) -> p l m", l=8)),
                            waits=[("dve", dn)], inc=ks, n=16)
                    stage_free[i % 2] = [(ks, p.c[ks])]
                k_store_done = [("kst0", p.c["kst0"]), ("kst1", p.c["kst1"])]
                for f in range(16):
                    if f == 2:
                        p.op("g", lambda g: g.collective_compute(
                            "AllGather", ALU.bypass,
                            replica_groups=[[0, 1, 2, 3], [4, 5, 6, 7]],
                            ins=[kbi.ap().rearrange("a p h m -> (a p h) m").opt()],
                            outs=[kbg.ap().opt()]),
                            waits=k_store_done, inc="cc")
                    vtmp = hh[2 * (f % 2)]
                    vtg = hh[2 * (f % 2) + 1]
                    vsem = "vst%d" % (f % 2)
                    i_ch = len(ws.consumed)
                    par = i_ch % WStream.NS
                    for sp in range(2):
                        bank = f"P{4 + sp}"
                        w = (ws.consume_waits(i_ch) if sp == 0 else []) + p.bank_wwaits(bank)
                        for kt in range(16):
                            p.op("t", lambda te, sp=sp, par=par, kt=kt: te.matmul(
                                P[4 + sp][:, :], wb[par][:, kt * 128:(kt + 1) * 128],
                                xn[kt][:, sp * 512:(sp + 1) * 512],
                                start=(kt == 0), stop=(kt == 15), skip_group_check=True),
                                waits=w if kt == 0 else [], inc="pe" if kt == 15 else None)
                        p.bank_written(bank)
                    ws.mark_consumed(i_ch)
                    for sp in range(2):
                        bank = f"P{4 + sp}"
                        w = p.bank_rwaits(bank)
                        if f >= 2 and sp == 0:
                            w += [(vsem, 16 * (f // 2))]
                        p.op("v", lambda v, sp=sp, vtmp=vtmp: v.tensor_copy(
                            vtmp[:, sp * 512:(sp + 1) * 512], P[4 + sp][:, :]),
                            waits=w, inc="dve")
                        p.bank_read(bank, "dve")
                    vt_done = p.c["dve"]
                    for m in range(8):
                        pb = p6b if m % 2 == 0 else p7b
                        bname = "p6b" if m % 2 == 0 else "p7b"
                        p.op("t", lambda te, pb=pb, m=m, vtmp=vtmp: te.transpose(
                            pb[:, 0:128], vtmp[:, m * 128:(m + 1) * 128], eyeb[:, :]),
                            waits=p.bank_wwaits(bname) +
                            ([("dve", vt_done), ("xld", consts_ld)] if m == 0 else []),
                            inc="pe")
                        p.bank_written(bname)
                        p.op("v", lambda v, pb=pb, m=m, vtg=vtg: v.tensor_copy(
                            vtg[:, m * 128:(m + 1) * 128], pb[:, 0:128]),
                            waits=p.bank_rwaits(bname), inc="dve")
                        p.bank_read(bname, "dve")
                    p.op("y", lambda y, f=f, vtg=vtg: y.dma_start(
                        out=vbi[:, f * 128:(f + 1) * 128].rearrange("(m p) c -> p m c", p=128),
                        in_=vtg[:, :].rearrange("p (m c) -> p m c", m=8)),
                        waits=[("dve", p.c["dve"])], inc=vsem, n=16)
                v_store_done = [("vst0", p.c["vst0"]), ("vst1", p.c["vst1"])]
                for i in range(8):
                    sfree = [("pe", state["w2_end"])]
                    if i in (3, 4, 5, 6):
                        sfree = sfree + k_store_done
                    if i < 4:
                        sfree = sfree + v_store_done
                    emit_pair(i, dcsq, dssq, hh[i], hh[i + 8], {"stage_free": sfree})
                    if i == 0:
                        p.op("g", lambda g: g.collective_compute(
                            "AllGather", ALU.bypass,
                            replica_groups=[[0, 1, 2, 3], [4, 5, 6, 7]],
                            ins=[vbi.ap().opt()],
                            outs=[vbg.ap().opt()]),
                            waits=v_store_done, inc="cc")
                state["qkv_end_pe"] = p.c["pe"]
                state["cc_done"] = p.c["cc"]

            def emit_attn():
                kq_consumed = []
                dq_consumed = []
                avg_done = []
                entry_dve = p.c["dve"]
                NQ = 48

                def slot_mb(q):
                    return (0, q) if q < 16 else (1, q - 16)

                def slab(mb):
                    pm = mb // 4
                    r = pm if pm < 4 else 7 - pm
                    return r * 8 + (0 if pm < 4 else 4) + mb % 4

                def load_kblock(q):
                    slot, mb = slot_mb(q)
                    s0 = slab(mb) * 2048
                    w = [("cc", state["cc_done"] - 1)]
                    if q >= 2:
                        w = [("pe", kq_consumed[q - 2])]
                    p.op("y", lambda y, s0=s0, q=q: y.dma_start(
                        out=kb[q % 2][:, :],
                        in_=kbg[s0:s0 + 2048, :].rearrange("(p h) m -> p (h m)", p=128)),
                        waits=w, inc="kld%d" % (q % 2), n=16)

                def load_dtile(q):
                    slot, mb = slot_mb(q)
                    w = [("dve", entry_dve)] if q < 2 else [("dve", dq_consumed[q - 2])]
                    sl = slice((q % 2) * 512, (q % 2) * 512 + 512)
                    p.op("y", lambda y, slot=slot, mb=mb, sl=sl: y.dma_start(
                        out=tb0[:, sl], in_=ddec[slot, mb, :, :]),
                        waits=w, inc="dld%d" % (q % 2), n=16)

                def load_vblock(g2, lb):
                    slot = 0 if g2 < 4 else 1
                    mb = (g2 if g2 < 4 else g2 - 4) * 4 + lb
                    s0 = slab(mb) * 128
                    w = [("cc", state["cc_done"])]
                    if g2 >= 1:
                        w = [("pe", avg_done[g2 - 1])]
                    p.op("y", lambda y, s0=s0, lb=lb: y.dma_start(
                        out=vb[lb][:, :], in_=vbg[s0:s0 + 128, :]),
                        waits=w, inc="vld%d" % lb, n=16)

                load_kblock(0)
                load_kblock(1)
                load_dtile(0)
                load_dtile(1)
                for lb in range(4):
                    load_vblock(0, lb)

                x_tile_dve = [0] * 16
                for g2 in range(12):
                    slot = 0 if g2 < 4 else 1
                    g = g2 if g2 < 4 else g2 - 4
                    for lb in range(4):
                        mb = 4 * g + lb
                        q = (0 if slot == 0 else 16) + mb
                        bank = f"P{q % 2}"
                        pi = P[q % 2]
                        w = [("kld%d" % (q % 2), 16 * (q // 2 + 1))] + p.bank_wwaits(bank)
                        if q == 0:
                            w += [("pe", state["qkv_end_pe"])]
                        for ht in range(16):
                            p.op("t", lambda te, pi=pi, q=q, ht=ht, slot=slot: te.matmul(
                                pi[:, :], kb[q % 2][:, ht * 128:(ht + 1) * 128],
                                hh[ht][:, slot * 512:(slot + 1) * 512],
                                start=(ht == 0), stop=(ht == 15), skip_group_check=True),
                                waits=w if ht == 0 else [], inc="pe" if ht == 15 else None)
                        p.bank_written(bank)
                        kq_consumed.append(p.c["pe"])
                        if q + 2 < NQ:
                            load_kblock(q + 2)
                        sl = slice((q % 2) * 512, (q % 2) * 512 + 512)
                        w = p.bank_rwaits(bank) + [("dld%d" % (q % 2), 16 * (q // 2 + 1))]
                        if g2 == 0 and lb == 0:
                            w += [("pe", state["qkv_end_pe"])]
                        if g2 == 4 and lb == 0:
                            w += [("pe", avg_done[3])]
                        p.op("v", lambda v, pi=pi, mb=mb, sl=sl: v.tensor_mul(
                            xn[mb // 2][:, (mb % 2) * 512:(mb % 2) * 512 + 512],
                            pi[:, :], tb0[:, sl]),
                            waits=w, inc="dve")
                        p.bank_read(bank, "dve")
                        dq_consumed.append(p.c["dve"])
                        if q + 2 < NQ:
                            load_dtile(q + 2)
                    attd_done = p.c["dve"]
                    for f in range(16):
                        bank = f"P{4 + f % 2}"
                        pi = P[4 + f % 2]
                        w = p.bank_wwaits(bank) + [("dve", attd_done)]
                        if f == 0:
                            w += [("vld%d" % lb2, 16 * (g2 + 1)) for lb2 in range(4)]
                        for lb in range(4):
                            mb = 4 * g + lb
                            p.op("t", lambda te, pi=pi, f=f, lb=lb, mb=mb: te.matmul(
                                pi[:, :], vb[lb][:, f * 128:(f + 1) * 128],
                                xn[mb // 2][:, (mb % 2) * 512:(mb % 2) * 512 + 512],
                                start=(lb == 0), stop=(lb == 3), skip_group_check=True),
                                waits=w if lb == 0 else [], inc="pe" if lb == 3 else None)
                        p.bank_written(bank)
                        p.op("v", lambda v, pi=pi, f=f, slot=slot: v.scalar_tensor_tensor(
                            x[f][:, slot * 512:(slot + 1) * 512], pi[:, :], 1.0,
                            x[f][:, slot * 512:(slot + 1) * 512], ALU.mult, ALU.add),
                            waits=p.bank_rwaits(bank), inc="dve")
                        p.bank_read(bank, "dve")
                        if g2 == 11:
                            x_tile_dve[f] = p.c["dve"]
                    avg_done.append(p.c["pe"])
                    if g2 + 1 < 12:
                        for lb in range(4):
                            load_vblock(g2 + 1, lb)
                state["x_ready_dve"] = p.c["dve"]
                state["x_tile_dve"] = x_tile_dve
                state["xn_free_pe"] = p.c["pe"]
                state["w2_end"] = p.c["pe"]

            emit_ffn()
            if stage >= 2:
                emit_qkv()
                emit_attn()
            if stage >= 3:
                emit_ffn()

            xfin = p.c["dve"]
            for f in range(16):
                p.op("y", lambda y, f=f: y.dma_start(
                    out=dy[f * 128:(f + 1) * 128, :], in_=x[f][:, :]),
                    waits=[("dve", xfin)] if f == 0 else [], inc="st", n=16)
            p.wait("y", [("st", p.c["st"])])

        @block.tensor
        def _(te):
            emit_all(Prog("t", te, sems))

        @block.vector
        def _(v):
            emit_all(Prog("v", v, sems))

        @block.scalar
        def _(s):
            emit_all(Prog("s", s, sems))

        @block.gpsimd
        def _(g):
            emit_all(Prog("g", g, sems))

        @block.sync
        def _(y):
            emit_all(Prog("y", y, sems))

    return nc


_CACHE = {}


def kernel(L0_attn_X, L0_attn_W_Q, L0_attn_W_K, L0_attn_W_V,
           L0_ffn_gate_w, L0_ffn_sgate_w, L0_ffn_w13, L0_ffn_w2,
           L0_ffn_shared_gate_up_w, L0_ffn_shared_down_w):
    X = np.asarray(L0_attn_X)
    shared = _prep_shared(np.asarray(L0_attn_W_Q), np.asarray(L0_attn_W_K),
                          np.asarray(L0_attn_W_V), np.asarray(L0_ffn_gate_w),
                          np.asarray(L0_ffn_sgate_w), np.asarray(L0_ffn_w13),
                          np.asarray(L0_ffn_w2), np.asarray(L0_ffn_shared_gate_up_w),
                          np.asarray(L0_ffn_shared_down_w))
    in_maps = []
    for c in range(8):
        m = dict(shared)
        m.update(_prep_core(X, c))
        in_maps.append(m)
    if "nc" not in _CACHE:
        _CACHE["nc"] = build_nc(STAGE)
    nc = _CACHE["nc"]
    res = run_bass_kernel_spmd(nc, in_maps, core_ids=list(range(8)))
    out = np.empty((2, 4096, 2048), np.float32)
    for c in range(8):
        b, j = c // 4, c % 4
        yt = res.results[c]["y"].T
        out[b, j * 512:(j + 1) * 512, :] = yt[0:512]
        out[b, (7 - j) * 512:(8 - j) * 512, :] = yt[512:1024]
    return out


if __name__ == "__main__":
    import sys
    if len(sys.argv) > 1 and sys.argv[1] == "buildtest":
        import time
        t0 = time.time()
        nc = build_nc(int(sys.argv[2]) if len(sys.argv) > 2 else STAGE)
        print(f"build ok in {time.time() - t0:.1f}s")


# version 06
# speedup vs baseline: 1.0844x; 1.0079x over previous
import numpy as np
import ml_dtypes
from contextlib import ExitStack

import concourse.bass as bass
import concourse.mybir as mybir
from concourse.bass_utils import run_bass_kernel_spmd

F32 = mybir.dt.float32
BF16 = mybir.dt.bfloat16
AF = mybir.ActivationFunctionType
ALU = mybir.AluOpType
AX = mybir.AxisListType.X
NPBF = ml_dtypes.bfloat16

STAGE = 3
GAMMA = 0.96875
EPS = 1e-5
NEXP = 20


def _lhsT_chunks(W, order):
    K, F = W.shape
    Kt = K // 128
    arr = W.reshape(Kt, 128, F).transpose(1, 0, 2)
    out = np.empty((len(order), 128, Kt * 128), dtype=np.float32)
    for i, f in enumerate(order):
        out[i] = arr[:, :, f * 128:(f + 1) * 128].reshape(128, Kt * 128)
    return out


def _prep_shared(WQ, WK, WV, gate_w, sgate_w, w13, w2, sgu, sd):
    d = {}
    perm = np.concatenate([np.arange(0, 2048, 2), np.arange(1, 2048, 2)])
    pair_order = [x for i in range(8) for x in (i, i + 8)]
    d["wk"] = _lhsT_chunks(np.ascontiguousarray(WK[:, perm]), pair_order)
    d["wq"] = _lhsT_chunks(np.ascontiguousarray(WQ[:, perm]), pair_order)
    d["wv"] = _lhsT_chunks(np.ascontiguousarray(WV), list(range(16)))
    w13s = np.empty((NEXP, 22, 128, 2048), np.float32)
    w2s = np.empty((NEXP, 16, 128, 1408), np.float32)
    inter = [x for c in range(11) for x in (c, 11 + c)]
    SGU = np.ascontiguousarray(sgu.T)
    SD = np.ascontiguousarray(sd.T)
    for ic in range(4):
        order = [x for c in range(11) for x in (ic * 11 + c, 44 + ic * 11 + c)]
        w13s[ic] = _lhsT_chunks(SGU, order)
        w2s[ic] = _lhsT_chunks(
            np.ascontiguousarray(SD[ic * 1408:(ic + 1) * 1408]), list(range(16)))
    for e in range(16):
        w13s[4 + e] = _lhsT_chunks(np.ascontiguousarray(w13[e].T), inter)
        w2s[4 + e] = _lhsT_chunks(np.ascontiguousarray(w2[e].T), list(range(16)))
    d["w13s"] = w13s
    d["w2s"] = w2s
    d["grhs"] = np.ascontiguousarray(
        gate_w.T.reshape(16, 128, 16).transpose(1, 0, 2).reshape(128, 256))
    d["sgw"] = np.ascontiguousarray(sgate_w.reshape(16, 128).T)
    d["eye"] = np.eye(128, dtype=np.float32)
    d["onesc"] = np.ones((1, 128), np.float32)
    d["onesk"] = np.ones((128, 1), np.float32)
    return d


def _prep_core(X, c):
    b, j = c // 4, c % 4
    pos = np.concatenate([np.arange(j * 512, (j + 1) * 512),
                          np.arange((7 - j) * 512, (8 - j) * 512)])
    d = {"x0": np.ascontiguousarray(X[b, pos, :].T).astype(np.float32)}
    i = np.arange(1024, dtype=np.float64)
    sv = (2.0 * i + 0.4 * 2048.0) / (1.4 * 2048.0)
    invf = 10000.0 ** (-i / 1024.0)
    p64 = pos.astype(np.float64)
    scale = sv[:, None] ** (p64[None, :] / 512.0)
    ang = invf[:, None] * p64[None, :]
    d["csq"] = (np.cos(ang) * scale).astype(np.float32)
    d["ssq"] = (np.sin(ang) * scale).astype(np.float32)
    d["csk"] = (np.cos(ang) / scale).astype(np.float32)
    d["ssk"] = (np.sin(ang) / scale).astype(np.float32)
    dec = np.zeros((2, 32, 128, 512), np.float32)
    n0s = (j * 512, (7 - j) * 512)
    for nc2 in range(2):
        n_idx = n0s[nc2] + np.arange(512).astype(np.float64)
        for mb in range(32):
            m_idx = mb * 128 + np.arange(128).astype(np.float64)
            dm = n_idx[None, :] - m_idx[:, None]
            dec[nc2, mb] = np.where(dm >= 0, GAMMA ** np.maximum(dm, 0.0), 0.0
                                    ).astype(np.float32)
    d["dec"] = dec
    return d


SEM_NAMES = ["wld0", "wld1", "wld2", "tld0", "tld1", "kld0", "kld1", "dld0", "dld1",
             "vld0", "vld1", "vld2", "vld3", "xld", "pe", "dve", "act", "cc", "st",
             "kst0", "kst1", "vst0", "vst1", "tld2", "tld3"]


class _Bank:
    __slots__ = ("writer", "readers")

    def __init__(self):
        self.writer = 0
        self.readers = {}


class Prog:

    def __init__(self, cur, eng, sems):
        self.cur = cur
        self.eng = eng
        self.sems = sems
        self.c = {n: 0 for n in SEM_NAMES}
        self.banks = {}

    def op(self, key, fn, waits=(), inc=None, n=1):
        if self.cur == key:
            for s, v in waits:
                assert v <= self.c[s], f"wait {s}>={v} but only {self.c[s]} emitted"
                if v > 0:
                    self.eng.wait_ge(self.sems[s], v)
            inst = fn(self.eng)
            if inc is not None:
                inst.then_inc(self.sems[inc], n)
        else:
            for s, v in waits:
                assert v <= self.c[s], f"wait {s}>={v} but only {self.c[s]} emitted"
        if inc is not None:
            self.c[inc] += n
            return self.c[inc]
        return None

    def wait(self, key, waits):
        if self.cur == key:
            for s, v in waits:
                assert v <= self.c[s]
                if v > 0:
                    self.eng.wait_ge(self.sems[s], v)
        else:
            for s, v in waits:
                assert v <= self.c[s]

    def bank(self, name):
        if name not in self.banks:
            self.banks[name] = _Bank()
        return self.banks[name]

    def bank_wwaits(self, name):
        return sorted(self.bank(name).readers.items())

    def bank_written(self, name):
        b = self.bank(name)
        b.writer = self.c["pe"]
        b.readers = {}

    def bank_rwaits(self, name):
        return [("pe", self.bank(name).writer)]

    def bank_read(self, name, sem):
        self.bank(name).readers[sem] = self.c[sem]


class WStream:
    NS = 3

    def __init__(self, p, slots, srcs):
        self.p = p
        self.slots = slots
        self.srcs = srcs
        self.consumed = []
        self.emitted = 0

    def emit_load(self, i):
        assert i == self.emitted
        ap, width = self.srcs[i]
        par = i % self.NS
        waits = []
        if i >= self.NS:
            waits = [("pe", self.consumed[i - self.NS])]
        slot = self.slots[par]
        self.p.op("g", lambda g, ap=ap, slot=slot, w=width:
                  g.dma_start(out=slot(w), in_=ap),
                  waits=waits, inc="wld%d" % par, n=16)
        self.emitted += 1

    def consume_waits(self, i):
        return [("wld%d" % (i % self.NS), 16 * (i // self.NS + 1))]

    def mark_consumed(self, i):
        assert len(self.consumed) == i
        self.consumed.append(self.p.c["pe"])
        if i + self.NS < len(self.srcs):
            self.emit_load(i + self.NS)



def build_nc(stage=STAGE, dumps=()):
    nc = bass.Bass("TRN2", target_bir_lowering=False)
    ddbg = {nm: nc.dram_tensor("dbg_" + nm, [128, 1024], F32, kind="ExternalOutput")
            for nm in dumps}

    dx0 = nc.dram_tensor("x0", [2048, 1024], F32, kind="ExternalInput")
    dw13 = nc.dram_tensor("w13s", [NEXP, 22, 128, 2048], F32, kind="ExternalInput")
    dw2 = nc.dram_tensor("w2s", [NEXP, 16, 128, 1408], F32, kind="ExternalInput")
    dwq = nc.dram_tensor("wq", [16, 128, 2048], F32, kind="ExternalInput")
    dwk = nc.dram_tensor("wk", [16, 128, 2048], F32, kind="ExternalInput")
    dwv = nc.dram_tensor("wv", [16, 128, 2048], F32, kind="ExternalInput")
    dgrhs = nc.dram_tensor("grhs", [128, 256], F32, kind="ExternalInput")
    dsgw = nc.dram_tensor("sgw", [128, 16], F32, kind="ExternalInput")
    deye = nc.dram_tensor("eye", [128, 128], F32, kind="ExternalInput")
    donesc = nc.dram_tensor("onesc", [1, 128], F32, kind="ExternalInput")
    donesk = nc.dram_tensor("onesk", [128, 1], F32, kind="ExternalInput")
    dcsq = nc.dram_tensor("csq", [1024, 1024], F32, kind="ExternalInput")
    dssq = nc.dram_tensor("ssq", [1024, 1024], F32, kind="ExternalInput")
    dcsk = nc.dram_tensor("csk", [1024, 1024], F32, kind="ExternalInput")
    dssk = nc.dram_tensor("ssk", [1024, 1024], F32, kind="ExternalInput")
    ddec = nc.dram_tensor("dec", [2, 32, 128, 512], F32, kind="ExternalInput")
    dy = nc.dram_tensor("y", [2048, 1024], F32, kind="ExternalOutput")
    kbi = nc.dram_tensor("kbi", [8, 128, 16, 128], BF16)
    kbg = nc.dram_tensor("kbg", [4 * 8 * 128 * 16, 128], BF16)
    vbi = nc.dram_tensor("vbi", [1024, 2048], BF16)
    vbg = nc.dram_tensor("vbg", [4096, 2048], BF16)

    ctx = ExitStack()
    with ctx:
        sb = ctx.enter_context
        x = [sb(nc.sbuf_tensor(f"sx{i}", [128, 1024], F32)) for i in range(16)]
        xn = [sb(nc.sbuf_tensor(f"xn{i}", [128, 1024], BF16)) for i in range(16)]
        hh = [sb(nc.sbuf_tensor(f"hh{i}", [128, 1024], BF16)) for i in range(16)]
        wb = [sb(nc.sbuf_tensor(f"wb{i}", [128, 2048], BF16)) for i in range(3)]
        kb = [sb(nc.sbuf_tensor(f"kb{i}", [128, 2048], BF16)) for i in range(2)]
        vb = [sb(nc.sbuf_tensor(f"vb{i}", [128, 2048], BF16)) for i in range(4)]
        tb1 = sb(nc.sbuf_tensor("tb1", [128, 1024], F32))
        tb0 = sb(nc.sbuf_tensor("tb0", [128, 1024], F32))
        dbgt = sb(nc.sbuf_tensor("dbgt", [128, 1024], F32)) if dumps else None
        ts1 = sb(nc.sbuf_tensor("ts1", [128, 1024], F32))
        aB = sb(nc.sbuf_tensor("aB", [128, 1024], F32))
        bB = sb(nc.sbuf_tensor("bB", [128, 1024], F32))
        cB = sb(nc.sbuf_tensor("cB", [128, 1024], F32))
        sgB = sb(nc.sbuf_tensor("sgB", [128, 1024], F32))
        mrow = sb(nc.sbuf_tensor("mrow", [1, 1024], F32))
        vrow = sb(nc.sbuf_tensor("vrow", [1, 1024], F32))
        rwall = sb(nc.sbuf_tensor("rwall", [128, 128], F32))
        rsc = sb(nc.sbuf_tensor("rsc", [128, 8], F32))
        mxs = sb(nc.sbuf_tensor("mxs", [128, 1], F32))
        msks = sb(nc.sbuf_tensor("msks", [128, 16], F32))
        rw2s = sb(nc.sbuf_tensor("rw2s", [128, 16], F32))
        cofs = sb(nc.sbuf_tensor("cofs", [128, 128], F32))
        coefT = sb(nc.sbuf_tensor("coefT", [16, 1024], F32))
        eyef = sb(nc.sbuf_tensor("eyef", [128, 128], F32))
        eyeb = sb(nc.sbuf_tensor("eyeb", [128, 128], BF16))
        onescf = sb(nc.sbuf_tensor("onescf", [1, 128], F32))
        oneskb = sb(nc.sbuf_tensor("oneskb", [128, 1], BF16))
        grhsb = sb(nc.sbuf_tensor("grhsb", [128, 256], BF16))
        sgwb = sb(nc.sbuf_tensor("sgwb", [128, 16], BF16))
        P = [sb(nc.psum_tensor(f"P{i}", [128, 512], F32)) for i in range(6)]
        p6b = sb(nc.psum_tensor("p6b", [128, 1024], BF16))
        p7b = sb(nc.psum_tensor("p7b", [128, 1024], BF16))
        sems = {n: sb(nc.semaphore(n)) for n in SEM_NAMES}
        block = sb(nc.Block())

        srcs = []
        for call in range(2 if stage >= 3 else 1):
            for e in range(NEXP):
                for c in range(22):
                    srcs.append((dw13[e, c, :, :], 2048))
                for f in range(16):
                    srcs.append((dw2[e, f, :, :], 1408))
            if call == 0 and stage >= 2:
                for i in range(16):
                    srcs.append((dwk[i, :, :], 2048))
                for i in range(16):
                    srcs.append((dwv[i, :, :], 2048))
                for i in range(16):
                    srcs.append((dwq[i, :, :], 2048))

        def emit_all(p: Prog):
            ws = WStream(p, [lambda w, s=wb[0]: s[:, 0:w],
                             lambda w, s=wb[1]: s[:, 0:w],
                             lambda w, s=wb[2]: s[:, 0:w]], srcs)
            st_total = [0]

            p.op("g", lambda g: g.dma_start(out=grhsb[:, :], in_=dgrhs[:, :]), inc="xld", n=16)
            p.op("g", lambda g: g.dma_start(out=sgwb[:, :], in_=dsgw[:, :]), inc="xld", n=16)
            p.op("g", lambda g: g.dma_start(out=eyeb[:, :], in_=deye[:, :]), inc="xld", n=16)
            p.op("g", lambda g: g.dma_start(out=oneskb[:, :], in_=donesk[:, :]), inc="xld", n=16)
            ws.emit_load(0)
            ws.emit_load(1)
            ws.emit_load(2)
            p.op("y", lambda y: y.dma_start(out=eyef[:, :], in_=deye[:, :]), inc="xld", n=16)
            p.op("y", lambda y: y.dma_start(out=onescf[:, :], in_=donesc[:, :]), inc="xld", n=16)
            consts_ld = p.c["xld"]
            for t in range(16):
                qk = "y" if t % 2 == 0 else "s"
                p.op(qk, lambda e, t=t: e.dma_start(out=x[t][:, :], in_=dx0[t * 128:(t + 1) * 128, :]),
                     inc="xld", n=16)
            pro_ld = p.c["xld"]
            consts_ld = pro_ld

            state = {"xn_ready": 0, "w2_end": 0, "xn_free_pe": 0, "x_ready_dve": 0,
                     "x_ready_xld": pro_ld, "x_tile_dve": [0] * 16}

            def dump(nm, ap, rows=128, cols=1024):
                if nm not in ddbg:
                    return
                w = [("pe", p.c["pe"]), ("act", p.c["act"]), ("st", p.c["st"])]
                p.op("v", lambda v: v.tensor_copy(dbgt[0:rows, 0:cols], ap), waits=w,
                     inc="dve")
                p.op("y", lambda y, nm=nm: y.dma_start(
                    out=ddbg[nm][0:rows, 0:cols], in_=dbgt[0:rows, 0:cols]),
                    waits=[("dve", p.c["dve"])], inc="st", n=16)

            def emit_ln():
                for t in range(16):
                    w = [("dve", max(state["x_ready_dve"], state["x_tile_dve"][t]))]
                    if t == 0:
                        w += [("xld", state["x_ready_xld"]), ("pe", state["xn_free_pe"]),
                              ("xld", consts_ld)]
                    p.op("s", lambda s, t=t: s.activation(xn[t][:, :], x[t][:, :], AF.Copy),
                         waits=w, inc="act")
                    a_copy = p.c["act"]
                    for sp in range(2):
                        bank = f"P{sp}"
                        bw = p.bank_wwaits(bank) if t == 0 else []
                        p.op("t", lambda te, t=t, sp=sp: te.matmul(
                            P[sp][0:1, :], oneskb[:, 0:1], xn[t][:, sp * 512:(sp + 1) * 512],
                            start=(t == 0), stop=(t == 15), skip_group_check=True),
                            waits=bw + [("act", a_copy)], inc="pe")
                        if t == 15:
                            p.bank_written(bank)
                sum_done = p.c["pe"]
                for t in range(16):
                    p.op("s", lambda s, t=t: s.activation(xn[t][:, :], x[t][:, :], AF.Square),
                         waits=[("pe", sum_done)] if t == 0 else [], inc="act")
                    a_sq = p.c["act"]
                    for sp in range(2):
                        bank = f"P{2 + sp}"
                        bw = p.bank_wwaits(bank) if t == 0 else []
                        p.op("t", lambda te, t=t, sp=sp: te.matmul(
                            P[2 + sp][0:1, :], oneskb[:, 0:1], xn[t][:, sp * 512:(sp + 1) * 512],
                            start=(t == 0), stop=(t == 15), skip_group_check=True),
                            waits=bw + [("act", a_sq)], inc="pe")
                        if t == 15:
                            p.bank_written(bank)
                sq_done = p.c["pe"]
                for sp in range(2):
                    sl = slice(sp * 512, (sp + 1) * 512)
                    p.op("v", lambda v, sp=sp, sl=sl: v.tensor_scalar_mul(
                        mrow[0:1, sl], P[sp][0:1, :], 1.0 / 2048.0),
                        waits=[("pe", sq_done)], inc="dve")
                    p.bank_read(f"P{sp}", "dve")
                    p.op("v", lambda v, sl=sl: v.tensor_mul(
                        vrow[0:1, sl], mrow[0:1, sl], mrow[0:1, sl]), inc="dve")
                    p.op("v", lambda v, sp=sp, sl=sl: v.scalar_tensor_tensor(
                        vrow[0:1, sl], P[2 + sp][0:1, :], 1.0 / 2048.0, vrow[0:1, sl],
                        ALU.mult, ALU.subtract), inc="dve")
                    p.bank_read(f"P{2 + sp}", "dve")
                for sp in range(2):
                    sl = slice(sp * 512, (sp + 1) * 512)
                    p.op("v", lambda v, sl=sl: v.tensor_scalar_add(
                        vrow[0:1, sl], vrow[0:1, sl], EPS), inc="dve")
                row_v = p.c["dve"]
                for sp in range(2):
                    sl = slice(sp * 512, (sp + 1) * 512)
                    p.op("s", lambda s, sl=sl: s.activation(
                        vrow[0:1, sl], vrow[0:1, sl], AF.Ln),
                        waits=[("dve", row_v)] if sp == 0 else [], inc="act")
                    p.op("s", lambda s, sl=sl: s.activation(
                        vrow[0:1, sl], vrow[0:1, sl], AF.Exp, scale=-0.5), inc="act")
                row_r = p.c["act"]
                for sp in range(2):
                    sl = slice(sp * 512, (sp + 1) * 512)
                    p.op("v", lambda v, sl=sl: v.scalar_tensor_tensor(
                        mrow[0:1, sl], mrow[0:1, sl], -1.0, vrow[0:1, sl],
                        ALU.mult, ALU.mult),
                        waits=[("act", row_r)] if sp == 0 else [], inc="dve")
                row_b = p.c["dve"]
                for sp in range(2):
                    sl = slice(sp * 512, (sp + 1) * 512)
                    for (dst, src, rw) in ((aB, vrow, row_r), (bB, mrow, row_b)):
                        bank = "P4" if dst is aB else "P5"
                        pi = P[4] if dst is aB else P[5]
                        p.op("t", lambda te, pi=pi, src=src, sl=sl: te.matmul(
                            pi[:, :], onescf[0:1, :], src[0:1, sl],
                            start=True, stop=True, skip_group_check=True),
                            waits=p.bank_wwaits(bank) + [("dve", row_b), ("act", row_r)],
                            inc="pe")
                        p.bank_written(bank)
                        p.op("v", lambda v, dst=dst, pi=pi, sl=sl: v.tensor_copy(
                            dst[:, sl], pi[:, :]),
                            waits=p.bank_rwaits(bank), inc="dve")
                        p.bank_read(bank, "dve")
                ab_done = p.c["dve"]
                xn_tile = [0] * 16
                for t in range(16):
                    w = [("dve", ab_done), ("pe", sq_done)] if t == 0 else []
                    p.op("v", lambda v, t=t: v.tensor_mul(ts1[:, :], x[t][:, :], aB[:, :]),
                         waits=w, inc="dve")
                    p.op("v", lambda v, t=t: v.tensor_add(xn[t][:, :], ts1[:, :], bB[:, :]),
                         inc="dve")
                    xn_tile[t] = p.c["dve"]
                state["xn_ready"] = p.c["dve"]
                state["xn_tile"] = xn_tile
                dump("xn0", xn[0][:, :])
                dump("ab", aB[:, :])
                dump("bb", bB[:, :])

            def emit_sgate_router():
                for sp in range(2):
                    bank = f"P{4 + sp}"
                    pi = P[4 + sp]
                    for kt in range(16):
                        w = (p.bank_wwaits(bank) if kt == 0 else []) + \
                            ([("dve", state["xn_tile"][kt])] if sp == 0 else [])
                        p.op("t", lambda te, pi=pi, kt=kt, sp=sp: te.matmul(
                            pi[0:1, :], sgwb[:, kt:kt + 1],
                            xn[kt][:, sp * 512:(sp + 1) * 512],
                            start=(kt == 0), stop=(kt == 15), skip_group_check=True),
                            waits=w, inc="pe")
                    p.bank_written(bank)
                    p.op("s", lambda s, pi=pi, sp=sp: s.activation(
                        mrow[0:1, sp * 512:(sp + 1) * 512], pi[0:1, :], AF.Sigmoid),
                        waits=p.bank_rwaits(bank), inc="act")
                    p.bank_read(bank, "act")
                sg_row = p.c["act"]
                for sp in range(2):
                    bank = f"P{4 + sp}"
                    pi = P[4 + sp]
                    sl = slice(sp * 512, (sp + 1) * 512)
                    p.op("t", lambda te, pi=pi, sl=sl: te.matmul(
                        pi[:, :], onescf[0:1, :], mrow[0:1, sl],
                        start=True, stop=True, skip_group_check=True),
                        waits=p.bank_wwaits(bank) + [("act", sg_row)], inc="pe")
                    p.bank_written(bank)
                    p.op("v", lambda v, pi=pi, sl=sl: v.tensor_scalar_mul(
                        sgB[:, sl], pi[:, :], 0.5),
                        waits=p.bank_rwaits(bank), inc="dve")
                    p.bank_read(bank, "dve")
                aexp_hist = [0, 0]
                lnx_hist = [0, 0]
                rinv_hist = [0, 0]

                def ved(fn, waits=()):
                    val = p.op("v", fn, waits=waits, inc="dve")
                    p.wait("v", [("dve", val)])
                    return val

                lg_pe = []
                for t in range(8):
                    bank = "P4" if t < 4 else "P5"
                    pi = P[4] if t < 4 else P[5]
                    cols = slice((t % 4) * 16, (t % 4) * 16 + 16)
                    for kt in range(16):
                        w = (p.bank_wwaits(bank) if kt == 0 and t % 4 == 0 else []) + \
                            ([("dve", state["xn_tile"][kt])] if t == 0 else [])
                        p.op("t", lambda te, pi=pi, kt=kt, t=t, cols=cols: te.matmul(
                            pi[:, cols], xn[kt][:, t * 128:(t + 1) * 128],
                            grhsb[:, kt * 16:(kt + 1) * 16],
                            start=(kt == 0), stop=(kt == 15), skip_group_check=True),
                            waits=w, inc="pe")
                    p.bank_written(bank)
                    lg_pe.append(p.c["pe"])
                coef_dve = []
                for t in range(8):
                    par = t % 2
                    bank = "P4" if t < 4 else "P5"
                    pi = P[4] if t < 4 else P[5]
                    cols = slice((t % 4) * 16, (t % 4) * 16 + 16)
                    nmc = rsc[:, par:par + 1]
                    sumc = rsc[:, 2 + par:3 + par]
                    rinvc = rsc[:, 4 + par:5 + par]
                    coefc = cofs[:, t * 16:(t + 1) * 16]
                    bank_done = lg_pe[3] if t < 4 else lg_pe[7]
                    nm = p.op("v", lambda v, pi=pi, nmc=nmc, cols=cols: v.reduce_max(
                        nmc, pi[:, cols], axis=AX, negate=True),
                        waits=[("pe", bank_done), ("act", aexp_hist[par])], inc="dve")
                    rsl = slice(t * 16, (t + 1) * 16)
                    p.op("s", lambda s, pi=pi, rsl=rsl, nmc=nmc, cols=cols: s.activation(
                        rwall[:, rsl], pi[:, cols], AF.Exp, bias=nmc[:, 0:1]),
                        waits=[("pe", bank_done), ("dve", nm)], inc="act")
                    p.bank_read(bank, "act")
                    p.bank_read(bank, "dve")
                    aexp = p.c["act"]
                    aexp_hist[par] = aexp
                    sm = p.op("v", lambda v, rsl=rsl, sumc=sumc: v.reduce_sum(
                        sumc, rwall[:, rsl], axis=AX),
                        waits=[("act", max(aexp, lnx_hist[par]))], inc="dve")
                    p.op("s", lambda s, sumc=sumc, rinvc=rinvc: s.activation(
                        rinvc, sumc, AF.Ln),
                        waits=[("dve", max(sm, rinv_hist[par]))], inc="act")
                    p.op("s", lambda s, rinvc=rinvc: s.activation(
                        rinvc, rinvc, AF.Exp, scale=-1.0), inc="act")
                    arinv = p.c["act"]
                    rinv_hist[par] = ved(lambda v, rsl=rsl, rinvc=rinvc: v.tensor_scalar_mul(
                        rwall[:, rsl], rwall[:, rsl], rinvc), waits=[("act", arinv)])
                    for it in range(4):
                        srcv = rwall[:, rsl] if it == 0 else rw2s[:, :]
                        ved(lambda v, srcv=srcv: v.reduce_max(mxs[:, 0:1], srcv, axis=AX))
                        if it < 3:
                            ved(lambda v, srcv=srcv: v.tensor_scalar(
                                msks[:, :], srcv, mxs[:, 0:1], None, ALU.is_lt))
                            ved(lambda v, srcv=srcv: v.tensor_mul(
                                rw2s[:, :], srcv, msks[:, :]))
                    ved(lambda v, rsl=rsl: v.tensor_scalar(
                        msks[:, :], rwall[:, rsl], mxs[:, 0:1], None, ALU.is_ge))
                    cd = ved(lambda v, rsl=rsl, coefc=coefc: v.scalar_tensor_tensor(
                        coefc, rwall[:, rsl], 0.5, msks[:, :],
                        ALU.mult, ALU.mult))
                    coef_dve.append(cd)
                state["rt_pending"] = coef_dve
                state["coefT_ready"] = p.c["dve"]
                dump("sgb", sgB[:, :])
                dump("rw", rwall[:, :], cols=128)
                dump("coeft", coefT[:, :], rows=16)

            def emit_router_transposes():
                coef_dve = state.pop("rt_pending")
                for t in range(8):
                    coefc = cofs[:, t * 16:(t + 1) * 16]
                    p.op("t", lambda te, coefc=coefc: te.transpose(
                        P[4][0:16, 0:128], coefc, eyef[:, :]),
                        waits=p.bank_wwaits("P4") + [("dve", coef_dve[t]),
                                                     ("xld", consts_ld)],
                        inc="pe")
                    p.bank_written("P4")
                    p.op("v", lambda v, t=t: v.tensor_copy(
                        coefT[:, t * 128:(t + 1) * 128], P[4][0:16, 0:128]),
                        waits=p.bank_rwaits("P4"), inc="dve")
                    p.bank_read("P4", "dve")
                state["coefT_ready"] = p.c["dve"]

            def emit_expert(e):
                routed = e >= 4
                er = e - 4
                cb = cB if routed else sgB
                if routed:
                    for sp in range(2):
                        bank = f"P{2 + sp}"
                        pi = P[2 + sp]
                        sl = slice(sp * 512, (sp + 1) * 512)
                        p.op("t", lambda te, pi=pi, sl=sl, er=er: te.matmul(
                            pi[0:1, :], eyef[0:16, er:er + 1], coefT[0:16, sl],
                            start=True, stop=True, skip_group_check=True),
                            waits=p.bank_wwaits(bank) + [("dve", state["coefT_ready"])],
                            inc="pe")
                        p.bank_written(bank)
                        p.op("v", lambda v, pi=pi, sl=sl: v.tensor_copy(mrow[0:1, sl], pi[0:1, :]),
                             waits=p.bank_rwaits(bank), inc="dve")
                        p.bank_read(bank, "dve")
                    row_done = p.c["dve"]
                    for sp in range(2):
                        bank = f"P{2 + sp}"
                        pi = P[2 + sp]
                        sl = slice(sp * 512, (sp + 1) * 512)
                        p.op("t", lambda te, pi=pi, sl=sl: te.matmul(
                            pi[:, :], onescf[0:1, :], mrow[0:1, sl],
                            start=True, stop=True, skip_group_check=True),
                            waits=p.bank_wwaits(bank) + [("dve", row_done)],
                            inc="pe")
                        p.bank_written(bank)
                        p.op("v", lambda v, pi=pi, sl=sl: v.tensor_copy(cB[:, sl], pi[:, :]),
                             waits=p.bank_rwaits(bank), inc="dve")
                        p.bank_read(bank, "dve")
                hprev = [0, 0]
                for c in range(11):
                    if e == 0 and c == 5 and state.get("rt_pending"):
                        emit_router_transposes()
                    sil = [0, 0]
                    for half in range(2):
                        i = len(ws.consumed)
                        bk = (0, 1) if half == 0 else (2, 3)
                        for sp in range(2):
                            bank = f"P{bk[sp]}"
                            w = []
                            if sp == 0:
                                w += ws.consume_waits(i)
                            w += p.bank_wwaits(bank)
                            par = i % WStream.NS
                            fine = (c == 0 and half == 0 and sp == 0)
                            for kt in range(16):
                                wk_ = (w if kt == 0 else [])
                                if fine:
                                    wk_ = wk_ + [("dve", state["xn_tile"][kt])]
                                p.op("t", lambda te, bk=bk, sp=sp, par=par, kt=kt: te.matmul(
                                    P[bk[sp]][:, :], wb[par][:, kt * 128:(kt + 1) * 128],
                                    xn[kt][:, sp * 512:(sp + 1) * 512],
                                    start=(kt == 0), stop=(kt == 15), skip_group_check=True),
                                    waits=wk_, inc="pe" if kt == 15 else None)
                            p.bank_written(bank)
                        ws.mark_consumed(i)
                        if half == 0:
                            for sp in range(2):
                                bank = f"P{sp}"
                                p.op("s", lambda s, sp=sp, c=c: s.activation(
                                    ts1[:, sp * 512:(sp + 1) * 512], P[sp][:, :], AF.Silu),
                                    waits=p.bank_rwaits(bank) +
                                    [("dve", hprev[sp])], inc="act")
                                p.bank_read(bank, "act")
                                sil[sp] = p.c["act"]
                    for sp in range(2):
                        bank = f"P{2 + sp}"
                        sl = slice(sp * 512, (sp + 1) * 512)
                        p.op("v", lambda v, sp=sp, sl=sl, cb=cb: v.tensor_mul(
                            tb0[:, sl], P[2 + sp][:, :], cb[:, sl]),
                            waits=p.bank_rwaits(bank), inc="dve")
                        p.bank_read(bank, "dve")
                        p.op("v", lambda v, c=c, sl=sl: v.tensor_mul(
                            hh[c][:, sl], tb0[:, sl], ts1[:, sl]),
                            waits=[("act", sil[sp]), ("pe", state["w2_end"])], inc="dve")
                        hprev[sp] = p.c["dve"]
                h_done = p.c["dve"]
                if e == 4:
                    dump("cb0", cB[:, :])
                    dump("h00", hh[0][:, :])
                for f in range(16):
                    i = len(ws.consumed)
                    bks = (4, 5) if f % 2 == 0 else (2, 3)
                    par = i % WStream.NS
                    for sp in range(2):
                        bank = f"P{bks[sp]}"
                        pi = P[bks[sp]]
                        w = []
                        if sp == 0:
                            w += ws.consume_waits(i)
                            if f == 0:
                                w += [("dve", h_done)]
                        w += p.bank_wwaits(bank)
                        for kt in range(11):
                            p.op("t", lambda te, pi=pi, par=par, kt=kt, sp=sp: te.matmul(
                                pi[:, :], wb[par][:, kt * 128:(kt + 1) * 128],
                                hh[kt][:, sp * 512:(sp + 1) * 512],
                                start=(kt == 0), stop=(kt == 10), skip_group_check=True),
                                waits=w if kt == 0 else [], inc="pe" if kt == 10 else None)
                        p.bank_written(bank)
                        p.op("v", lambda v, pi=pi, f=f, sp=sp: v.scalar_tensor_tensor(
                            x[f][:, sp * 512:(sp + 1) * 512], pi[:, :], 1.0,
                            x[f][:, sp * 512:(sp + 1) * 512], ALU.mult, ALU.add),
                            waits=p.bank_rwaits(bank), inc="dve")
                        p.bank_read(bank, "dve")
                        if e == NEXP - 1 and sp == 1:
                            state["x_tile_dve"][f] = p.c["dve"]
                    ws.mark_consumed(i)
                state["w2_end"] = p.c["pe"]

            def emit_ffn():
                emit_ln()
                emit_sgate_router()
                for e in range(NEXP):
                    emit_expert(e)
                state["x_ready_dve"] = p.c["dve"]
                state["xn_free_pe"] = p.c["pe"]

            def emit_qkv():
                emit_ln()
                tab_consumed = {k: p.c["dve"] for k in ("c0", "c1", "s0", "s1")}

                def emit_pair(i, dcs, dss, out_lo, out_hi, store):
                    cs_slot = tb0 if i % 2 == 0 else tb1
                    ss_slot = aB if i % 2 == 0 else bB
                    cs_sem = "tld%d" % (i % 2)
                    ss_sem = "tld%d" % (2 + i % 2)
                    p.op("y", lambda y, i=i, dcs=dcs, cs_slot=cs_slot: y.dma_start(
                        out=cs_slot[:, :], in_=dcs[i * 128:(i + 1) * 128, :]),
                        waits=[("dve", tab_consumed["c%d" % (i % 2)])], inc=cs_sem, n=16)
                    p.op("y", lambda y, i=i, dss=dss, ss_slot=ss_slot: y.dma_start(
                        out=ss_slot[:, :], in_=dss[i * 128:(i + 1) * 128, :]),
                        waits=[("dve", tab_consumed["s%d" % (i % 2)])], inc=ss_sem, n=16)
                    tlw = [(cs_sem, p.c[cs_sem]), (ss_sem, p.c[ss_sem])]
                    i_lo = len(ws.consumed)
                    i_up = i_lo + 1
                    par_lo = i_lo % WStream.NS
                    par_up = i_up % WStream.NS
                    for sp in range(2):
                        blo, bup = f"P{2 * sp}", f"P{2 * sp + 1}"
                        for half in range(2):
                            bank = blo if half == 0 else bup
                            pi = P[2 * sp + half]
                            par = par_lo if half == 0 else par_up
                            i_ch = i_lo if half == 0 else i_up
                            w = p.bank_wwaits(bank)
                            if sp == 0:
                                w += ws.consume_waits(i_ch)
                            fine = (i == 0 and sp == 0 and half == 0)
                            for kt in range(16):
                                wk_ = (w if kt == 0 else [])
                                if fine:
                                    wk_ = wk_ + [("dve", state["xn_tile"][kt])]
                                p.op("t", lambda te, pi=pi, sp=sp, par=par, kt=kt: te.matmul(
                                    pi[:, :], wb[par][:, kt * 128:(kt + 1) * 128],
                                    xn[kt][:, sp * 512:(sp + 1) * 512],
                                    start=(kt == 0), stop=(kt == 15), skip_group_check=True),
                                    waits=wk_, inc="pe" if kt == 15 else None)
                            p.bank_written(bank)
                        sl = slice(sp * 512, (sp + 1) * 512)
                        w0 = p.bank_rwaits(blo) + p.bank_rwaits(bup) + tlw
                        if sp == 0 and store is not None:
                            w0 += store["stage_free"]
                        p.op("v", lambda v, sp=sp, sl=sl, cs_slot=cs_slot: v.tensor_mul(
                            ts1[:, sl], P[2 * sp][:, :], cs_slot[:, sl]), waits=w0, inc="dve")
                        p.op("v", lambda v, sp=sp, sl=sl, ss_slot=ss_slot: v.tensor_mul(
                            sgB[:, sl], P[2 * sp + 1][:, :], ss_slot[:, sl]), inc="dve")
                        p.op("v", lambda v, sl=sl, out_lo=out_lo: v.tensor_sub(
                            out_lo[:, sl], ts1[:, sl], sgB[:, sl]), inc="dve")
                        p.op("v", lambda v, sp=sp, sl=sl, cs_slot=cs_slot: v.tensor_mul(
                            ts1[:, sl], P[2 * sp + 1][:, :], cs_slot[:, sl]), inc="dve")
                        p.bank_read(bup, "dve")
                        p.op("v", lambda v, sp=sp, sl=sl, ss_slot=ss_slot: v.tensor_mul(
                            sgB[:, sl], P[2 * sp][:, :], ss_slot[:, sl]), inc="dve")
                        p.bank_read(blo, "dve")
                        p.op("v", lambda v, sl=sl, out_hi=out_hi: v.tensor_add(
                            out_hi[:, sl], ts1[:, sl], sgB[:, sl]), inc="dve")
                    tab_consumed["c%d" % (i % 2)] = p.c["dve"]
                    tab_consumed["s%d" % (i % 2)] = p.c["dve"]
                    ws.mark_consumed(i_lo)
                    ws.mark_consumed(i_up)
                    return p.c["dve"]

                stage_free = [[], []]
                for i in range(8):
                    lo, hi = hh[11 + 2 * (i % 2)], hh[12 + 2 * (i % 2)]
                    dn = emit_pair(i, dcsk, dssk, lo, hi,
                                   {"stage_free": stage_free[i % 2]})
                    ks = "kst%d" % (i % 2)
                    for (tile, ht) in ((lo, i), (hi, i + 8)):
                        p.op("s", lambda e, tile=tile, ht=ht: e.dma_start(
                            out=kbi[:, :, ht, :].rearrange("l p m -> p l m"),
                            in_=tile[:, :].rearrange("p (l m) -> p l m", l=8)),
                            waits=[("dve", dn)], inc=ks, n=16)
                    stage_free[i % 2] = [(ks, p.c[ks])]
                k_store_done = [("kst0", p.c["kst0"]), ("kst1", p.c["kst1"])]
                for f in range(16):
                    if f == 2:
                        p.op("g", lambda g: g.collective_compute(
                            "AllGather", ALU.bypass,
                            replica_groups=[[0, 1, 2, 3], [4, 5, 6, 7]],
                            ins=[kbi.ap().rearrange("a p h m -> (a p h) m").opt()],
                            outs=[kbg.ap().opt()]),
                            waits=k_store_done, inc="cc")
                    vtmp = hh[2 * (f % 2)]
                    vtg = hh[2 * (f % 2) + 1]
                    vsem = "vst%d" % (f % 2)
                    i_ch = len(ws.consumed)
                    par = i_ch % WStream.NS
                    for sp in range(2):
                        bank = f"P{4 + sp}"
                        w = (ws.consume_waits(i_ch) if sp == 0 else []) + p.bank_wwaits(bank)
                        for kt in range(16):
                            p.op("t", lambda te, sp=sp, par=par, kt=kt: te.matmul(
                                P[4 + sp][:, :], wb[par][:, kt * 128:(kt + 1) * 128],
                                xn[kt][:, sp * 512:(sp + 1) * 512],
                                start=(kt == 0), stop=(kt == 15), skip_group_check=True),
                                waits=w if kt == 0 else [], inc="pe" if kt == 15 else None)
                        p.bank_written(bank)
                    ws.mark_consumed(i_ch)
                    for sp in range(2):
                        bank = f"P{4 + sp}"
                        w = p.bank_rwaits(bank)
                        if f >= 2 and sp == 0:
                            w += [(vsem, 16 * (f // 2))]
                        p.op("v", lambda v, sp=sp, vtmp=vtmp: v.tensor_copy(
                            vtmp[:, sp * 512:(sp + 1) * 512], P[4 + sp][:, :]),
                            waits=w, inc="dve")
                        p.bank_read(bank, "dve")
                    vt_done = p.c["dve"]
                    for m in range(8):
                        pb = p6b if m % 2 == 0 else p7b
                        bname = "p6b" if m % 2 == 0 else "p7b"
                        p.op("t", lambda te, pb=pb, m=m, vtmp=vtmp: te.transpose(
                            pb[:, 0:128], vtmp[:, m * 128:(m + 1) * 128], eyeb[:, :]),
                            waits=p.bank_wwaits(bname) +
                            ([("dve", vt_done), ("xld", consts_ld)] if m == 0 else []),
                            inc="pe")
                        p.bank_written(bname)
                        p.op("v", lambda v, pb=pb, m=m, vtg=vtg: v.tensor_copy(
                            vtg[:, m * 128:(m + 1) * 128], pb[:, 0:128]),
                            waits=p.bank_rwaits(bname), inc="dve")
                        p.bank_read(bname, "dve")
                    p.op("s", lambda e, f=f, vtg=vtg: e.dma_start(
                        out=vbi[:, f * 128:(f + 1) * 128].rearrange("(m p) c -> p m c", p=128),
                        in_=vtg[:, :].rearrange("p (m c) -> p m c", m=8)),
                        waits=[("dve", p.c["dve"])], inc=vsem, n=16)
                v_store_done = [("vst0", p.c["vst0"]), ("vst1", p.c["vst1"])]
                for i in range(8):
                    sfree = [("pe", state["w2_end"])]
                    if i in (3, 4, 5, 6):
                        sfree = sfree + k_store_done
                    if i < 4:
                        sfree = sfree + v_store_done
                    emit_pair(i, dcsq, dssq, hh[i], hh[i + 8], {"stage_free": sfree})
                    if i == 0:
                        p.op("g", lambda g: g.collective_compute(
                            "AllGather", ALU.bypass,
                            replica_groups=[[0, 1, 2, 3], [4, 5, 6, 7]],
                            ins=[vbi.ap().opt()],
                            outs=[vbg.ap().opt()]),
                            waits=v_store_done, inc="cc")
                state["qkv_end_pe"] = p.c["pe"]
                state["cc_done"] = p.c["cc"]

            def emit_attn():
                kq_consumed = []
                dq_consumed = []
                avg_done = []
                entry_dve = p.c["dve"]
                NQ = 48

                def slot_mb(q):
                    return (0, q) if q < 16 else (1, q - 16)

                def slab(mb):
                    pm = mb // 4
                    r = pm if pm < 4 else 7 - pm
                    return r * 8 + (0 if pm < 4 else 4) + mb % 4

                def load_kblock(q):
                    slot, mb = slot_mb(q)
                    s0 = slab(mb) * 2048
                    w = [("cc", state["cc_done"] - 1)]
                    if q >= 2:
                        w = [("pe", kq_consumed[q - 2])]
                    p.op("y", lambda y, s0=s0, q=q: y.dma_start(
                        out=kb[q % 2][:, :],
                        in_=kbg[s0:s0 + 2048, :].rearrange("(p h) m -> p (h m)", p=128)),
                        waits=w, inc="kld%d" % (q % 2), n=16)

                def load_dtile(q):
                    slot, mb = slot_mb(q)
                    w = [("dve", entry_dve)] if q < 2 else [("dve", dq_consumed[q - 2])]
                    sl = slice((q % 2) * 512, (q % 2) * 512 + 512)
                    p.op("y", lambda y, slot=slot, mb=mb, sl=sl: y.dma_start(
                        out=tb0[:, sl], in_=ddec[slot, mb, :, :]),
                        waits=w, inc="dld%d" % (q % 2), n=16)

                def load_vblock(g2, lb):
                    slot = 0 if g2 < 4 else 1
                    mb = (g2 if g2 < 4 else g2 - 4) * 4 + lb
                    s0 = slab(mb) * 128
                    w = [("cc", state["cc_done"])]
                    if g2 >= 1:
                        w = [("pe", avg_done[g2 - 1])]
                    p.op("y", lambda y, s0=s0, lb=lb: y.dma_start(
                        out=vb[lb][:, :], in_=vbg[s0:s0 + 128, :]),
                        waits=w, inc="vld%d" % lb, n=16)

                load_kblock(0)
                load_kblock(1)
                load_dtile(0)
                load_dtile(1)
                for lb in range(4):
                    load_vblock(0, lb)

                x_tile_dve = [0] * 16
                for g2 in range(12):
                    slot = 0 if g2 < 4 else 1
                    g = g2 if g2 < 4 else g2 - 4
                    for lb in range(4):
                        mb = 4 * g + lb
                        q = (0 if slot == 0 else 16) + mb
                        bank = f"P{q % 2}"
                        pi = P[q % 2]
                        w = [("kld%d" % (q % 2), 16 * (q // 2 + 1))] + p.bank_wwaits(bank)
                        if q == 0:
                            w += [("pe", state["qkv_end_pe"])]
                        for ht in range(16):
                            p.op("t", lambda te, pi=pi, q=q, ht=ht, slot=slot: te.matmul(
                                pi[:, :], kb[q % 2][:, ht * 128:(ht + 1) * 128],
                                hh[ht][:, slot * 512:(slot + 1) * 512],
                                start=(ht == 0), stop=(ht == 15), skip_group_check=True),
                                waits=w if ht == 0 else [], inc="pe" if ht == 15 else None)
                        p.bank_written(bank)
                        kq_consumed.append(p.c["pe"])
                        if q + 2 < NQ:
                            load_kblock(q + 2)
                        sl = slice((q % 2) * 512, (q % 2) * 512 + 512)
                        w = p.bank_rwaits(bank) + [("dld%d" % (q % 2), 16 * (q // 2 + 1))]
                        if g2 == 0 and lb == 0:
                            w += [("pe", state["qkv_end_pe"])]
                        if g2 == 4 and lb == 0:
                            w += [("pe", avg_done[3])]
                        p.op("v", lambda v, pi=pi, mb=mb, sl=sl: v.tensor_mul(
                            xn[mb // 2][:, (mb % 2) * 512:(mb % 2) * 512 + 512],
                            pi[:, :], tb0[:, sl]),
                            waits=w, inc="dve")
                        p.bank_read(bank, "dve")
                        dq_consumed.append(p.c["dve"])
                        if q + 2 < NQ:
                            load_dtile(q + 2)
                    attd_done = p.c["dve"]
                    for f in range(16):
                        bank = f"P{4 + f % 2}"
                        pi = P[4 + f % 2]
                        w = p.bank_wwaits(bank) + [("dve", attd_done)]
                        if f == 0:
                            w += [("vld%d" % lb2, 16 * (g2 + 1)) for lb2 in range(4)]
                        for lb in range(4):
                            mb = 4 * g + lb
                            p.op("t", lambda te, pi=pi, f=f, lb=lb, mb=mb: te.matmul(
                                pi[:, :], vb[lb][:, f * 128:(f + 1) * 128],
                                xn[mb // 2][:, (mb % 2) * 512:(mb % 2) * 512 + 512],
                                start=(lb == 0), stop=(lb == 3), skip_group_check=True),
                                waits=w if lb == 0 else [], inc="pe" if lb == 3 else None)
                        p.bank_written(bank)
                        p.op("v", lambda v, pi=pi, f=f, slot=slot: v.scalar_tensor_tensor(
                            x[f][:, slot * 512:(slot + 1) * 512], pi[:, :], 1.0,
                            x[f][:, slot * 512:(slot + 1) * 512], ALU.mult, ALU.add),
                            waits=p.bank_rwaits(bank), inc="dve")
                        p.bank_read(bank, "dve")
                        if g2 == 11:
                            x_tile_dve[f] = p.c["dve"]
                    avg_done.append(p.c["pe"])
                    if g2 + 1 < 12:
                        for lb in range(4):
                            load_vblock(g2 + 1, lb)
                state["x_ready_dve"] = p.c["dve"]
                state["x_tile_dve"] = x_tile_dve
                state["xn_free_pe"] = p.c["pe"]
                state["w2_end"] = p.c["pe"]

            emit_ffn()
            if stage >= 2:
                emit_qkv()
                emit_attn()
            if stage >= 3:
                emit_ffn()

            xfin = p.c["dve"]
            for f in range(16):
                p.op("y", lambda y, f=f: y.dma_start(
                    out=dy[f * 128:(f + 1) * 128, :], in_=x[f][:, :]),
                    waits=[("dve", xfin)] if f == 0 else [], inc="st", n=16)
            p.wait("y", [("st", p.c["st"])])

        @block.tensor
        def _(te):
            emit_all(Prog("t", te, sems))

        @block.vector
        def _(v):
            emit_all(Prog("v", v, sems))

        @block.scalar
        def _(s):
            emit_all(Prog("s", s, sems))

        @block.gpsimd
        def _(g):
            emit_all(Prog("g", g, sems))

        @block.sync
        def _(y):
            emit_all(Prog("y", y, sems))

    return nc


_CACHE = {}


def kernel(L0_attn_X, L0_attn_W_Q, L0_attn_W_K, L0_attn_W_V,
           L0_ffn_gate_w, L0_ffn_sgate_w, L0_ffn_w13, L0_ffn_w2,
           L0_ffn_shared_gate_up_w, L0_ffn_shared_down_w):
    X = np.asarray(L0_attn_X)
    shared = _prep_shared(np.asarray(L0_attn_W_Q), np.asarray(L0_attn_W_K),
                          np.asarray(L0_attn_W_V), np.asarray(L0_ffn_gate_w),
                          np.asarray(L0_ffn_sgate_w), np.asarray(L0_ffn_w13),
                          np.asarray(L0_ffn_w2), np.asarray(L0_ffn_shared_gate_up_w),
                          np.asarray(L0_ffn_shared_down_w))
    in_maps = []
    for c in range(8):
        m = dict(shared)
        m.update(_prep_core(X, c))
        in_maps.append(m)
    if "nc" not in _CACHE:
        _CACHE["nc"] = build_nc(STAGE)
    nc = _CACHE["nc"]
    res = run_bass_kernel_spmd(nc, in_maps, core_ids=list(range(8)))
    out = np.empty((2, 4096, 2048), np.float32)
    for c in range(8):
        b, j = c // 4, c % 4
        yt = res.results[c]["y"].T
        out[b, j * 512:(j + 1) * 512, :] = yt[0:512]
        out[b, (7 - j) * 512:(8 - j) * 512, :] = yt[512:1024]
    return out


if __name__ == "__main__":
    import sys
    if len(sys.argv) > 1 and sys.argv[1] == "buildtest":
        import time
        t0 = time.time()
        nc = build_nc(int(sys.argv[2]) if len(sys.argv) > 2 else STAGE)
        print(f"build ok in {time.time() - t0:.1f}s")


# version 07
# speedup vs baseline: 1.0852x; 1.0007x over previous
import numpy as np
import ml_dtypes
from contextlib import ExitStack

import concourse.bass as bass
import concourse.mybir as mybir
from concourse.bass_utils import run_bass_kernel_spmd

F32 = mybir.dt.float32
BF16 = mybir.dt.bfloat16
AF = mybir.ActivationFunctionType
ALU = mybir.AluOpType
AX = mybir.AxisListType.X
NPBF = ml_dtypes.bfloat16

STAGE = 3
GAMMA = 0.96875
EPS = 1e-5
NEXP = 20


def _lhsT_chunks(W, order):
    K, F = W.shape
    Kt = K // 128
    arr = W.reshape(Kt, 128, F).transpose(1, 0, 2)
    out = np.empty((len(order), 128, Kt * 128), dtype=np.float32)
    for i, f in enumerate(order):
        out[i] = arr[:, :, f * 128:(f + 1) * 128].reshape(128, Kt * 128)
    return out


def _prep_shared(WQ, WK, WV, gate_w, sgate_w, w13, w2, sgu, sd):
    d = {}
    perm = np.concatenate([np.arange(0, 2048, 2), np.arange(1, 2048, 2)])
    pair_order = [x for i in range(8) for x in (i, i + 8)]
    d["wk"] = _lhsT_chunks(np.ascontiguousarray(WK[:, perm]), pair_order)
    d["wq"] = _lhsT_chunks(np.ascontiguousarray(WQ[:, perm]), pair_order)
    d["wv"] = _lhsT_chunks(np.ascontiguousarray(WV), list(range(16)))
    w13s = np.empty((NEXP, 22, 128, 2048), np.float32)
    w2s = np.empty((NEXP, 16, 128, 1408), np.float32)
    inter = [x for c in range(11) for x in (c, 11 + c)]
    SGU = np.ascontiguousarray(sgu.T)
    SD = np.ascontiguousarray(sd.T)
    for ic in range(4):
        order = [x for c in range(11) for x in (ic * 11 + c, 44 + ic * 11 + c)]
        w13s[ic] = _lhsT_chunks(SGU, order)
        w2s[ic] = _lhsT_chunks(
            np.ascontiguousarray(SD[ic * 1408:(ic + 1) * 1408]), list(range(16)))
    for e in range(16):
        w13s[4 + e] = _lhsT_chunks(np.ascontiguousarray(w13[e].T), inter)
        w2s[4 + e] = _lhsT_chunks(np.ascontiguousarray(w2[e].T), list(range(16)))
    d["w13s"] = w13s
    d["w2s"] = w2s
    d["grhs"] = np.ascontiguousarray(
        gate_w.T.reshape(16, 128, 16).transpose(1, 0, 2).reshape(128, 256))
    d["sgw"] = np.ascontiguousarray(sgate_w.reshape(16, 128).T)
    d["eye"] = np.eye(128, dtype=np.float32)
    d["onesc"] = np.ones((1, 128), np.float32)
    d["onesk"] = np.ones((128, 1), np.float32)
    return d


def _prep_core(X, c):
    b, j = c // 4, c % 4
    pos = np.concatenate([np.arange(j * 512, (j + 1) * 512),
                          np.arange((7 - j) * 512, (8 - j) * 512)])
    d = {"x0": np.ascontiguousarray(X[b, pos, :].T).astype(np.float32)}
    i = np.arange(1024, dtype=np.float64)
    sv = (2.0 * i + 0.4 * 2048.0) / (1.4 * 2048.0)
    invf = 10000.0 ** (-i / 1024.0)
    p64 = pos.astype(np.float64)
    scale = sv[:, None] ** (p64[None, :] / 512.0)
    ang = invf[:, None] * p64[None, :]
    d["csq"] = (np.cos(ang) * scale).astype(np.float32)
    d["ssq"] = (np.sin(ang) * scale).astype(np.float32)
    d["csk"] = (np.cos(ang) / scale).astype(np.float32)
    d["ssk"] = (np.sin(ang) / scale).astype(np.float32)
    dec = np.zeros((2, 32, 128, 512), np.float32)
    n0s = (j * 512, (7 - j) * 512)
    for nc2 in range(2):
        n_idx = n0s[nc2] + np.arange(512).astype(np.float64)
        for mb in range(32):
            m_idx = mb * 128 + np.arange(128).astype(np.float64)
            dm = n_idx[None, :] - m_idx[:, None]
            dec[nc2, mb] = np.where(dm >= 0, GAMMA ** np.maximum(dm, 0.0), 0.0
                                    ).astype(np.float32)
    d["dec"] = dec
    return d


SEM_NAMES = ["wld0", "wld1", "wld2", "tld0", "tld1", "kld0", "kld1", "dld0", "dld1",
             "vld0", "vld1", "vld2", "vld3", "xld", "pe", "dve", "act", "cc", "st",
             "kst0", "kst1", "vst0", "vst1", "tld2", "tld3", "xlg"]


class _Bank:
    __slots__ = ("writer", "readers")

    def __init__(self):
        self.writer = 0
        self.readers = {}


class Prog:

    def __init__(self, cur, eng, sems):
        self.cur = cur
        self.eng = eng
        self.sems = sems
        self.c = {n: 0 for n in SEM_NAMES}
        self.banks = {}

    def op(self, key, fn, waits=(), inc=None, n=1):
        if self.cur == key:
            for s, v in waits:
                assert v <= self.c[s], f"wait {s}>={v} but only {self.c[s]} emitted"
                if v > 0:
                    self.eng.wait_ge(self.sems[s], v)
            inst = fn(self.eng)
            if inc is not None:
                inst.then_inc(self.sems[inc], n)
        else:
            for s, v in waits:
                assert v <= self.c[s], f"wait {s}>={v} but only {self.c[s]} emitted"
        if inc is not None:
            self.c[inc] += n
            return self.c[inc]
        return None

    def wait(self, key, waits):
        if self.cur == key:
            for s, v in waits:
                assert v <= self.c[s]
                if v > 0:
                    self.eng.wait_ge(self.sems[s], v)
        else:
            for s, v in waits:
                assert v <= self.c[s]

    def bank(self, name):
        if name not in self.banks:
            self.banks[name] = _Bank()
        return self.banks[name]

    def bank_wwaits(self, name):
        return sorted(self.bank(name).readers.items())

    def bank_written(self, name):
        b = self.bank(name)
        b.writer = self.c["pe"]
        b.readers = {}

    def bank_rwaits(self, name):
        return [("pe", self.bank(name).writer)]

    def bank_read(self, name, sem):
        self.bank(name).readers[sem] = self.c[sem]


class WStream:
    NS = 3

    def __init__(self, p, slots, srcs):
        self.p = p
        self.slots = slots
        self.srcs = srcs
        self.consumed = []
        self.emitted = 0

    def emit_load(self, i):
        assert i == self.emitted
        ap, width = self.srcs[i]
        par = i % self.NS
        waits = []
        if i >= self.NS:
            waits = [("pe", self.consumed[i - self.NS])]
        slot = self.slots[par]
        self.p.op("g", lambda g, ap=ap, slot=slot, w=width:
                  g.dma_start(out=slot(w), in_=ap),
                  waits=waits, inc="wld%d" % par, n=16)
        self.emitted += 1

    def consume_waits(self, i):
        return [("wld%d" % (i % self.NS), 16 * (i // self.NS + 1))]

    def mark_consumed(self, i):
        assert len(self.consumed) == i
        self.consumed.append(self.p.c["pe"])
        if i + self.NS < len(self.srcs):
            self.emit_load(i + self.NS)



def build_nc(stage=STAGE, dumps=()):
    nc = bass.Bass("TRN2", target_bir_lowering=False)
    ddbg = {nm: nc.dram_tensor("dbg_" + nm, [128, 1024], F32, kind="ExternalOutput")
            for nm in dumps}

    dx0 = nc.dram_tensor("x0", [2048, 1024], F32, kind="ExternalInput")
    dw13 = nc.dram_tensor("w13s", [NEXP, 22, 128, 2048], F32, kind="ExternalInput")
    dw2 = nc.dram_tensor("w2s", [NEXP, 16, 128, 1408], F32, kind="ExternalInput")
    dwq = nc.dram_tensor("wq", [16, 128, 2048], F32, kind="ExternalInput")
    dwk = nc.dram_tensor("wk", [16, 128, 2048], F32, kind="ExternalInput")
    dwv = nc.dram_tensor("wv", [16, 128, 2048], F32, kind="ExternalInput")
    dgrhs = nc.dram_tensor("grhs", [128, 256], F32, kind="ExternalInput")
    dsgw = nc.dram_tensor("sgw", [128, 16], F32, kind="ExternalInput")
    deye = nc.dram_tensor("eye", [128, 128], F32, kind="ExternalInput")
    donesc = nc.dram_tensor("onesc", [1, 128], F32, kind="ExternalInput")
    donesk = nc.dram_tensor("onesk", [128, 1], F32, kind="ExternalInput")
    dcsq = nc.dram_tensor("csq", [1024, 1024], F32, kind="ExternalInput")
    dssq = nc.dram_tensor("ssq", [1024, 1024], F32, kind="ExternalInput")
    dcsk = nc.dram_tensor("csk", [1024, 1024], F32, kind="ExternalInput")
    dssk = nc.dram_tensor("ssk", [1024, 1024], F32, kind="ExternalInput")
    ddec = nc.dram_tensor("dec", [2, 32, 128, 512], F32, kind="ExternalInput")
    dy = nc.dram_tensor("y", [2048, 1024], F32, kind="ExternalOutput")
    kbi = nc.dram_tensor("kbi", [8, 128, 16, 128], BF16)
    kbg = nc.dram_tensor("kbg", [4 * 8 * 128 * 16, 128], BF16)
    vbi = nc.dram_tensor("vbi", [1024, 2048], BF16)
    vbg = nc.dram_tensor("vbg", [4096, 2048], BF16)

    ctx = ExitStack()
    with ctx:
        sb = ctx.enter_context
        x = [sb(nc.sbuf_tensor(f"sx{i}", [128, 1024], F32)) for i in range(16)]
        xn = [sb(nc.sbuf_tensor(f"xn{i}", [128, 1024], BF16)) for i in range(16)]
        hh = [sb(nc.sbuf_tensor(f"hh{i}", [128, 1024], BF16)) for i in range(16)]
        wb = [sb(nc.sbuf_tensor(f"wb{i}", [128, 2048], BF16)) for i in range(3)]
        kb = [sb(nc.sbuf_tensor(f"kb{i}", [128, 2048], BF16)) for i in range(2)]
        vb = [sb(nc.sbuf_tensor(f"vb{i}", [128, 2048], BF16)) for i in range(4)]
        tb1 = sb(nc.sbuf_tensor("tb1", [128, 1024], F32))
        tb0 = sb(nc.sbuf_tensor("tb0", [128, 1024], F32))
        dbgt = sb(nc.sbuf_tensor("dbgt", [128, 1024], F32)) if dumps else None
        ts1 = sb(nc.sbuf_tensor("ts1", [128, 1024], F32))
        aB = sb(nc.sbuf_tensor("aB", [128, 1024], F32))
        bB = sb(nc.sbuf_tensor("bB", [128, 1024], F32))
        cB = sb(nc.sbuf_tensor("cB", [128, 1024], F32))
        sgB = sb(nc.sbuf_tensor("sgB", [128, 1024], F32))
        mrow = sb(nc.sbuf_tensor("mrow", [1, 1024], F32))
        vrow = sb(nc.sbuf_tensor("vrow", [1, 1024], F32))
        rwall = sb(nc.sbuf_tensor("rwall", [128, 128], F32))
        rsc = sb(nc.sbuf_tensor("rsc", [128, 8], F32))
        mxs = sb(nc.sbuf_tensor("mxs", [128, 1], F32))
        msks = sb(nc.sbuf_tensor("msks", [128, 16], F32))
        rw2s = sb(nc.sbuf_tensor("rw2s", [128, 16], F32))
        cofs = sb(nc.sbuf_tensor("cofs", [128, 128], F32))
        coefT = sb(nc.sbuf_tensor("coefT", [16, 1024], F32))
        eyef = sb(nc.sbuf_tensor("eyef", [128, 128], F32))
        eyeb = sb(nc.sbuf_tensor("eyeb", [128, 128], BF16))
        onescf = sb(nc.sbuf_tensor("onescf", [1, 128], F32))
        oneskb = sb(nc.sbuf_tensor("oneskb", [128, 1], BF16))
        grhsb = sb(nc.sbuf_tensor("grhsb", [128, 256], BF16))
        sgwb = sb(nc.sbuf_tensor("sgwb", [128, 16], BF16))
        P = [sb(nc.psum_tensor(f"P{i}", [128, 512], F32)) for i in range(6)]
        p6b = sb(nc.psum_tensor("p6b", [128, 1024], BF16))
        p7b = sb(nc.psum_tensor("p7b", [128, 1024], BF16))
        sems = {n: sb(nc.semaphore(n)) for n in SEM_NAMES}
        block = sb(nc.Block())

        srcs = []
        for call in range(2 if stage >= 3 else 1):
            for e in range(NEXP):
                for c in range(22):
                    srcs.append((dw13[e, c, :, :], 2048))
                for f in range(16):
                    srcs.append((dw2[e, f, :, :], 1408))
            if call == 0 and stage >= 2:
                for i in range(16):
                    srcs.append((dwk[i, :, :], 2048))
                for i in range(16):
                    srcs.append((dwv[i, :, :], 2048))
                for i in range(16):
                    srcs.append((dwq[i, :, :], 2048))

        def emit_all(p: Prog):
            ws = WStream(p, [lambda w, s=wb[0]: s[:, 0:w],
                             lambda w, s=wb[1]: s[:, 0:w],
                             lambda w, s=wb[2]: s[:, 0:w]], srcs)
            st_total = [0]

            p.op("g", lambda g: g.dma_start(out=grhsb[:, :], in_=dgrhs[:, :]), inc="xlg", n=16)
            p.op("g", lambda g: g.dma_start(out=sgwb[:, :], in_=dsgw[:, :]), inc="xlg", n=16)
            p.op("g", lambda g: g.dma_start(out=eyeb[:, :], in_=deye[:, :]), inc="xlg", n=16)
            p.op("g", lambda g: g.dma_start(out=oneskb[:, :], in_=donesk[:, :]), inc="xlg", n=16)
            ws.emit_load(0)
            ws.emit_load(1)
            ws.emit_load(2)
            p.op("y", lambda y: y.dma_start(out=eyef[:, :], in_=deye[:, :]), inc="xld", n=16)
            p.op("y", lambda y: y.dma_start(out=onescf[:, :], in_=donesc[:, :]), inc="xld", n=16)
            consts_ld = p.c["xld"]
            for t in range(16):
                qk = "y" if t % 2 == 0 else "s"
                p.op(qk, lambda e, t=t: e.dma_start(out=x[t][:, :], in_=dx0[t * 128:(t + 1) * 128, :]),
                     inc="xld", n=16)
            pro_ld = p.c["xld"]
            consts_ld = pro_ld
            gp_ld = p.c["xlg"]

            state = {"xn_ready": 0, "w2_end": 0, "xn_free_pe": 0, "x_ready_dve": 0,
                     "x_ready_xld": pro_ld, "x_tile_dve": [0] * 16}

            def dump(nm, ap, rows=128, cols=1024):
                if nm not in ddbg:
                    return
                w = [("pe", p.c["pe"]), ("act", p.c["act"]), ("st", p.c["st"])]
                p.op("v", lambda v: v.tensor_copy(dbgt[0:rows, 0:cols], ap), waits=w,
                     inc="dve")
                p.op("y", lambda y, nm=nm: y.dma_start(
                    out=ddbg[nm][0:rows, 0:cols], in_=dbgt[0:rows, 0:cols]),
                    waits=[("dve", p.c["dve"])], inc="st", n=16)

            def emit_ln():
                for t in range(16):
                    w = [("dve", max(state["x_ready_dve"], state["x_tile_dve"][t]))]
                    if t == 0:
                        w += [("xld", state["x_ready_xld"]), ("pe", state["xn_free_pe"]),
                              ("xlg", gp_ld)]
                    p.op("s", lambda s, t=t: s.activation(xn[t][:, :], x[t][:, :], AF.Copy),
                         waits=w, inc="act")
                    a_copy = p.c["act"]
                    for sp in range(2):
                        bank = f"P{sp}"
                        bw = p.bank_wwaits(bank) if t == 0 else []
                        p.op("t", lambda te, t=t, sp=sp: te.matmul(
                            P[sp][0:1, :], oneskb[:, 0:1], xn[t][:, sp * 512:(sp + 1) * 512],
                            start=(t == 0), stop=(t == 15), skip_group_check=True),
                            waits=bw + [("act", a_copy)], inc="pe")
                        if t == 15:
                            p.bank_written(bank)
                sum_done = p.c["pe"]
                for t in range(16):
                    p.op("s", lambda s, t=t: s.activation(xn[t][:, :], x[t][:, :], AF.Square),
                         waits=[("pe", sum_done)] if t == 0 else [], inc="act")
                    a_sq = p.c["act"]
                    for sp in range(2):
                        bank = f"P{2 + sp}"
                        bw = p.bank_wwaits(bank) if t == 0 else []
                        p.op("t", lambda te, t=t, sp=sp: te.matmul(
                            P[2 + sp][0:1, :], oneskb[:, 0:1], xn[t][:, sp * 512:(sp + 1) * 512],
                            start=(t == 0), stop=(t == 15), skip_group_check=True),
                            waits=bw + [("act", a_sq)], inc="pe")
                        if t == 15:
                            p.bank_written(bank)
                sq_done = p.c["pe"]
                for sp in range(2):
                    sl = slice(sp * 512, (sp + 1) * 512)
                    p.op("v", lambda v, sp=sp, sl=sl: v.tensor_scalar_mul(
                        mrow[0:1, sl], P[sp][0:1, :], 1.0 / 2048.0),
                        waits=[("pe", sq_done)], inc="dve")
                    p.bank_read(f"P{sp}", "dve")
                    p.op("v", lambda v, sl=sl: v.tensor_mul(
                        vrow[0:1, sl], mrow[0:1, sl], mrow[0:1, sl]), inc="dve")
                    p.op("v", lambda v, sp=sp, sl=sl: v.scalar_tensor_tensor(
                        vrow[0:1, sl], P[2 + sp][0:1, :], 1.0 / 2048.0, vrow[0:1, sl],
                        ALU.mult, ALU.subtract), inc="dve")
                    p.bank_read(f"P{2 + sp}", "dve")
                for sp in range(2):
                    sl = slice(sp * 512, (sp + 1) * 512)
                    p.op("v", lambda v, sl=sl: v.tensor_scalar_add(
                        vrow[0:1, sl], vrow[0:1, sl], EPS), inc="dve")
                row_v = p.c["dve"]
                for sp in range(2):
                    sl = slice(sp * 512, (sp + 1) * 512)
                    p.op("s", lambda s, sl=sl: s.activation(
                        vrow[0:1, sl], vrow[0:1, sl], AF.Ln),
                        waits=[("dve", row_v)] if sp == 0 else [], inc="act")
                    p.op("s", lambda s, sl=sl: s.activation(
                        vrow[0:1, sl], vrow[0:1, sl], AF.Exp, scale=-0.5), inc="act")
                row_r = p.c["act"]
                for sp in range(2):
                    sl = slice(sp * 512, (sp + 1) * 512)
                    p.op("v", lambda v, sl=sl: v.scalar_tensor_tensor(
                        mrow[0:1, sl], mrow[0:1, sl], -1.0, vrow[0:1, sl],
                        ALU.mult, ALU.mult),
                        waits=[("act", row_r)] if sp == 0 else [], inc="dve")
                row_b = p.c["dve"]
                for sp in range(2):
                    sl = slice(sp * 512, (sp + 1) * 512)
                    for (dst, src, rw) in ((aB, vrow, row_r), (bB, mrow, row_b)):
                        bank = "P4" if dst is aB else "P5"
                        pi = P[4] if dst is aB else P[5]
                        p.op("t", lambda te, pi=pi, src=src, sl=sl: te.matmul(
                            pi[:, :], onescf[0:1, :], src[0:1, sl],
                            start=True, stop=True, skip_group_check=True),
                            waits=p.bank_wwaits(bank) + [("dve", row_b), ("act", row_r)],
                            inc="pe")
                        p.bank_written(bank)
                        p.op("v", lambda v, dst=dst, pi=pi, sl=sl: v.tensor_copy(
                            dst[:, sl], pi[:, :]),
                            waits=p.bank_rwaits(bank), inc="dve")
                        p.bank_read(bank, "dve")
                ab_done = p.c["dve"]
                xn_tile = [0] * 16
                for t in range(16):
                    w = [("dve", ab_done), ("pe", sq_done)] if t == 0 else []
                    p.op("v", lambda v, t=t: v.tensor_mul(ts1[:, :], x[t][:, :], aB[:, :]),
                         waits=w, inc="dve")
                    p.op("v", lambda v, t=t: v.tensor_add(xn[t][:, :], ts1[:, :], bB[:, :]),
                         inc="dve")
                    xn_tile[t] = p.c["dve"]
                state["xn_ready"] = p.c["dve"]
                state["xn_tile"] = xn_tile
                dump("xn0", xn[0][:, :])
                dump("ab", aB[:, :])
                dump("bb", bB[:, :])

            def emit_sgate_router():
                for sp in range(2):
                    bank = f"P{4 + sp}"
                    pi = P[4 + sp]
                    for kt in range(16):
                        w = (p.bank_wwaits(bank) if kt == 0 else []) + \
                            ([("dve", state["xn_tile"][kt])] if sp == 0 else [])
                        p.op("t", lambda te, pi=pi, kt=kt, sp=sp: te.matmul(
                            pi[0:1, :], sgwb[:, kt:kt + 1],
                            xn[kt][:, sp * 512:(sp + 1) * 512],
                            start=(kt == 0), stop=(kt == 15), skip_group_check=True),
                            waits=w, inc="pe")
                    p.bank_written(bank)
                    p.op("s", lambda s, pi=pi, sp=sp: s.activation(
                        mrow[0:1, sp * 512:(sp + 1) * 512], pi[0:1, :], AF.Sigmoid),
                        waits=p.bank_rwaits(bank), inc="act")
                    p.bank_read(bank, "act")
                sg_row = p.c["act"]
                for sp in range(2):
                    bank = f"P{4 + sp}"
                    pi = P[4 + sp]
                    sl = slice(sp * 512, (sp + 1) * 512)
                    p.op("t", lambda te, pi=pi, sl=sl: te.matmul(
                        pi[:, :], onescf[0:1, :], mrow[0:1, sl],
                        start=True, stop=True, skip_group_check=True),
                        waits=p.bank_wwaits(bank) + [("act", sg_row)], inc="pe")
                    p.bank_written(bank)
                    p.op("v", lambda v, pi=pi, sl=sl: v.tensor_scalar_mul(
                        sgB[:, sl], pi[:, :], 0.5),
                        waits=p.bank_rwaits(bank), inc="dve")
                    p.bank_read(bank, "dve")
                aexp_hist = [0, 0]
                lnx_hist = [0, 0]
                rinv_hist = [0, 0]

                def ved(fn, waits=()):
                    val = p.op("v", fn, waits=waits, inc="dve")
                    p.wait("v", [("dve", val)])
                    return val

                lg_pe = []
                for t in range(8):
                    bank = "P4" if t < 4 else "P5"
                    pi = P[4] if t < 4 else P[5]
                    cols = slice((t % 4) * 16, (t % 4) * 16 + 16)
                    for kt in range(16):
                        w = (p.bank_wwaits(bank) if kt == 0 and t % 4 == 0 else []) + \
                            ([("dve", state["xn_tile"][kt])] if t == 0 else [])
                        p.op("t", lambda te, pi=pi, kt=kt, t=t, cols=cols: te.matmul(
                            pi[:, cols], xn[kt][:, t * 128:(t + 1) * 128],
                            grhsb[:, kt * 16:(kt + 1) * 16],
                            start=(kt == 0), stop=(kt == 15), skip_group_check=True),
                            waits=w, inc="pe")
                    p.bank_written(bank)
                    lg_pe.append(p.c["pe"])
                coef_dve = []
                for t in range(8):
                    par = t % 2
                    bank = "P4" if t < 4 else "P5"
                    pi = P[4] if t < 4 else P[5]
                    cols = slice((t % 4) * 16, (t % 4) * 16 + 16)
                    nmc = rsc[:, par:par + 1]
                    sumc = rsc[:, 2 + par:3 + par]
                    rinvc = rsc[:, 4 + par:5 + par]
                    coefc = cofs[:, t * 16:(t + 1) * 16]
                    bank_done = lg_pe[3] if t < 4 else lg_pe[7]
                    nm = p.op("v", lambda v, pi=pi, nmc=nmc, cols=cols: v.reduce_max(
                        nmc, pi[:, cols], axis=AX, negate=True),
                        waits=[("pe", bank_done), ("act", aexp_hist[par])], inc="dve")
                    rsl = slice(t * 16, (t + 1) * 16)
                    p.op("s", lambda s, pi=pi, rsl=rsl, nmc=nmc, cols=cols: s.activation(
                        rwall[:, rsl], pi[:, cols], AF.Exp, bias=nmc[:, 0:1]),
                        waits=[("pe", bank_done), ("dve", nm)], inc="act")
                    p.bank_read(bank, "act")
                    p.bank_read(bank, "dve")
                    aexp = p.c["act"]
                    aexp_hist[par] = aexp
                    sm = p.op("v", lambda v, rsl=rsl, sumc=sumc: v.reduce_sum(
                        sumc, rwall[:, rsl], axis=AX),
                        waits=[("act", max(aexp, lnx_hist[par]))], inc="dve")
                    p.op("s", lambda s, sumc=sumc, rinvc=rinvc: s.activation(
                        rinvc, sumc, AF.Ln),
                        waits=[("dve", max(sm, rinv_hist[par]))], inc="act")
                    p.op("s", lambda s, rinvc=rinvc: s.activation(
                        rinvc, rinvc, AF.Exp, scale=-1.0), inc="act")
                    arinv = p.c["act"]
                    rinv_hist[par] = ved(lambda v, rsl=rsl, rinvc=rinvc: v.tensor_scalar_mul(
                        rwall[:, rsl], rwall[:, rsl], rinvc), waits=[("act", arinv)])
                    for it in range(4):
                        srcv = rwall[:, rsl] if it == 0 else rw2s[:, :]
                        ved(lambda v, srcv=srcv: v.reduce_max(mxs[:, 0:1], srcv, axis=AX))
                        if it < 3:
                            ved(lambda v, srcv=srcv: v.tensor_scalar(
                                msks[:, :], srcv, mxs[:, 0:1], None, ALU.is_lt))
                            ved(lambda v, srcv=srcv: v.tensor_mul(
                                rw2s[:, :], srcv, msks[:, :]))
                    ved(lambda v, rsl=rsl: v.tensor_scalar(
                        msks[:, :], rwall[:, rsl], mxs[:, 0:1], None, ALU.is_ge))
                    cd = ved(lambda v, rsl=rsl, coefc=coefc: v.scalar_tensor_tensor(
                        coefc, rwall[:, rsl], 0.5, msks[:, :],
                        ALU.mult, ALU.mult))
                    coef_dve.append(cd)
                state["rt_pending"] = coef_dve
                state["coefT_ready"] = p.c["dve"]
                dump("sgb", sgB[:, :])
                dump("rw", rwall[:, :], cols=128)
                dump("coeft", coefT[:, :], rows=16)

            def emit_router_transposes():
                coef_dve = state.pop("rt_pending")
                for t in range(8):
                    coefc = cofs[:, t * 16:(t + 1) * 16]
                    p.op("t", lambda te, coefc=coefc: te.transpose(
                        P[4][0:16, 0:128], coefc, eyef[:, :]),
                        waits=p.bank_wwaits("P4") + [("dve", coef_dve[t]),
                                                     ("xld", consts_ld)],
                        inc="pe")
                    p.bank_written("P4")
                    p.op("v", lambda v, t=t: v.tensor_copy(
                        coefT[:, t * 128:(t + 1) * 128], P[4][0:16, 0:128]),
                        waits=p.bank_rwaits("P4"), inc="dve")
                    p.bank_read("P4", "dve")
                state["coefT_ready"] = p.c["dve"]

            def emit_expert(e):
                routed = e >= 4
                er = e - 4
                cb = cB if routed else sgB
                if routed:
                    for sp in range(2):
                        bank = f"P{2 + sp}"
                        pi = P[2 + sp]
                        sl = slice(sp * 512, (sp + 1) * 512)
                        p.op("t", lambda te, pi=pi, sl=sl, er=er: te.matmul(
                            pi[0:1, :], eyef[0:16, er:er + 1], coefT[0:16, sl],
                            start=True, stop=True, skip_group_check=True),
                            waits=p.bank_wwaits(bank) + [("dve", state["coefT_ready"])],
                            inc="pe")
                        p.bank_written(bank)
                        p.op("v", lambda v, pi=pi, sl=sl: v.tensor_copy(mrow[0:1, sl], pi[0:1, :]),
                             waits=p.bank_rwaits(bank), inc="dve")
                        p.bank_read(bank, "dve")
                    row_done = p.c["dve"]
                    for sp in range(2):
                        bank = f"P{2 + sp}"
                        pi = P[2 + sp]
                        sl = slice(sp * 512, (sp + 1) * 512)
                        p.op("t", lambda te, pi=pi, sl=sl: te.matmul(
                            pi[:, :], onescf[0:1, :], mrow[0:1, sl],
                            start=True, stop=True, skip_group_check=True),
                            waits=p.bank_wwaits(bank) + [("dve", row_done)],
                            inc="pe")
                        p.bank_written(bank)
                        p.op("v", lambda v, pi=pi, sl=sl: v.tensor_copy(cB[:, sl], pi[:, :]),
                             waits=p.bank_rwaits(bank), inc="dve")
                        p.bank_read(bank, "dve")
                hprev = [0, 0]
                for c in range(11):
                    if e == 0 and c == 5 and state.get("rt_pending"):
                        emit_router_transposes()
                    sil = [0, 0]
                    for half in range(2):
                        i = len(ws.consumed)
                        bk = (0, 1) if half == 0 else (2, 3)
                        for sp in range(2):
                            bank = f"P{bk[sp]}"
                            w = []
                            if sp == 0:
                                w += ws.consume_waits(i)
                            w += p.bank_wwaits(bank)
                            par = i % WStream.NS
                            fine = (c == 0 and half == 0 and sp == 0)
                            for kt in range(16):
                                wk_ = (w if kt == 0 else [])
                                if fine:
                                    wk_ = wk_ + [("dve", state["xn_tile"][kt])]
                                p.op("t", lambda te, bk=bk, sp=sp, par=par, kt=kt: te.matmul(
                                    P[bk[sp]][:, :], wb[par][:, kt * 128:(kt + 1) * 128],
                                    xn[kt][:, sp * 512:(sp + 1) * 512],
                                    start=(kt == 0), stop=(kt == 15), skip_group_check=True),
                                    waits=wk_, inc="pe" if kt == 15 else None)
                            p.bank_written(bank)
                        ws.mark_consumed(i)
                        if half == 0:
                            for sp in range(2):
                                bank = f"P{sp}"
                                p.op("s", lambda s, sp=sp, c=c: s.activation(
                                    ts1[:, sp * 512:(sp + 1) * 512], P[sp][:, :], AF.Silu),
                                    waits=p.bank_rwaits(bank) +
                                    [("dve", hprev[sp])], inc="act")
                                p.bank_read(bank, "act")
                                sil[sp] = p.c["act"]
                    for sp in range(2):
                        bank = f"P{2 + sp}"
                        sl = slice(sp * 512, (sp + 1) * 512)
                        p.op("v", lambda v, sp=sp, sl=sl, cb=cb: v.tensor_mul(
                            tb0[:, sl], P[2 + sp][:, :], cb[:, sl]),
                            waits=p.bank_rwaits(bank), inc="dve")
                        p.bank_read(bank, "dve")
                        p.op("v", lambda v, c=c, sl=sl: v.tensor_mul(
                            hh[c][:, sl], tb0[:, sl], ts1[:, sl]),
                            waits=[("act", sil[sp]), ("pe", state["w2_end"])], inc="dve")
                        hprev[sp] = p.c["dve"]
                h_done = p.c["dve"]
                if e == 4:
                    dump("cb0", cB[:, :])
                    dump("h00", hh[0][:, :])
                for f in range(16):
                    i = len(ws.consumed)
                    bks = (4, 5) if f % 2 == 0 else (2, 3)
                    par = i % WStream.NS
                    for sp in range(2):
                        bank = f"P{bks[sp]}"
                        pi = P[bks[sp]]
                        w = []
                        if sp == 0:
                            w += ws.consume_waits(i)
                            if f == 0:
                                w += [("dve", h_done)]
                        w += p.bank_wwaits(bank)
                        for kt in range(11):
                            p.op("t", lambda te, pi=pi, par=par, kt=kt, sp=sp: te.matmul(
                                pi[:, :], wb[par][:, kt * 128:(kt + 1) * 128],
                                hh[kt][:, sp * 512:(sp + 1) * 512],
                                start=(kt == 0), stop=(kt == 10), skip_group_check=True),
                                waits=w if kt == 0 else [], inc="pe" if kt == 10 else None)
                        p.bank_written(bank)
                        p.op("v", lambda v, pi=pi, f=f, sp=sp: v.scalar_tensor_tensor(
                            x[f][:, sp * 512:(sp + 1) * 512], pi[:, :], 1.0,
                            x[f][:, sp * 512:(sp + 1) * 512], ALU.mult, ALU.add),
                            waits=p.bank_rwaits(bank), inc="dve")
                        p.bank_read(bank, "dve")
                        if e == NEXP - 1 and sp == 1:
                            state["x_tile_dve"][f] = p.c["dve"]
                    ws.mark_consumed(i)
                state["w2_end"] = p.c["pe"]

            def emit_ffn():
                emit_ln()
                emit_sgate_router()
                for e in range(NEXP):
                    emit_expert(e)
                state["x_ready_dve"] = p.c["dve"]
                state["xn_free_pe"] = p.c["pe"]

            def emit_qkv():
                emit_ln()
                tab_consumed = {k: p.c["dve"] for k in ("c0", "c1", "s0", "s1")}

                def emit_pair(i, dcs, dss, out_lo, out_hi, store):
                    cs_slot = tb0 if i % 2 == 0 else tb1
                    ss_slot = aB if i % 2 == 0 else bB
                    cs_sem = "tld%d" % (i % 2)
                    ss_sem = "tld%d" % (2 + i % 2)
                    p.op("y", lambda y, i=i, dcs=dcs, cs_slot=cs_slot: y.dma_start(
                        out=cs_slot[:, :], in_=dcs[i * 128:(i + 1) * 128, :]),
                        waits=[("dve", tab_consumed["c%d" % (i % 2)])], inc=cs_sem, n=16)
                    p.op("y", lambda y, i=i, dss=dss, ss_slot=ss_slot: y.dma_start(
                        out=ss_slot[:, :], in_=dss[i * 128:(i + 1) * 128, :]),
                        waits=[("dve", tab_consumed["s%d" % (i % 2)])], inc=ss_sem, n=16)
                    tlw = [(cs_sem, p.c[cs_sem]), (ss_sem, p.c[ss_sem])]
                    i_lo = len(ws.consumed)
                    i_up = i_lo + 1
                    par_lo = i_lo % WStream.NS
                    par_up = i_up % WStream.NS
                    for sp in range(2):
                        blo, bup = f"P{2 * sp}", f"P{2 * sp + 1}"
                        for half in range(2):
                            bank = blo if half == 0 else bup
                            pi = P[2 * sp + half]
                            par = par_lo if half == 0 else par_up
                            i_ch = i_lo if half == 0 else i_up
                            w = p.bank_wwaits(bank)
                            if sp == 0:
                                w += ws.consume_waits(i_ch)
                            fine = (i == 0 and sp == 0 and half == 0)
                            for kt in range(16):
                                wk_ = (w if kt == 0 else [])
                                if fine:
                                    wk_ = wk_ + [("dve", state["xn_tile"][kt])]
                                p.op("t", lambda te, pi=pi, sp=sp, par=par, kt=kt: te.matmul(
                                    pi[:, :], wb[par][:, kt * 128:(kt + 1) * 128],
                                    xn[kt][:, sp * 512:(sp + 1) * 512],
                                    start=(kt == 0), stop=(kt == 15), skip_group_check=True),
                                    waits=wk_, inc="pe" if kt == 15 else None)
                            p.bank_written(bank)
                        sl = slice(sp * 512, (sp + 1) * 512)
                        w0 = p.bank_rwaits(blo) + p.bank_rwaits(bup) + tlw
                        if sp == 0 and store is not None:
                            w0 += store["stage_free"]
                        p.op("v", lambda v, sp=sp, sl=sl, cs_slot=cs_slot: v.tensor_mul(
                            ts1[:, sl], P[2 * sp][:, :], cs_slot[:, sl]), waits=w0, inc="dve")
                        p.op("v", lambda v, sp=sp, sl=sl, ss_slot=ss_slot: v.tensor_mul(
                            sgB[:, sl], P[2 * sp + 1][:, :], ss_slot[:, sl]), inc="dve")
                        p.op("v", lambda v, sl=sl, out_lo=out_lo: v.tensor_sub(
                            out_lo[:, sl], ts1[:, sl], sgB[:, sl]), inc="dve")
                        p.op("v", lambda v, sp=sp, sl=sl, cs_slot=cs_slot: v.tensor_mul(
                            ts1[:, sl], P[2 * sp + 1][:, :], cs_slot[:, sl]), inc="dve")
                        p.bank_read(bup, "dve")
                        p.op("v", lambda v, sp=sp, sl=sl, ss_slot=ss_slot: v.tensor_mul(
                            sgB[:, sl], P[2 * sp][:, :], ss_slot[:, sl]), inc="dve")
                        p.bank_read(blo, "dve")
                        p.op("v", lambda v, sl=sl, out_hi=out_hi: v.tensor_add(
                            out_hi[:, sl], ts1[:, sl], sgB[:, sl]), inc="dve")
                    tab_consumed["c%d" % (i % 2)] = p.c["dve"]
                    tab_consumed["s%d" % (i % 2)] = p.c["dve"]
                    ws.mark_consumed(i_lo)
                    ws.mark_consumed(i_up)
                    return p.c["dve"]

                stage_free = [[], []]
                for i in range(8):
                    lo, hi = hh[11 + 2 * (i % 2)], hh[12 + 2 * (i % 2)]
                    dn = emit_pair(i, dcsk, dssk, lo, hi,
                                   {"stage_free": stage_free[i % 2]})
                    ks = "kst%d" % (i % 2)
                    for (tile, ht) in ((lo, i), (hi, i + 8)):
                        p.op("s", lambda e, tile=tile, ht=ht: e.dma_start(
                            out=kbi[:, :, ht, :].rearrange("l p m -> p l m"),
                            in_=tile[:, :].rearrange("p (l m) -> p l m", l=8)),
                            waits=[("dve", dn)], inc=ks, n=16)
                    stage_free[i % 2] = [(ks, p.c[ks])]
                k_store_done = [("kst0", p.c["kst0"]), ("kst1", p.c["kst1"])]
                for f in range(16):
                    if f == 2:
                        p.op("g", lambda g: g.collective_compute(
                            "AllGather", ALU.bypass,
                            replica_groups=[[0, 1, 2, 3], [4, 5, 6, 7]],
                            ins=[kbi.ap().rearrange("a p h m -> (a p h) m").opt()],
                            outs=[kbg.ap().opt()]),
                            waits=k_store_done, inc="cc")
                    vtmp = hh[2 * (f % 2)]
                    vtg = hh[2 * (f % 2) + 1]
                    vsem = "vst%d" % (f % 2)
                    i_ch = len(ws.consumed)
                    par = i_ch % WStream.NS
                    for sp in range(2):
                        bank = f"P{4 + sp}"
                        w = (ws.consume_waits(i_ch) if sp == 0 else []) + p.bank_wwaits(bank)
                        for kt in range(16):
                            p.op("t", lambda te, sp=sp, par=par, kt=kt: te.matmul(
                                P[4 + sp][:, :], wb[par][:, kt * 128:(kt + 1) * 128],
                                xn[kt][:, sp * 512:(sp + 1) * 512],
                                start=(kt == 0), stop=(kt == 15), skip_group_check=True),
                                waits=w if kt == 0 else [], inc="pe" if kt == 15 else None)
                        p.bank_written(bank)
                    ws.mark_consumed(i_ch)
                    for sp in range(2):
                        bank = f"P{4 + sp}"
                        w = p.bank_rwaits(bank)
                        if f >= 2 and sp == 0:
                            w += [(vsem, 16 * (f // 2))]
                        p.op("v", lambda v, sp=sp, vtmp=vtmp: v.tensor_copy(
                            vtmp[:, sp * 512:(sp + 1) * 512], P[4 + sp][:, :]),
                            waits=w, inc="dve")
                        p.bank_read(bank, "dve")
                    vt_done = p.c["dve"]
                    for m in range(8):
                        pb = p6b if m % 2 == 0 else p7b
                        bname = "p6b" if m % 2 == 0 else "p7b"
                        p.op("t", lambda te, pb=pb, m=m, vtmp=vtmp: te.transpose(
                            pb[:, 0:128], vtmp[:, m * 128:(m + 1) * 128], eyeb[:, :]),
                            waits=p.bank_wwaits(bname) +
                            ([("dve", vt_done), ("xlg", gp_ld)] if m == 0 else []),
                            inc="pe")
                        p.bank_written(bname)
                        p.op("v", lambda v, pb=pb, m=m, vtg=vtg: v.tensor_copy(
                            vtg[:, m * 128:(m + 1) * 128], pb[:, 0:128]),
                            waits=p.bank_rwaits(bname), inc="dve")
                        p.bank_read(bname, "dve")
                    p.op("s", lambda e, f=f, vtg=vtg: e.dma_start(
                        out=vbi[:, f * 128:(f + 1) * 128].rearrange("(m p) c -> p m c", p=128),
                        in_=vtg[:, :].rearrange("p (m c) -> p m c", m=8)),
                        waits=[("dve", p.c["dve"])], inc=vsem, n=16)
                v_store_done = [("vst0", p.c["vst0"]), ("vst1", p.c["vst1"])]
                for i in range(8):
                    sfree = [("pe", state["w2_end"])]
                    if i in (3, 4, 5, 6):
                        sfree = sfree + k_store_done
                    if i < 4:
                        sfree = sfree + v_store_done
                    emit_pair(i, dcsq, dssq, hh[i], hh[i + 8], {"stage_free": sfree})
                    if i == 0:
                        p.op("g", lambda g: g.collective_compute(
                            "AllGather", ALU.bypass,
                            replica_groups=[[0, 1, 2, 3], [4, 5, 6, 7]],
                            ins=[vbi.ap().opt()],
                            outs=[vbg.ap().opt()]),
                            waits=v_store_done, inc="cc")
                state["qkv_end_pe"] = p.c["pe"]
                state["cc_done"] = p.c["cc"]

            def emit_attn():
                kq_consumed = []
                dq_consumed = []
                avg_done = []
                entry_dve = p.c["dve"]
                NQ = 48

                def slot_mb(q):
                    return (0, q) if q < 16 else (1, q - 16)

                def slab(mb):
                    pm = mb // 4
                    r = pm if pm < 4 else 7 - pm
                    return r * 8 + (0 if pm < 4 else 4) + mb % 4

                def load_kblock(q):
                    slot, mb = slot_mb(q)
                    s0 = slab(mb) * 2048
                    w = [("cc", state["cc_done"] - 1)]
                    if q >= 2:
                        w = [("pe", kq_consumed[q - 2])]
                    p.op("y", lambda y, s0=s0, q=q: y.dma_start(
                        out=kb[q % 2][:, :],
                        in_=kbg[s0:s0 + 2048, :].rearrange("(p h) m -> p (h m)", p=128)),
                        waits=w, inc="kld%d" % (q % 2), n=16)

                def load_dtile(q):
                    slot, mb = slot_mb(q)
                    w = [("dve", entry_dve)] if q < 2 else [("dve", dq_consumed[q - 2])]
                    sl = slice((q % 2) * 512, (q % 2) * 512 + 512)
                    p.op("y", lambda y, slot=slot, mb=mb, sl=sl: y.dma_start(
                        out=tb0[:, sl], in_=ddec[slot, mb, :, :]),
                        waits=w, inc="dld%d" % (q % 2), n=16)

                def load_vblock(g2, lb):
                    slot = 0 if g2 < 4 else 1
                    mb = (g2 if g2 < 4 else g2 - 4) * 4 + lb
                    s0 = slab(mb) * 128
                    w = [("cc", state["cc_done"])]
                    if g2 >= 1:
                        w = [("pe", avg_done[g2 - 1])]
                    p.op("y", lambda y, s0=s0, lb=lb: y.dma_start(
                        out=vb[lb][:, :], in_=vbg[s0:s0 + 128, :]),
                        waits=w, inc="vld%d" % lb, n=16)

                load_kblock(0)
                load_kblock(1)
                load_dtile(0)
                load_dtile(1)
                for lb in range(4):
                    load_vblock(0, lb)

                x_tile_dve = [0] * 16
                for g2 in range(12):
                    slot = 0 if g2 < 4 else 1
                    g = g2 if g2 < 4 else g2 - 4
                    for lb in range(4):
                        mb = 4 * g + lb
                        q = (0 if slot == 0 else 16) + mb
                        bank = f"P{q % 2}"
                        pi = P[q % 2]
                        w = [("kld%d" % (q % 2), 16 * (q // 2 + 1))] + p.bank_wwaits(bank)
                        if q == 0:
                            w += [("pe", state["qkv_end_pe"])]
                        for ht in range(16):
                            p.op("t", lambda te, pi=pi, q=q, ht=ht, slot=slot: te.matmul(
                                pi[:, :], kb[q % 2][:, ht * 128:(ht + 1) * 128],
                                hh[ht][:, slot * 512:(slot + 1) * 512],
                                start=(ht == 0), stop=(ht == 15), skip_group_check=True),
                                waits=w if ht == 0 else [], inc="pe" if ht == 15 else None)
                        p.bank_written(bank)
                        kq_consumed.append(p.c["pe"])
                        if q + 2 < NQ:
                            load_kblock(q + 2)
                        sl = slice((q % 2) * 512, (q % 2) * 512 + 512)
                        w = p.bank_rwaits(bank) + [("dld%d" % (q % 2), 16 * (q // 2 + 1))]
                        if g2 == 0 and lb == 0:
                            w += [("pe", state["qkv_end_pe"])]
                        if g2 == 4 and lb == 0:
                            w += [("pe", avg_done[3])]
                        p.op("v", lambda v, pi=pi, mb=mb, sl=sl: v.tensor_mul(
                            xn[mb // 2][:, (mb % 2) * 512:(mb % 2) * 512 + 512],
                            pi[:, :], tb0[:, sl]),
                            waits=w, inc="dve")
                        p.bank_read(bank, "dve")
                        dq_consumed.append(p.c["dve"])
                        if q + 2 < NQ:
                            load_dtile(q + 2)
                    attd_done = p.c["dve"]
                    for f in range(16):
                        bank = f"P{4 + f % 2}"
                        pi = P[4 + f % 2]
                        w = p.bank_wwaits(bank) + [("dve", attd_done)]
                        if f == 0:
                            w += [("vld%d" % lb2, 16 * (g2 + 1)) for lb2 in range(4)]
                        for lb in range(4):
                            mb = 4 * g + lb
                            p.op("t", lambda te, pi=pi, f=f, lb=lb, mb=mb: te.matmul(
                                pi[:, :], vb[lb][:, f * 128:(f + 1) * 128],
                                xn[mb // 2][:, (mb % 2) * 512:(mb % 2) * 512 + 512],
                                start=(lb == 0), stop=(lb == 3), skip_group_check=True),
                                waits=w if lb == 0 else [], inc="pe" if lb == 3 else None)
                        p.bank_written(bank)
                        p.op("v", lambda v, pi=pi, f=f, slot=slot: v.scalar_tensor_tensor(
                            x[f][:, slot * 512:(slot + 1) * 512], pi[:, :], 1.0,
                            x[f][:, slot * 512:(slot + 1) * 512], ALU.mult, ALU.add),
                            waits=p.bank_rwaits(bank), inc="dve")
                        p.bank_read(bank, "dve")
                        if g2 == 11:
                            x_tile_dve[f] = p.c["dve"]
                    avg_done.append(p.c["pe"])
                    if g2 + 1 < 12:
                        for lb in range(4):
                            load_vblock(g2 + 1, lb)
                state["x_ready_dve"] = p.c["dve"]
                state["x_tile_dve"] = x_tile_dve
                state["xn_free_pe"] = p.c["pe"]
                state["w2_end"] = p.c["pe"]

            emit_ffn()
            if stage >= 2:
                emit_qkv()
                emit_attn()
            if stage >= 3:
                emit_ffn()

            xfin = p.c["dve"]
            for f in range(16):
                p.op("y", lambda y, f=f: y.dma_start(
                    out=dy[f * 128:(f + 1) * 128, :], in_=x[f][:, :]),
                    waits=[("dve", xfin)] if f == 0 else [], inc="st", n=16)
            p.wait("y", [("st", p.c["st"])])

        @block.tensor
        def _(te):
            emit_all(Prog("t", te, sems))

        @block.vector
        def _(v):
            emit_all(Prog("v", v, sems))

        @block.scalar
        def _(s):
            emit_all(Prog("s", s, sems))

        @block.gpsimd
        def _(g):
            emit_all(Prog("g", g, sems))

        @block.sync
        def _(y):
            emit_all(Prog("y", y, sems))

    return nc


_CACHE = {}


def kernel(L0_attn_X, L0_attn_W_Q, L0_attn_W_K, L0_attn_W_V,
           L0_ffn_gate_w, L0_ffn_sgate_w, L0_ffn_w13, L0_ffn_w2,
           L0_ffn_shared_gate_up_w, L0_ffn_shared_down_w):
    X = np.asarray(L0_attn_X)
    shared = _prep_shared(np.asarray(L0_attn_W_Q), np.asarray(L0_attn_W_K),
                          np.asarray(L0_attn_W_V), np.asarray(L0_ffn_gate_w),
                          np.asarray(L0_ffn_sgate_w), np.asarray(L0_ffn_w13),
                          np.asarray(L0_ffn_w2), np.asarray(L0_ffn_shared_gate_up_w),
                          np.asarray(L0_ffn_shared_down_w))
    in_maps = []
    for c in range(8):
        m = dict(shared)
        m.update(_prep_core(X, c))
        in_maps.append(m)
    if "nc" not in _CACHE:
        _CACHE["nc"] = build_nc(STAGE)
    nc = _CACHE["nc"]
    res = run_bass_kernel_spmd(nc, in_maps, core_ids=list(range(8)))
    out = np.empty((2, 4096, 2048), np.float32)
    for c in range(8):
        b, j = c // 4, c % 4
        yt = res.results[c]["y"].T
        out[b, j * 512:(j + 1) * 512, :] = yt[0:512]
        out[b, (7 - j) * 512:(8 - j) * 512, :] = yt[512:1024]
    return out


if __name__ == "__main__":
    import sys
    if len(sys.argv) > 1 and sys.argv[1] == "buildtest":
        import time
        t0 = time.time()
        nc = build_nc(int(sys.argv[2]) if len(sys.argv) > 2 else STAGE)
        print(f"build ok in {time.time() - t0:.1f}s")
